# version 09
# baseline (speedup 1.0000x reference)
import numpy as np
from contextlib import ExitStack

import concourse.bass as bass
import concourse.mybir as mybir
from concourse.bass_utils import run_bass_kernel_spmd

F32 = mybir.dt.float32
BF16 = mybir.dt.bfloat16
I32 = mybir.dt.int32
U8 = mybir.dt.uint8
AF = mybir.ActivationFunctionType
ALU = mybir.AluOpType

NCORES = 8
D = 2048
NF = 16
H = 16
DH = 128
TOK = 1024
BLK = 256
HALF = 512
FI = 8192
NI = 64
NSTATE = 1024
LN_EPS = 1e-5
RMS_EPS = 1e-6

DBG = None
STAGE = 5

ENGS = ("pe", "act", "dve", "pool", "sp")


class Sched:

    def __init__(self, nc, stack):
        self.nc = nc
        self.q = {e: [] for e in ENGS}
        self.sems = {}
        self.cnt = {}
        for name in ("ld", "lc", "lx", "st", "pe", "act", "dve", "pool",
                     "q0", "q1", "q2", "q3", "q4", "q5", "q6", "q7"):
            self.sems[name] = stack.enter_context(nc.semaphore(name))
            self.cnt[name] = 0

    def emit(self, eng, fn, wait=(), inc=None, n=1):
        w = tuple((s, v) for (s, v) in wait if v and v > 0)
        for s, v in w:
            assert v <= self.cnt[s], f"forward wait {s} {v} > {self.cnt[s]}"
        tgt = None
        if inc is not None:
            self.cnt[inc] += n
            tgt = self.cnt[inc]
        sems = self.sems

        pad = eng in ("dve", "act", "pool") and not w

        def thunk(e, w=w, fn=fn, inc=inc, n=n, pad=pad):
            if pad:
                e.nop()
            for s, v in w:
                e.wait_ge(sems[s], v)
            ins = fn(e)
            if inc is not None:
                ins.then_inc(sems[inc], n)

        self.q[eng].append(thunk)
        return tgt

    def waitonly(self, eng, wait):
        sems = self.sems
        w = tuple((s, v) for (s, v) in wait if v and v > 0)

        def thunk(e, w=w):
            for s, v in w:
                e.wait_ge(sems[s], v)

        self.q[eng].append(thunk)

    def dma(self, q, dst, src, wait=(), sem=None):
        if sem is None:
            sem = "ld" if q == "pool" else "lx"
        return self.emit(q if q == "pool" else "sp",
                         lambda e, d=dst, s=src: e.dma_start(d, s),
                         wait=wait, inc=sem, n=16)

    def store(self, dst, src, wait=(), sem="st"):
        return self.emit("sp", lambda e, d=dst, s=src: e.dma_start(d, s),
                         wait=wait, inc=sem, n=16)

    def flush(self):
        q = self.q
        self.q = {e: [] for e in ENGS}
        if not any(q.values()):
            return
        with self.nc.Block() as blk:
            m = {"pe": blk.tensor, "act": blk.scalar, "dve": blk.vector,
                 "pool": blk.gpsimd, "sp": blk.sync}
            for e in ENGS:
                if q[e]:
                    m[e](lambda eng, lst=q[e]: [t(eng) for t in lst])


class Builder:
    def __init__(self, stage=STAGE):
        self.stage = stage
        nc = bass.Bass("TRN2", target_bir_lowering=False)
        self.nc = nc
        self.stack = ExitStack()
        self.sc = Sched(nc, self.stack)

        dt = nc.dram_tensor
        self.xo = dt("xo", [TOK, D], F32, kind="ExternalInput")
        self.kvsel = dt("kvsel", [1, 8], F32, kind="ExternalInput")
        self.kvsnd = dt("kvsnd", [128, H * DH], BF16, kind="Internal")
        self.kvrcv = dt("kvrcv", [8 * 128, H * DH], BF16, kind="Internal",
                        addr_space="Shared")
        self.aoD = dt("aoD", [4, 128, H * BLK], BF16, kind="Internal")
        self.qhD = dt("qhD", [4, 128, H * BLK], BF16, kind="Internal")
        self.gateD = dt("gateD", [2, 128, NF * HALF], BF16, kind="Internal")
        self.wqkvT = dt("wqkvT", [24, 128, NF, 256], F32, kind="ExternalInput")
        self.wgateT = dt("wgateT", [8, 128, NF, 256], F32, kind="ExternalInput")
        self.woutT = dt("woutT", [8, 128, NF, 256], F32, kind="ExternalInput")
        self.bT = dt("bT", [4, 128, NF, 256], F32, kind="ExternalInput")
        self.cT = dt("cT", [8, 128, 8, 256], F32, kind="ExternalInput")
        self.gwT = dt("gwT", [NI, 128, 16, 128], I32, kind="ExternalInput")
        self.uwT = dt("uwT", [NI, 128, 16, 128], I32, kind="ExternalInput")
        self.dwT = dt("dwT", [16, 2, 128, 4, 1024], I32, kind="ExternalInput")
        self.slope = dt("slope", [1, H], F32, kind="ExternalInput")
        self.nw_c = dt("nw_c", [128, NF], F32, kind="ExternalInput")
        self.dv_c = dt("dv_c", [128, NF], F32, kind="ExternalInput")
        self.gsc_c = dt("gsc_c", [128, NI], F32, kind="ExternalInput")
        self.usc_c = dt("usc_c", [128, NI], F32, kind="ExternalInput")
        self.dsc_c = dt("dsc_c", [128, NF], F32, kind="ExternalInput")
        self.ident = dt("ident", [128, 128], F32, kind="ExternalInput")
        self.out = dt("out", [TOK, D], F32, kind="ExternalOutput")
        self.dbgt = dt("dbg", [128, 16384], F32, kind="ExternalOutput")
        self.u8g = dt("u8g", [NI, 128, 16, 128], U8, kind="Internal")
        self.wqkvB = dt("wqkvB", [24, 128, NF, 256], BF16, kind="Internal")
        self.wgateB = dt("wgateB", [8, 128, NF, 256], BF16, kind="Internal")
        self.woutB = dt("woutB", [8, 128, NF, 256], BF16, kind="Internal")
        self.bTB = dt("bTB", [4, 128, NF, 256], BF16, kind="Internal")
        self.cTB = dt("cTB", [8, 128, 8, 256], BF16, kind="Internal")
        self.u8u = dt("u8u", [NI, 128, 16, 128], U8, kind="Internal")
        self.u8d = dt("u8d", [16, 2, 128, 4, 1024], U8, kind="Internal")

        al = nc.alloc_sbuf_tensor
        self.x_sb = al("x_sb", [128, NF * TOK], F32)
        self.idf = al("idf", [128, 128], F32)
        self.selr = al("selr", [1, 8], F32)
        self.selb = al("selb", [128, 8], F32)
        self.ones_r = al("ones_r", [1, 128], F32)
        self.ones_cf = al("ones_cf", [128, 1], F32)
        self.ones_cb = al("ones_cb", [128, 1], BF16)
        self.m128_r = al("m128_r", [1, 128], BF16)
        self.slope_sb = al("slope_sb", [1, H], F32)
        self.nw = al("nw", [128, NF], F32)
        self.dv = al("dv", [128, NF], F32)
        self.gsc = al("gsc", [128, NI], F32)
        self.usc = al("usc", [128, NI], F32)
        self.dsc = al("dsc", [128, NF], F32)
        self.epsv = al("epsv", [1, 2], F32)
        self.attn_ctx = ExitStack()
        ac = lambda n_, sh, dt_: self.attn_ctx.enter_context(
            nc.sbuf_tensor(n_, sh, dt_))
        self.kv = ac("kv", [128, H * DH], F32)
        self.idb = ac("idb", [128, 128], BF16)
        self.SF = ac("SF", [128, H], F32)
        self.SFn = ac("SFn", [128, H], F32)
        self.KD = ac("KD", [128, 2 * H], F32)
        self.BD = ac("BD", [128, H], F32)
        self.BDg = ac("BDg", [128, 4 * H], F32)

        self.conv_jobs = []
        self.lc_kvjob = []
        if self.stage >= 2:
            for j in range(8):
                self.conv_jobs.append((
                    self.wqkvB.ap()[8 + 2 * j: 8 + 2 * (j + 1)],
                    self.wqkvT.ap()[8 + 2 * j: 8 + 2 * (j + 1)]))
            for j in range(4):
                self.conv_jobs.append((
                    self.wgateB.ap()[2 * j: 2 * (j + 1)],
                    self.wgateT.ap()[2 * j: 2 * (j + 1)]))
            for j in range(4):
                self.conv_jobs.append((
                    self.woutB.ap()[2 * j: 2 * (j + 1)],
                    self.woutT.ap()[2 * j: 2 * (j + 1)]))
            for j in range(2):
                self.conv_jobs.append((
                    self.bTB.ap()[2 * j: 2 * (j + 1)],
                    self.bT.ap()[2 * j: 2 * (j + 1)]))
            for j in range(2):
                self.conv_jobs.append((
                    self.cTB.ap()[4 * j: 4 * (j + 1)],
                    self.cT.ap()[4 * j: 4 * (j + 1)]))
            for j in range(4):
                self.conv_jobs.append((
                    self.wqkvB.ap()[2 * j: 2 * (j + 1)],
                    self.wqkvT.ap()[2 * j: 2 * (j + 1)]))
        self.n_attn_conv = len(self.conv_jobs)
        if self.stage >= 3:
            for ib in range(NI):
                self.conv_jobs.append((self.u8g.ap()[ib], self.gwT.ap()[ib]))
                self.conv_jobs.append((self.u8u.ap()[ib], self.uwT.ap()[ib]))
            for r in range(16):
                self.conv_jobs.append((self.u8d.ap()[r], self.dwT.ap()[r]))
        self.conv_done = 0
        self.lc_hist = []
        self.kv_stt = [0] * H
        self._ao_war = 0
        self._qh_war = 0
        self.p0_pool = 0
        self.bdg_done = []

    def sbuf(self, name, shape, dtype):
        self._uid = getattr(self, "_uid", 0) + 1
        return self.nc.sbuf_tensor(f"{name}_{self._uid}", shape, dtype)

    def psum(self, name, shape, dtype):
        self._uid = getattr(self, "_uid", 0) + 1
        return self.nc.psum_tensor(f"{name}_{self._uid}", shape, dtype)

    def maybe_conv(self, k=1):
        for _ in range(k):
            if self.conv_jobs:
                dst, src = self.conv_jobs.pop(0)
                self.conv_done = self.sc.dma("pool", dst, src, sem="lc")
                self.lc_hist.append(self.conv_done)

    def finish_conv(self):
        while self.conv_jobs:
            self.maybe_conv()

    def dump(self, tag, p, region_ap, width):
        if DBG != (tag, p):
            return
        sc = self.sc
        sc.flush()
        t = sc.dma("pool", self.dbgt.ap()[:, 0:width], region_ap)
        sc.waitonly("pool", [("ld", t)])
        sc.flush()

    def xsl(self, c, t0, L):
        return self.x_sb[:, c * TOK + t0: c * TOK + t0 + L]

    def emit_ln_gen(self, xs, outs, L, scr, res, sum_eng="pool",
                    ap2_eng="pool", war=()):
        sc = self.sc
        s1, s2, sqtA, rowb, psA, psB = scr
        n = len(xs)
        inv = 1.0 / (128.0 * n)
        row = lambda i: rowb[0:1, i * 512: i * 512 + L]

        sc.emit("act", lambda e, o=s2[:, 0:L], i=xs[0]:
                e.activation(o, i, AF.Square), wait=list(war), inc="act")
        d_s2 = 0
        sq_war = [0, 0]
        for c in range(1, n):
            P = c % 2
            a = sc.emit("act", lambda e, o=sqtA[:, P * 512:P * 512 + L], i=xs[c]:
                        e.activation(o, i, AF.Square),
                        wait=[("dve", sq_war[P])], inc="act")
            d_s2 = sc.emit("dve",
                           lambda e, o=s2[:, 0:L], i=sqtA[:, P * 512:P * 512 + L]:
                           e.tensor_add(o, o, i), wait=[("act", a)], inc="dve")
            sq_war[P] = d_s2
            yield
        p_s1 = sc.emit(sum_eng, lambda e, o=s1[:, 0:L], i=xs[0]:
                       e.tensor_copy(o, i), wait=list(war), inc=sum_eng)
        for c in range(1, n):
            p_s1 = sc.emit(sum_eng, lambda e, o=s1[:, 0:L], i=xs[c]:
                           e.tensor_add(o, o, i), inc=sum_eng)
            if c % 2 == 0:
                yield
        t1 = sc.emit("pe", lambda e, o=psA[0:1, 0:L], a=self.ones_cf[:, :],
                     b=s1[:, 0:L]: e.matmul(o, a, b, start=True, stop=True),
                     wait=[(sum_eng, p_s1)], inc="pe")
        t2 = sc.emit("pe", lambda e, o=psB[0:1, 0:L], a=self.ones_cf[:, :],
                     b=s2[:, 0:L]: e.matmul(o, a, b, start=True, stop=True),
                     wait=[("dve", d_s2)], inc="pe")
        yield
        sc.emit("dve", lambda e, o=row(0), i=psA[0:1, 0:L]:
                e.tensor_scalar(o, i, inv, None, ALU.mult),
                wait=[("pe", t1)], inc="dve")
        sc.emit("dve", lambda e, o=row(1), i=psB[0:1, 0:L]:
                e.tensor_scalar(o, i, inv, None, ALU.mult),
                wait=[("pe", t2)], inc="dve")
        sc.emit("dve", lambda e, o=row(2), a=row(0): e.tensor_mul(o, a, a),
                inc="dve")
        d = sc.emit("dve", lambda e, o=row(1), a=row(1), b=row(2):
                    e.tensor_sub(o, a, b), inc="dve")
        a_sd = sc.emit("act", lambda e, o=row(2), i=row(1):
                       e.activation(o, i, AF.Sqrt, bias=self.epsv[0:1, 0:1]),
                       wait=[("dve", d)], inc="act")
        d_a = sc.emit("dve", lambda e, o=row(3), i=row(2):
                      e.reciprocal(o, i), wait=[("act", a_sd)], inc="dve")
        yield
        tA = sc.emit("pe", lambda e, o=psA[:, 0:L], a=self.ones_r[:, :], b=row(3):
                     e.matmul(o, a, b, start=True, stop=True),
                     wait=[("dve", d_a)], inc="pe")
        dA = sc.emit("dve", lambda e, o=s1[:, 0:L], i=psA[:, 0:L]:
                     e.tensor_copy(o, i), wait=[("pe", tA)], inc="dve")
        d_b = sc.emit("dve", lambda e, o=row(2), a=row(0), b=row(3):
                      e.scalar_tensor_tensor(o, a, -1.0, b, ALU.mult, ALU.mult),
                      inc="dve")
        tB = sc.emit("pe", lambda e, o=psB[:, 0:L], a=self.ones_r[:, :], b=row(2):
                     e.matmul(o, a, b, start=True, stop=True),
                     wait=[("dve", d_b)], inc="pe")
        dB = sc.emit("dve", lambda e, o=s2[:, 0:L], i=psB[:, 0:L]:
                     e.tensor_copy(o, i), wait=[("pe", tB)], inc="dve")
        yield
        ap_war = [0, 0]
        p_last = 0
        for c in range(n):
            P = c % 2
            dm = sc.emit("dve", lambda e, o=sqtA[:, P * 512:P * 512 + L],
                         a=xs[c], b=s1[:, 0:L]: e.tensor_mul(o, a, b),
                         wait=[("dve", dA), (ap2_eng, ap_war[P])], inc="dve")
            p_last = sc.emit(ap2_eng, lambda e, o=outs[c],
                             a=sqtA[:, P * 512:P * 512 + L], b=s2[:, 0:L]:
                             e.tensor_add(o, a, b),
                             wait=[("dve", max(dm, dB))], inc=ap2_eng)
            ap_war[P] = p_last
            res["done"] = (ap2_eng, p_last)
            yield

    def emit_ln(self, xs, outs, L, scr):
        res = {}
        for _ in self.emit_ln_gen(xs, outs, L, scr, res):
            pass
        return res["done"][1]

    def load_feature_major(self, src_dram, row0, nrows, strip, psT, dst_xsl,
                           extra_wait=(), dma_extra=()):
        sc = self.sc
        nstrips = nrows // 128
        strip_war = [0, 0]
        ps_war = [("dve", 0), ("dve", 0)]
        lx = [0, 0]
        for s in range(nstrips):
            par = s % 2
            sl = strip[:, par * D:(par + 1) * D]
            w_d = [("pe", strip_war[par])]
            if s < 2:
                w_d += list(dma_extra)
            lx[par] = sc.dma(
                "sp", sl,
                src_dram.ap()[row0 + s * 128: row0 + (s + 1) * 128, :],
                wait=w_d, sem=f"q{par}")
            for g in range(4):
                gp = g % 2
                for qv in range(4):
                    c = g * 4 + qv
                    w = []
                    if qv == 0:
                        w = [(f"q{par}", lx[par]), ps_war[gp]]
                        if s == 0 and g == 0:
                            w += list(extra_wait)
                    sc.emit("pe", lambda e, o=psT[gp][:, qv * 128:(qv + 1) * 128],
                            i=sl[:, c * 128:(c + 1) * 128]:
                            e.transpose(o, i, self.idf[:, :]), wait=w,
                            inc="pe" if qv == 3 else None)
                tpe = sc.cnt["pe"]
                strip_war[par] = tpe
                eng = "dve" if (s + g) % 2 == 0 else "act"
                tgt = 0
                for qv2 in range(4):
                    w = [("pe", tpe)] if qv2 == 0 else []
                    tgt = sc.emit(eng, lambda e, o=dst_xsl(g * 4 + qv2, s * 128, 128),
                                  i=psT[gp][:, qv2 * 128:(qv2 + 1) * 128], eng=eng:
                                  (e.tensor_copy(o, i) if eng == "dve"
                                   else e.activation(o, i, AF.Copy)),
                                  wait=w, inc="act" if eng == "act" else "dve")
                ps_war[gp] = (eng, tgt)

    def p0(self):
        sc, nc = self.sc, self.nc
        st = ExitStack()
        strip = st.enter_context(self.sbuf("p0_strip", [128, 2 * D], F32))
        kcol = st.enter_context(self.sbuf("p0_kcol", [128, 2], F32))
        psT = [st.enter_context(self.psum(f"p0_psT{i}", [128, 512], F32))
               for i in range(2)]
        psW = st.enter_context(self.psum("p0_psW", [128, 128], F32))

        l_sl = sc.dma("sp", self.slope_sb[:, :], self.slope.ap())
        sc.dma("sp", self.selr[:, :], self.kvsel.ap())
        for dst, src in ((self.nw, self.nw_c), (self.dv, self.dv_c),
                         (self.gsc, self.gsc_c), (self.usc, self.usc_c),
                         (self.dsc, self.dsc_c), (self.idf, self.ident)):
            sc.dma("sp", dst[:, :], src.ap())
        sc.dma("pool", self.idb[:, :], self.ident.ap())
        d_ms = 0
        for buf, val in ((self.ones_r, 1.0), (self.ones_cf, 1.0),
                         (self.ones_cb, 1.0), (self.m128_r, -128.0)):
            d_ms = sc.emit("dve", lambda e, b=buf, v=val: e.memset(b[:, :], v),
                           inc="dve")
        d_ms = sc.emit("dve", lambda e: e.memset(self.kv[:, :], 0.0), inc="dve")
        self.kv_ms = d_ms
        d_ms = sc.emit("dve", lambda e: e.memset(self.epsv[0:1, 0:1], LN_EPS),
                       inc="dve")
        d_ms = sc.emit("dve", lambda e: e.memset(self.epsv[0:1, 1:2], RMS_EPS),
                       inc="dve")
        t_sf = sc.emit("pe", lambda e: e.matmul(psW[:, 0:H], self.ones_r[:, :],
                                                self.slope_sb[:, :],
                                                start=True, stop=True),
                       wait=[("lx", sc.cnt["lx"]), ("dve", d_ms)], inc="pe")
        d_sfc = sc.emit("dve", lambda e: e.tensor_copy(self.SF[:, :], psW[:, 0:H]),
                        wait=[("pe", t_sf)], inc="dve")
        d_sf = sc.emit("dve", lambda e: e.tensor_scalar(
            self.SFn[:, :], self.SF[:, :], -1.0, None, ALU.mult), inc="dve")
        p_io = sc.emit("pool", lambda e: e.iota(
            kcol[:, 0:1], [[0, 1]], base=-255, channel_multiplier=1,
            allow_small_or_imprecise_dtypes=True), inc="pool")
        p_io = sc.emit("pool", lambda e: e.iota(
            kcol[:, 1:2], [[0, 1]], base=-127, channel_multiplier=1,
            allow_small_or_imprecise_dtypes=True), inc="pool")
        t_se = sc.emit("pe", lambda e: e.matmul(psW[:, 32:40], self.ones_r[:, :],
                                                self.selr[:, :],
                                                start=True, stop=True),
                       wait=[("dve", d_sfc)], inc="pe")
        sc.emit("dve", lambda e: e.tensor_copy(self.selb[:, :], psW[:, 32:40]),
                wait=[("pe", t_se)], inc="dve")
        a_bd = 0
        for h in range(H):
            for c in range(2):
                sc.emit("act", lambda e, h=h, c=c:
                        e.activation(self.KD[:, h * 2 + c:h * 2 + c + 1],
                                     kcol[:, c:c + 1], AF.Exp,
                                     scale=self.SF[:, h:h + 1]),
                        wait=[("dve", d_sf), ("pool", p_io)], inc="act")
            a_bd = sc.emit("act", lambda e, h=h:
                           e.activation(self.BD[:, h:h + 1], self.SF[:, h:h + 1],
                                        AF.Exp, scale=-256.0), inc="act")
        p_b0 = sc.emit("pool", lambda e: e.memset(self.BDg[:, 0:H], 1.0),
                       inc="pool")
        d_b = sc.emit("dve", lambda e: e.tensor_copy(self.BDg[:, H:2 * H],
                                                     self.BD[:, :]),
                      wait=[("act", a_bd)], inc="dve")
        d_b = sc.emit("dve", lambda e: e.tensor_mul(
            self.BDg[:, 2 * H:3 * H], self.BDg[:, H:2 * H], self.BD[:, :]),
            inc="dve")
        d_b = sc.emit("dve", lambda e: e.tensor_mul(
            self.BDg[:, 3 * H:4 * H], self.BDg[:, 2 * H:3 * H], self.BD[:, :]),
            inc="dve")
        self.bdg_done = [("dve", d_b), ("pool", p_b0)]
        self.load_feature_major(self.xo, 0, TOK, strip, psT,
                                lambda c, tt, L: self.xsl(c, tt, L),
                                extra_wait=[("lx", sc.cnt["lx"])])
        self.maybe_conv(12)
        self.p0_pool = sc.cnt["pool"]
        sc.waitonly("sp", [("lx", sc.cnt["lx"])])
        sc.waitonly("pool", [("ld", sc.cnt["ld"])])
        sc.flush()
        st.close()

    def p2(self):
        sc, nc = self.sc, self.nc
        st = ExitStack()
        qkvctx = ExitStack()
        lnctx = ExitStack()
        s1 = st.enter_context(self.sbuf("p2_s1", [128, 512], F32))
        s2 = st.enter_context(self.sbuf("p2_s2", [128, 512], F32))
        rgt = st.enter_context(self.sbuf("p2_rgt", [128, 256], BF16))
        qb = qkvctx.enter_context(self.sbuf("p2_q", [128, NF * HALF], BF16))
        kb = qkvctx.enter_context(self.sbuf("p2_k", [128, NF * HALF], BF16))
        vT = qkvctx.enter_context(self.sbuf("p2_vT", [128, 4 * D], BF16))
        ln0 = lnctx.enter_context(self.sbuf("p2_ln0", [128, NF * HALF], BF16))
        ln1 = lnctx.enter_context(self.sbuf("p2_ln1", [128, NF * HALF], BF16))
        rowb = None
        sqtA = None
        psS = [st.enter_context(self.psum(f"p2_psS{i}", [128, 512], F32))
               for i in range(2)]
        psS_free = [("dve", 0), ("dve", 0)]
        blk_last = [0, 0]

        def lcw(j):
            assert j < len(self.lc_hist), f"conv job {j} not yet emitted"
            return ("lc", self.lc_hist[j])

        blk_cv = [0, 0]

        def std_block(wdram, blk0, ofb, dst, act_fn, wst, ln_t, ln_done, lc_j,
                      evac_war=None, preload=None):
            par = ofb % 2
            wsl = wst[:, par * NF * 256:(par + 1) * NF * 256]
            if preload is None:
                l_w = sc.dma(
                    "sp", wsl.rearrange("p (c f) -> p c f", c=NF),
                    wdram.ap()[blk0 + ofb],
                    wait=[("pe", blk_last[par]), lcw(lc_j)], sem=f"q{2 + par}")
                wrdy = (f"q{2 + par}", l_w)
            else:
                wrdy = preload
            self.maybe_conv(2)
            for ofh in range(2):
                u = (ofb * 2 + ofh) % 2
                for kc in range(NF):
                    w = []
                    if kc == 0:
                        w = [wrdy, psS_free[u], ln_done]
                    sc.emit("pe", lambda e, o=psS[u][:, 0:HALF],
                            a=wsl[:, kc * 256 + ofh * 128:
                                  kc * 256 + ofh * 128 + 128],
                            b=ln_t[:, kc * HALF:(kc + 1) * HALF],
                            s=(kc == 0), t=(kc == NF - 1):
                            e.matmul(o, a, b, start=s, stop=t),
                            wait=w, inc="pe" if kc == NF - 1 else None)
                tmm = sc.cnt["pe"]
                blk_last[par] = tmm
                oc = ofb * 2 + ofh
                w2 = [("pe", tmm)]
                if evac_war is not None:
                    w2.append(evac_war)
                a_ev = sc.emit("act",
                               lambda e, o=dst[:, oc * HALF:(oc + 1) * HALF],
                               i=psS[u][:, 0:HALF], f=act_fn:
                               e.activation(o, i, f),
                               wait=w2, inc="act")
                psS_free[u] = ("act", a_ev)

        def v_block(ofb, wst, ln_t, ln_done, evac_war=None):
            par = ofb % 2
            wsl = wst[:, par * NF * 256:(par + 1) * NF * 256]
            l_w = sc.dma(
                "sp", wsl.rearrange("p (c f) -> p c f", c=NF),
                self.wqkvB.ap()[16 + ofb],
                wait=[("pe", blk_last[par]), lcw(4 + ofb // 2)],
                sem=f"q{2 + par}")
            self.maybe_conv(2)
            for mc in range(4):
                u = mc % 2
                for kc in range(NF):
                    w = []
                    if kc == 0:
                        w = [(f"q{2 + par}", l_w), psS_free[u], ln_done]
                    sc.emit("pe", lambda e, o=psS[u][:, 0:256],
                            a=ln_t[:, kc * HALF + mc * 128:
                                  kc * HALF + mc * 128 + 128],
                            b=wsl[:, kc * 256:(kc + 1) * 256],
                            s=(kc == 0), t=(kc == NF - 1):
                            e.matmul(o, a, b, start=s, stop=t),
                            wait=w, inc="pe" if kc == NF - 1 else None)
                tmm = sc.cnt["pe"]
                blk_last[par] = tmm
                w2 = [("pe", tmm)]
                if evac_war is not None:
                    w2.append(evac_war)
                a_ev = sc.emit("act", lambda e,
                               o=vT[:, mc * D + ofb * 256: mc * D + (ofb + 1) * 256],
                               i=psS[u][:, 0:256]: e.activation(o, i, AF.Silu),
                               wait=w2, inc="act")
                psS_free[u] = ("act", a_ev)

        def gate_block(ofb, wst, gateS, ld0, ld1):
            par = ofb % 2
            wsl = wst[:, par * NF * 256:(par + 1) * NF * 256]
            l_w = sc.dma(
                "sp", wsl.rearrange("p (c f) -> p c f", c=NF),
                self.wgateB.ap()[ofb],
                wait=[("pe", blk_last[par]), lcw(8 + ofb // 2)],
                sem=f"q{2 + par}")
            self.maybe_conv(2)
            for si, (ln_t, ldone) in enumerate(((ln0, ld0), (ln1, ld1))):
                for ofh in range(2):
                    u = (si * 2 + ofh) % 2
                    for kc in range(NF):
                        w = []
                        if kc == 0:
                            w = [(f"q{2 + par}", l_w), psS_free[u], ldone]
                        sc.emit("pe", lambda e, o=psS[u][:, 0:HALF],
                                a=wsl[:, kc * 256 + ofh * 128:
                                      kc * 256 + ofh * 128 + 128],
                                b=ln_t[:, kc * HALF:(kc + 1) * HALF],
                                s=(kc == 0), t=(kc == NF - 1):
                                e.matmul(o, a, b, start=s, stop=t),
                                wait=w, inc="pe" if kc == NF - 1 else None)
                    tmm = sc.cnt["pe"]
                    blk_last[par] = tmm
                    oc = ofb * 2 + ofh
                    dsl = gateS[si][:, oc * HALF:(oc + 1) * HALF]
                    a_ev = sc.emit("act",
                                   lambda e, o=dsl, i=psS[u][:, 0:HALF]:
                                   e.activation(o, i, AF.Sigmoid),
                                   wait=[("pe", tmm)], inc="act")
                    psS_free[u] = ("act", a_ev)
                    self._g_war = sc.dma(
                        "sp", self.gateD.ap()[si, :, oc * HALF:(oc + 1) * HALF],
                        dsl, wait=[("act", a_ev)], sem="q7")

        def emit_decay(stX, war):
            QD = stX.enter_context(self.sbuf("p2_QD", [128, H * 256], BF16))
            DDt = stX.enter_context(self.sbuf("p2_DD", [128, 2 * H * 256], BF16))
            it = stX.enter_context(self.sbuf("p2_it", [128, 256], F32))
            p_io = sc.emit("pool", lambda e: e.iota(
                it[:, 0:256], [[1, 256]], base=1, channel_multiplier=0,
                allow_small_or_imprecise_dtypes=True), wait=list(war),
                inc="pool")
            a_qd = 0
            for h in range(H):
                a_qd = sc.emit("act", lambda e, h=h:
                               e.activation(QD[:, h * 256:(h + 1) * 256],
                                            it[:, 0:256], AF.Exp,
                                            scale=self.SFn[:, h:h + 1]),
                               wait=[("pool", p_io)], inc="act")
            a_dd = 0
            for c in range(2):
                p_io = sc.emit("pool", lambda e, c=c: e.iota(
                    it[:, 0:256], [[-1, 256]], base=128 * c, channel_multiplier=1,
                    allow_small_or_imprecise_dtypes=True),
                    wait=[("act", a_qd if c == 0 else a_dd)], inc="pool")
                for h in range(H):
                    a_dd = sc.emit("act", lambda e, h=h, c=c:
                                   e.activation(
                                       DDt[:, (h * 2 + c) * 256:(h * 2 + c + 1) * 256],
                                       it[:, 0:256], AF.Exp,
                                       scale=self.SF[:, h:h + 1]),
                                   wait=[("pool", p_io)], inc="act")
            p_dd = 0
            for h in range(H):
                for c in range(2):
                    p_dd = sc.emit("pool", lambda e, h=h, c=c:
                                   e.affine_select(
                                       DDt[:, (h * 2 + c) * 256:(h * 2 + c + 1) * 256],
                                       DDt[:, (h * 2 + c) * 256:(h * 2 + c + 1) * 256],
                                       [[1, 256]], ALU.is_ge, 0.0,
                                       base=-(128 * c), channel_multiplier=-1),
                                   wait=[("act", a_dd)], inc="pool")
            return QD, DDt, a_qd, p_dd

        def emit_scan2(p, sb):
            QD, DDt, a_qd, p_dd = sb["decay"]
            aow, qhw = sb["aow"], sb["qhw"]
            khat, kvb, sS, qh = sb["khat"], sb["kvb"], sb["sS"], sb["qh"]
            psTr, psO, psKV = sb["psTr"], sb["psO"], sb["psKV"]
            b_done, kvmm_done = sb["b_done"], sb["kvmm"]
            g_done, j_done, i_done = sb["g"], sb["j"], sb["i"]
            for jj in range(2):
                g = 2 * p + jj
                tc = jj * 256
                for h in range(H):
                    if h % 2 == 0:
                        self.maybe_conv()
                    P = h % 2
                    for c in range(2):
                        w = [("dve", b_done[P])] if c == 0 else []
                        sc.emit("pe", lambda e, o=psTr[P][:, c * 128:(c + 1) * 128],
                                i=kb[:, h * HALF + tc + c * 128:
                                     h * HALF + tc + c * 128 + 128]:
                                e.transpose(o, i, self.idb[:, :]), wait=w,
                                inc="pe" if c == 1 else None)
                    ta = sc.cnt["pe"]
                    for c in range(2):
                        w = [("pe", ta), ("pe", kvmm_done[P])] if c == 0 else []
                        b_done[P] = sc.emit(
                            "dve", lambda e,
                            o=khat[:, P * 256 + c * 128: P * 256 + c * 128 + 128],
                            i=psTr[P][:, c * 128:(c + 1) * 128],
                            s=self.KD[:, h * 2 + c:h * 2 + c + 1]:
                            e.tensor_scalar(o, i, s, None, ALU.mult),
                            wait=w, inc="dve")
                    for c in range(2):
                        w = [psS_free[P]] if c == 0 else []
                        sc.emit("pe", lambda e, o=psS[P][:, c * 256:(c + 1) * 256],
                                a=kb[:, h * HALF + tc + c * 128:
                                     h * HALF + tc + c * 128 + 128],
                                b=qb[:, h * HALF + tc: h * HALF + tc + 256]:
                                e.matmul(o, a, b, start=True, stop=True),
                                wait=w, inc="pe" if c == 1 else None)
                    tcc = sc.cnt["pe"]
                    dd = 0
                    for c in range(2):
                        w = []
                        if c == 0:
                            w = [("pe", tcc), ("pe", g_done[P]), ("pool", p_dd)]
                        dd = sc.emit("dve", lambda e,
                                     o=sS[:, P * 512 + c * 256: P * 512 + c * 256 + 256],
                                     a=psS[P][:, c * 256:(c + 1) * 256],
                                     b=DDt[:, (h * 2 + c) * 256:(h * 2 + c + 1) * 256]:
                                     e.tensor_mul(o, a, b), wait=w, inc="dve")
                    psS_free[P] = ("dve", dd)
                    ve = sc.emit("pool", lambda e, o=qh[:, P * 256:P * 256 + 256],
                                 a=qb[:, h * HALF + tc: h * HALF + tc + 256],
                                 b=QD[:, h * 256:(h + 1) * 256]:
                                 e.tensor_mul(o, a, b),
                                 wait=[("act", a_qd), ("pe", g_done[P])],
                                 inc="pool")
                    wq = []
                    if h < 2:
                        wq = [("q5", self._qh_war)]
                        if g == 0:
                            wq += self.bdg_done
                    sc.emit("pool", lambda e, o=qhw[:, h * 256:(h + 1) * 256],
                            a=qh[:, P * 256:P * 256 + 256],
                            s=self.BDg[:, g * H + h:g * H + h + 1]:
                            e.tensor_scalar(o, a, s, None, ALU.mult),
                            wait=wq, inc="pool")
                    wv = [("dve", max(self.kv_stt[h], self.kv_ms)),
                          ("pe", g_done[P])]
                    vf = sc.emit("act", lambda e, o=kvb[:, P * 128:P * 128 + 128],
                                 i=self.kv[:, h * DH:(h + 1) * DH]:
                                 e.activation(o, i, AF.Copy),
                                 wait=wv, inc="act")
                    sc.emit("pe", lambda e, o=psO[P][:, :],
                            a=kvb[:, P * 128:P * 128 + 128],
                            b=qh[:, P * 256:P * 256 + 256]:
                            e.matmul(o, a, b, start=True, stop=False),
                            wait=[("act", vf), ("act", j_done[P]),
                                  ("pool", ve), ("dve", dd)])
                    for c in range(2):
                        sc.emit("pe", lambda e, o=psO[P][:, :],
                                a=vT[:, (jj * 2 + c) * D + h * 128:
                                     (jj * 2 + c) * D + h * 128 + 128],
                                b=sS[:, P * 512 + c * 256: P * 512 + c * 256 + 256],
                                t=(c == 1):
                                e.matmul(o, a, b, start=False, stop=t),
                                inc="pe" if c == 1 else None)
                    tg = sc.cnt["pe"]
                    g_done[P] = tg
                    for c in range(2):
                        w = []
                        if c == 0:
                            w = [("dve", b_done[P]), ("dve", i_done[P])]
                        sc.emit("pe", lambda e, o=psKV[P][:, :],
                                a=khat[:, P * 256 + c * 128: P * 256 + c * 128 + 128],
                                b=vT[:, (jj * 2 + c) * D + h * 128:
                                     (jj * 2 + c) * D + h * 128 + 128],
                                s=(c == 0), t=(c == 1):
                                e.matmul(o, a, b, start=s, stop=t),
                                wait=w, inc="pe" if c == 1 else None)
                    th = sc.cnt["pe"]
                    kvmm_done[P] = th
                    di = sc.emit("dve", lambda e, o=self.kv[:, h * DH:(h + 1) * DH],
                                 s=self.BD[:, h:h + 1], i=psKV[P][:, :]:
                                 e.scalar_tensor_tensor(o, o, s, i, ALU.mult, ALU.add),
                                 wait=[("pe", th), ("act", vf)], inc="dve")
                    self.kv_stt[h] = di
                    i_done[P] = di
                    w3 = [("pe", tg)]
                    if h < 2:
                        w3.append(("q4", self._ao_war))
                    j_done[P] = sc.emit("act",
                                        lambda e, o=aow[:, h * 256:(h + 1) * 256],
                                        i=psO[P][:, :]: e.activation(o, i, AF.Copy),
                                        wait=w3, inc="act")
                q_last = sc.cnt["pool"]
                self._ao_war = sc.dma("sp", self.aoD.ap()[g], aow[:, :],
                                      wait=[("act", max(j_done))], sem="q4")
                self._qh_war = sc.dma("sp", self.qhD.ap()[g], qhw[:, :],
                                      wait=[("pool", q_last)], sem="q5")

        def emit_rg(aot, coff, gate_t, tcg, first_wait):
            aos = [aot[:, coff + c * 256: coff + (c + 1) * 256]
                   for c in range(NF)]
            sc.emit("act", lambda e, o=s2[:, 0:256], i=aos[0]:
                    e.activation(o, i, AF.Square),
                    wait=first_wait, inc="act")
            d_s2 = 0
            sq_war = [0, 0]
            for c in range(1, NF):
                P2 = c % 2
                a = sc.emit("act", lambda e, o=sqtA[:, P2 * 512:P2 * 512 + 256],
                            i=aos[c]: e.activation(o, i, AF.Square),
                            wait=[("dve", sq_war[P2])], inc="act")
                d_s2 = sc.emit("dve", lambda e, o=s2[:, 0:256],
                               i=sqtA[:, P2 * 512:P2 * 512 + 256]:
                               e.tensor_add(o, o, i),
                               wait=[("act", a)], inc="dve")
                sq_war[P2] = d_s2
            t1 = sc.emit("pe", lambda e, o=psS[0][0:1, 0:256],
                         a=self.ones_cf[:, :], b=s2[:, 0:256]:
                         e.matmul(o, a, b, start=True, stop=True),
                         wait=[("dve", d_s2), psS_free[0]], inc="pe")
            d = sc.emit("dve", lambda e, o=rowb[0:1, 0:256],
                        i=psS[0][0:1, 0:256]:
                        e.tensor_scalar(o, i, 1.0 / 2048.0, None, ALU.mult),
                        wait=[("pe", t1)], inc="dve")
            a_sd = sc.emit("act", lambda e, o=rowb[0:1, 512:768],
                           i=rowb[0:1, 0:256]:
                           e.activation(o, i, AF.Sqrt, bias=self.epsv[0:1, 1:2]),
                           wait=[("dve", d)], inc="act")
            d_rs = sc.emit("dve", lambda e, o=rowb[0:1, 1024:1280],
                           i=rowb[0:1, 512:768]: e.reciprocal(o, i),
                           wait=[("act", a_sd)], inc="dve")
            tB = sc.emit("pe", lambda e, o=psS[1][:, 0:256], a=self.ones_r[:, :],
                         b=rowb[0:1, 1024:1280]:
                         e.matmul(o, a, b, start=True, stop=True),
                         wait=[("dve", d_rs), psS_free[1]], inc="pe")
            d_bc = sc.emit("dve", lambda e, o=s1[:, 0:256], i=psS[1][:, 0:256]:
                           e.tensor_copy(o, i), wait=[("pe", tB)], inc="dve")
            psS_free[0] = ("dve", d)
            psS_free[1] = ("dve", d_bc)
            rg_war = 0
            for c in range(NF):
                dm = sc.emit("dve", lambda e, o=rgt[:, 0:256],
                             a=aos[c], s=self.nw[:, c:c + 1], b=s1[:, 0:256]:
                             e.scalar_tensor_tensor(o, a, s, b, ALU.mult, ALU.mult),
                             wait=[("dve", d_bc), ("pool", rg_war)],
                             inc="dve")
                rg_war = sc.emit("pool", lambda e,
                                 o=gate_t[:, c * HALF + tcg: c * HALF + tcg + 256],
                                 a=rgt[:, 0:256]:
                                 e.tensor_mul(o, a, o),
                                 wait=[("dve", dm)], inc="pool")
            return sc.cnt["pool"]

        def emit_wout(p, rg_done, wst, gate_t, x_rdy):
            t0 = p * HALF
            for ofb in range(8):
                par = ofb % 2
                wsl = wst[:, par * NF * 256:(par + 1) * NF * 256]
                l_w = sc.dma(
                    "sp", wsl.rearrange("p (c f) -> p c f", c=NF),
                    self.woutB.ap()[ofb],
                    wait=[("pe", blk_last[par]), lcw(12 + ofb // 2)],
                    sem=f"q{2 + par}")
                self.maybe_conv(2)
                for ofh in range(2):
                    u = (ofb * 2 + ofh) % 2
                    for kc in range(NF):
                        w = []
                        if kc == 0:
                            w = [(f"q{2 + par}", l_w), psS_free[u],
                                 ("pool", rg_done)]
                        sc.emit("pe", lambda e, o=psS[u][:, 0:HALF],
                                a=wsl[:, kc * 256 + ofh * 128:
                                      kc * 256 + ofh * 128 + 128],
                                b=gate_t[:, kc * HALF:(kc + 1) * HALF],
                                s=(kc == 0), t=(kc == NF - 1):
                                e.matmul(o, a, b, start=s, stop=t),
                                wait=w, inc="pe" if kc == NF - 1 else None)
                    tmm = sc.cnt["pe"]
                    blk_last[par] = tmm
                    oc = ofb * 2 + ofh
                    d_x = sc.emit("dve", lambda e, o=self.xsl(oc, t0, HALF),
                                  i=psS[u][:, 0:HALF]: e.tensor_add(o, o, i),
                                  wait=[("pe", tmm)] + x_rdy, inc="dve")
                    psS_free[u] = ("dve", d_x)

        stA = ExitStack()
        wstA = stA.enter_context(self.sbuf("p2_wstA", [128, 2 * NF * 256], BF16))
        rowb = stA.enter_context(self.sbuf("p2_rowbA", [1, 2048], F32))
        sqtA = stA.enter_context(self.sbuf("p2_sqtAA", [128, 1024], F32))
        psA = stA.enter_context(self.psum("p2_psA", [128, 512], F32))
        psB = stA.enter_context(self.psum("p2_psB", [128, 512], F32))
        l0s = [ln0[:, c * HALF:(c + 1) * HALF] for c in range(NF)]
        l1s = [ln1[:, c * HALF:(c + 1) * HALF] for c in range(NF)]
        ln_t0 = self.emit_ln([self.xsl(c, 0, HALF) for c in range(NF)],
                             l0s, HALF, (s1, s2, sqtA, rowb, psA, psB))
        ln_d0 = ("pool", ln_t0)
        res1 = {}
        lngen = self.emit_ln_gen([self.xsl(c, HALF, HALF) for c in range(NF)],
                                 l1s, HALF, (s1, s2, sqtA, rowb, psA, psB),
                                 res1, sum_eng="dve", ap2_eng="dve",
                                 war=[("pool", ln_t0)])

        def drive(n):
            nonlocal lngen
            if lngen is None:
                return
            for _ in range(n):
                if next(lngen, "END") == "END":
                    lngen = None
                    break

        qstg = stA.enter_context(self.sbuf("p2_qstg", [128, 4096], F32))
        for ofb in range(8):
            par = ofb % 2
            wsl = wstA[:, par * NF * 256:(par + 1) * NF * 256]
            l_f = sc.dma("sp", qstg[:, :].rearrange("p (c f) -> p c f", c=NF),
                         self.wqkvT.ap()[ofb],
                         wait=[("dve", blk_cv[0])], sem=f"q{2 + par}")
            cv = sc.emit("dve", lambda e, o=wsl, i=qstg[:, :]:
                         e.tensor_copy(o, i),
                         wait=[(f"q{2 + par}", l_f), ("pe", blk_last[par])],
                         inc="dve")
            blk_cv[0] = cv
            std_block(self.wqkvB, 0, ofb, qb, AF.Silu, wstA, ln0, ln_d0,
                      None, preload=("dve", cv))
            drive(3)
        for ofb in range(8):
            std_block(self.wqkvB, 8, ofb, kb, AF.Silu, wstA, ln0, ln_d0,
                      ofb // 2)
            drive(3)
        for ofb in range(8):
            v_block(ofb, wstA, ln0, ln_d0)
            drive(3)
        drive(100)
        ln_d1 = res1["done"]
        sc.flush()
        stA.close()

        def mk_scanctx(war, trk):
            ctx = ExitStack()
            sb = {"decay": emit_decay(ctx, war)}
            sb["aow"] = ctx.enter_context(self.sbuf("p2_aow", [128, H * BLK], BF16))
            sb["qhw"] = ctx.enter_context(self.sbuf("p2_qhw", [128, H * BLK], BF16))
            sb["khat"] = ctx.enter_context(self.sbuf("p2_khat", [128, 2 * 256], BF16))
            sb["kvb"] = ctx.enter_context(self.sbuf("p2_kvb", [128, 2 * 128], BF16))
            sb["sS"] = ctx.enter_context(self.sbuf("p2_sS", [128, 2 * 512], BF16))
            sb["qh"] = ctx.enter_context(self.sbuf("p2_qh", [128, 2 * 256], BF16))
            sb["psTr"] = [ctx.enter_context(self.psum(f"p2_psTr{i}", [128, 256], BF16))
                          for i in range(2)]
            sb["psO"] = [ctx.enter_context(self.psum(f"p2_psO{i}", [128, 256], F32))
                         for i in range(2)]
            sb["psKV"] = [ctx.enter_context(self.psum(f"p2_psKV{i}", [128, 128], F32))
                          for i in range(2)]
            for k, v in trk.items():
                sb[k] = v
            return ctx, sb
        trk = {"b_done": [0, 0], "kvmm": [0, 0], "g": [0, 0],
               "j": [0, 0], "i": [0, 0]}
        scanB, sb = mk_scanctx([ln_d1], trk)
        emit_scan2(0, sb)
        sc.flush()
        scanB.close()

        stC = ExitStack()
        wstC = stC.enter_context(self.sbuf("p2_wstC", [128, 2 * NF * 256], BF16))
        gateS = [stC.enter_context(self.sbuf(f"p2_gS{i}", [128, NF * HALF], BF16))
                 for i in range(2)]
        pool_snap = sc.cnt["pool"]
        for ofb in range(8):
            std_block(self.wqkvB, 0, ofb, qb, AF.Silu, wstC, ln1, ln_d1,
                      20 + ofb // 2,
                      evac_war=("pool", pool_snap) if ofb < 2 else None)
        for ofb in range(8):
            std_block(self.wqkvB, 8, ofb, kb, AF.Silu, wstC, ln1, ln_d1,
                      ofb // 2)
        for ofb in range(8):
            v_block(ofb, wstC, ln1, ln_d1)
        for ofb in range(8):
            gate_block(ofb, wstC, gateS, ln_d0, ln_d1)
        sc.flush()
        stC.close()
        lnctx.close()

        scanD, sb = mk_scanctx([("q7", self._g_war), ("q4", self._ao_war),
                                ("q5", self._qh_war)], trk)
        kvbS = scanD.enter_context(self.sbuf("p2_kvbS", [128, H * DH], BF16))
        emit_scan2(1, sb)
        kvw = 0
        for h in range(H):
            kvw = max(kvw, self.kv_stt[h])
        a_ks = sc.emit("act", lambda e: e.activation(kvbS[:, :], self.kv[:, :],
                                                     AF.Copy),
                       wait=[("dve", kvw)], inc="act")
        t_snd = sc.dma("sp", self.kvsnd.ap(), kvbS[:, :],
                       wait=[("act", a_ks)], sem="q6")
        cc_t = sc.emit("pool", lambda e: e.collective_compute(
            "AllGather", ALU.bypass, replica_groups=[list(range(8))],
            ins=[self.kvsnd.ap().opt()], outs=[self.kvrcv.ap().opt()]),
            wait=[("q6", t_snd)], inc="pool")
        sc.flush()
        scanD.close()
        qkvctx.close()

        stE = ExitStack()
        x_rdy = []
        rowb = stE.enter_context(self.sbuf("p2_rowbE", [1, 2048], F32))
        sqtA = stE.enter_context(self.sbuf("p2_sqtAE", [128, 1024], F32))
        ktmp = stE.enter_context(self.sbuf("p2e_ktmp", [128, 2 * 2048], BF16))
        kvinB = stE.enter_context(self.sbuf("p2e_kvinB", [128, 2048], BF16))
        d_acc = 0
        d_prev = 0
        for i, r in enumerate((0, 2, 4, 6)):
            kt = ktmp[:, (i % 2) * 2048:(i % 2) * 2048 + 2048]
            t_l = sc.dma("sp", kt,
                         self.kvrcv.ap()[r * 128:(r + 1) * 128, :],
                         wait=[("pool", cc_t), ("dve", d_prev)], sem="q6")
            d_prev = d_acc
            if i == 0:
                d_acc = sc.emit("dve", lambda e, kt=kt: e.tensor_scalar(
                    kvinB[:, :], kt, self.selb[:, 0:1], None, ALU.mult),
                    wait=[("q6", t_l)], inc="dve")
            else:
                d_acc = sc.emit("dve", lambda e, kt=kt, r=r: e.scalar_tensor_tensor(
                    kvinB[:, :], kt, self.selb[:, r:r + 1], kvinB[:, :],
                    ALU.mult, ALU.add), wait=[("q6", t_l)], inc="dve")
        a_kb = d_acc

        psC = [stE.enter_context(self.psum(f"p2e_psC{i}", [128, 256], F32))
               for i in range(2)]
        wstE = stE.enter_context(self.sbuf("p2e_wst", [128, 2 * NF * 256], BF16))
        for p in range(2):
            stP = ExitStack()
            aor = stP.enter_context(self.sbuf("p2e_aor", [128, 2 * H * BLK], BF16))
            qhr = stP.enter_context(self.sbuf("p2e_qhr", [128, 2 * H * BLK], BF16))
            gate_t = stP.enter_context(self.sbuf("p2e_gt", [128, NF * HALF], BF16))
            snap = [("dve", sc.cnt["dve"]), ("act", sc.cnt["act"]),
                    ("pool", sc.cnt["pool"]), ("pe", sc.cnt["pe"])]
            rst = []
            for jj in range(2):
                g = 2 * p + jj
                rst.append((
                    sc.dma("sp", aor[:, jj * 4096:(jj + 1) * 4096],
                           self.aoD.ap()[g],
                           wait=[("q4", self._ao_war)] + snap, sem="q4"),
                    sc.dma("sp", qhr[:, jj * 4096:(jj + 1) * 4096],
                           self.qhD.ap()[g],
                           wait=[("q5", self._qh_war)] + snap, sem="q5")))
            t_g = sc.dma("sp", gate_t[:, :], self.gateD.ap()[p],
                         wait=[("q7", self._g_war)] + snap, sem="q7")
            sc.waitonly("pool", [("q7", t_g)])
            corr_war = [("dve", a_kb), ("dve", a_kb)]
            rgd = 0
            for jj in range(2):
                tcg = jj * 256
                t_ao, t_qh = rst[jj]
                d_last = 0
                for h in range(H):
                    Pc = h % 2
                    w = [corr_war[Pc]]
                    if h < 2:
                        w += [("dve", a_kb), ("q5", t_qh)] + x_rdy
                    t_mm = sc.emit("pe", lambda e, o=psC[Pc][:, :],
                                   a=kvinB[:, h * DH:(h + 1) * DH],
                                   b=qhr[:, jj * 4096 + h * 256:
                                         jj * 4096 + (h + 1) * 256]:
                                   e.matmul(o, a, b, start=True, stop=True),
                                   wait=w, inc="pe")
                    wd = [("pe", t_mm)]
                    if h < 2:
                        wd.append(("q4", t_ao))
                    d_c = sc.emit("dve", lambda e,
                                  o=aor[:, jj * 4096 + h * 256:
                                        jj * 4096 + (h + 1) * 256],
                                  i=psC[Pc][:, :]:
                                  e.tensor_add(o, o, i),
                                  wait=wd, inc="dve")
                    corr_war[Pc] = ("dve", d_c)
                    d_last = d_c
                rgd = emit_rg(aor, jj * 4096, gate_t, tcg, [("dve", d_last)])
            emit_wout(p, rgd, wstE, gate_t, x_rdy)
            sc.flush()
            stP.close()
        sc.flush()
        stE.close()
        st.close()

    def ffn(self, tag):
        sc, nc = self.sc, self.nc
        st = ExitStack()
        ln2 = st.enter_context(self.sbuf(f"{tag}_ln2", [128, 2 * NF * HALF], BF16))
        hfull = st.enter_context(self.sbuf(f"{tag}_h", [128, NI * HALF], BF16))
        rowb = st.enter_context(self.sbuf(f"{tag}_rowb", [1, 2048], F32))
        shilo = st.enter_context(self.sbuf(f"{tag}_shl", [1, 1024], BF16))
        s1 = st.enter_context(self.sbuf(f"{tag}_s1", [128, 512], F32))
        s2 = st.enter_context(self.sbuf(f"{tag}_s2", [128, 512], F32))
        sqtA = st.enter_context(self.sbuf(f"{tag}_sqtA", [128, 1024], F32))
        tmpG = st.enter_context(self.sbuf(f"{tag}_tG", [128, 512], F32))
        tmpU = st.enter_context(self.sbuf(f"{tag}_tU", [128, 512], F32))
        l2 = lambda t, c: ln2[:, (t * NF + c) * HALF:(t * NF + c + 1) * HALF]
        srow_f = s1[0:1, 0:HALF]
        srow_t = s2[0:1, 0:HALF]

        with ExitStack() as stl:
            psA = stl.enter_context(self.psum(f"{tag}_psA", [128, 512], F32))
            psB = stl.enter_context(self.psum(f"{tag}_psB", [128, 512], F32))
            xs0 = [self.xsl(c, 0, HALF) for c in range(NF)]
            ln_done = [("pool", self.emit_ln(xs0, [l2(0, c) for c in range(NF)],
                                             HALF, (s1, s2, sqtA, rowb, psA, psB))),
                       None]
            sc.flush()

        for th in range(2):
            t0 = th * HALF
            stG = ExitStack()
            psG = [stG.enter_context(self.psum(f"{tag}_psG{i}", [128, 512], F32))
                   for i in range(2)]
            psU = [stG.enter_context(self.psum(f"{tag}_psU{i}", [128, 512], F32))
                   for i in range(2)]
            psR = stG.enter_context(self.psum(f"{tag}_psR", [128, 512], F32))
            wg = stG.enter_context(self.sbuf(f"{tag}_wg", [128, 2 * NF * 128], BF16))
            wu = stG.enter_context(self.sbuf(f"{tag}_wu", [128, 2 * NF * 128], BF16))
            lngen = None
            resB = {}
            if th == 0:
                psA2 = stG.enter_context(self.psum(f"{tag}_psA2", [128, 512], F32))
                psB2 = stG.enter_context(self.psum(f"{tag}_psB2", [128, 512], F32))
                xs1 = [self.xsl(c, HALF, HALF) for c in range(NF)]
                lngen = self.emit_ln_gen(xs1, [l2(1, c) for c in range(NF)],
                                         HALF, (s1, s2, sqtA, rowb, psA2, psB2),
                                         resB, sum_eng="dve", ap2_eng="dve")
            if self.conv_done:
                sc.waitonly("pool", [("lc", self.conv_done)])
                self.conv_done = 0
            hmul_last = 0
            gevac = [0, 0]
            uevac = [0, 0]
            blk_last = [0, 0]
            tsr = 0
            pend_sr = []

            def emit_sr(ic, dh):
                nonlocal tsr
                tsr = sc.emit("pe", lambda e, o=psR[0:1, 0:HALF],
                              a=self.ones_cb[:, :],
                              b=hfull[:, ic * HALF:(ic + 1) * HALF],
                              s=(ic == 0), t=(ic == NI - 1):
                              e.matmul(o, a, b, start=s, stop=t),
                              wait=[("dve", dh)], inc="pe")
            for ic in range(NI):
                P = ic % 2
                wgs = wg[:, P * NF * 128:(P + 1) * NF * 128]
                wus = wu[:, P * NF * 128:(P + 1) * NF * 128]
                lg = sc.dma("pool", wgs.rearrange("p (c f) -> p c f", c=NF),
                            self.u8g.ap()[ic], wait=[("pe", blk_last[P])],
                            sem=f"q{4 + P}")
                lu = sc.dma("pool", wus.rearrange("p (c f) -> p c f", c=NF),
                            self.u8u.ap()[ic], wait=[("pe", blk_last[P])],
                            sem=f"q{6 + P}")
                for kc in range(NF):
                    w = []
                    if kc == 0:
                        w = [(f"q{4 + P}", lg), (f"q{6 + P}", lu),
                             ("act", gevac[P]), ln_done[th]]
                    sc.emit("pe", lambda e, o=psG[P][:, 0:HALF],
                            a=wgs[:, kc * 128:(kc + 1) * 128],
                            b=l2(th, kc),
                            s=(kc == 0), t=(kc == NF - 1):
                            e.matmul(o, a, b, start=s, stop=t),
                            wait=w, inc="pe" if kc == NF - 1 else None)
                tgG = sc.cnt["pe"]
                for kc in range(NF):
                    w = [("act", uevac[P])] if kc == 0 else []
                    sc.emit("pe", lambda e, o=psU[P][:, 0:HALF],
                            a=wus[:, kc * 128:(kc + 1) * 128],
                            b=l2(th, kc),
                            s=(kc == 0), t=(kc == NF - 1):
                            e.matmul(o, a, b, start=s, stop=t),
                            wait=w, inc="pe" if kc == NF - 1 else None)
                tgU = sc.cnt["pe"]
                blk_last[P] = tgU
                ag = sc.emit("act", lambda e, o=tmpG[:, :],
                             i=psG[P][:, 0:HALF], s=self.gsc[:, ic:ic + 1]:
                             e.activation(o, i, AF.Silu, scale=s),
                             wait=[("pe", tgG), ("dve", hmul_last)], inc="act")
                gevac[P] = ag
                au = sc.emit("act", lambda e, o=tmpU[:, :],
                             i=psU[P][:, 0:HALF], s=self.usc[:, ic:ic + 1]:
                             e.activation(o, i, AF.Copy, scale=s),
                             wait=[("pe", tgU), ("dve", hmul_last)], inc="act")
                uevac[P] = au
                dh = sc.emit("dve", lambda e, o=hfull[:, ic * HALF:(ic + 1) * HALF],
                             a=tmpG[:, :], b=tmpU[:, :]:
                             e.tensor_mul(o, a, b), wait=[("act", au)], inc="dve")
                hmul_last = dh
                pend_sr.append((ic, dh))
                if len(pend_sr) > 2:
                    emit_sr(*pend_sr.pop(0))
                if lngen is not None:
                    for _ in range(2):
                        if next(lngen, "END") == "END":
                            lngen = None
                            break
            if lngen is not None:
                for _ in lngen:
                    pass
            if th == 0:
                ln_done[1] = resB["done"]
            for args in pend_sr:
                emit_sr(*args)
            d_sr = sc.emit("dve", lambda e, o=srow_f, i=psR[0:1, 0:HALF]:
                           e.tensor_copy(o, i), wait=[("pe", tsr)], inc="dve")
            a_hi = sc.emit("act", lambda e, o=shilo[0:1, 0:HALF], i=srow_f:
                           e.activation(o, i, AF.Copy),
                           wait=[("dve", d_sr)], inc="act")
            d_lo = sc.emit("dve", lambda e, o=srow_t, a=srow_f,
                           b=shilo[0:1, 0:HALF]:
                           e.tensor_sub(o, a, b), wait=[("act", a_hi)], inc="dve")
            a_lo = sc.emit("act", lambda e, o=shilo[0:1, 512:512 + HALF],
                           i=srow_t: e.activation(o, i, AF.Copy),
                           wait=[("dve", d_lo)], inc="act")
            h_done = hmul_last
            sc.flush()
            stG.close()

            stD = ExitStack()
            psD = [stD.enter_context(self.psum(f"{tag}_psD{i}", [128, 512], F32))
                   for i in range(8)]
            wd = stD.enter_context(self.sbuf(f"{tag}_wd", [128, 2 * 4096], BF16))
            psD_free = [0] * 8
            xadd = [0, 0]
            wd_last = [0, 0]
            tmpE = [tmpG, tmpU]
            for dhh in range(2):
                for i4 in range(16):
                    par = i4 % 2
                    wds = wd[:, par * 4096:(par + 1) * 4096]
                    lw = sc.dma(
                        "pool", wds.rearrange("p (c f) -> p c f", c=4),
                        self.u8d.ap()[i4, dhh],
                        wait=[("pe", wd_last[par])], sem=f"q{4 + par}")
                    for c in range(4):
                        ic = i4 * 4 + c
                        for dc in range(8):
                            w = []
                            if dc == 0 and c == 0:
                                w.append((f"q{4 + par}", lw))
                                if dhh == 0 and i4 == 0:
                                    w.append(("dve", h_done))
                            if ic == 0:
                                w.append(("dve", psD_free[dc]))
                            sc.emit("pe", lambda e, o=psD[dc][:, :],
                                    a=wds[:, c * 1024 + dc * 128:
                                          c * 1024 + dc * 128 + 128],
                                    b=hfull[:, ic * HALF:(ic + 1) * HALF],
                                    s=(ic == 0):
                                    e.matmul(o, a, b, start=s, stop=False),
                                    wait=w, inc="pe" if dc == 7 else None)
                        wd_last[par] = sc.cnt["pe"]
                for dc in range(8):
                    w = [("act", a_lo)] if dc == 0 else []
                    sc.emit("pe", lambda e, o=psD[dc][:, :]:
                            e.matmul(o, self.m128_r[0:1, :], shilo[0:1, 0:HALF],
                                     start=False, stop=False), wait=w)
                    tcor = sc.emit("pe", lambda e, o=psD[dc][:, :]:
                                   e.matmul(o, self.m128_r[0:1, :],
                                            shilo[0:1, 512:512 + HALF],
                                            start=False, stop=True), inc="pe")
                    dcg = dhh * 8 + dc
                    P = dc % 2
                    a_ev = sc.emit("act",
                                   lambda e, o=tmpE[P][:, :],
                                   i=psD[dc][:, :], s=self.dsc[:, dcg:dcg + 1]:
                                   e.activation(o, i, AF.Copy, scale=s),
                                   wait=[("pe", tcor), ("dve", xadd[P])],
                                   inc="act")
                    d_x = sc.emit("dve", lambda e, o=self.xsl(dcg, t0, HALF),
                                  i=tmpE[P][:, :]: e.tensor_add(o, o, i),
                                  wait=[("act", a_ev)], inc="dve")
                    xadd[P] = d_x
                    psD_free[dc] = d_x
            sc.flush()
            stD.close()
        st.close()

    def p4(self):
        sc, nc = self.sc, self.nc
        st = ExitStack()
        ln3 = st.enter_context(self.sbuf("p4_ln3", [128, 2 * NF * HALF], BF16))
        t1b = st.enter_context(self.sbuf("p4_t1", [128, 8 * HALF], BF16))
        wst = st.enter_context(self.sbuf("p4_wst", [128, 2 * NF * 256], BF16))
        s1 = st.enter_context(self.sbuf("p4_s1", [128, 512], F32))
        s2 = st.enter_context(self.sbuf("p4_s2", [128, 512], F32))
        sqtA = st.enter_context(self.sbuf("p4_sqtA", [128, 1024], F32))
        rowb = st.enter_context(self.sbuf("p4_rowb", [1, 2048], F32))
        tmpC = st.enter_context(self.sbuf("p4_tmpC", [128, 1024], F32))
        psA = st.enter_context(self.psum("p4_psA", [128, 512], F32))
        psB = st.enter_context(self.psum("p4_psB", [128, 512], F32))
        psA2 = st.enter_context(self.psum("p4_psA2", [128, 512], F32))
        psB2 = st.enter_context(self.psum("p4_psB2", [128, 512], F32))
        ps = [psA, psB]
        l3 = lambda t, c: ln3[:, (t * NF + c) * HALF:(t * NF + c + 1) * HALF]

        xs0 = [self.xsl(c, 0, HALF) for c in range(NF)]
        ln_done = [("pool", self.emit_ln(xs0, [l3(0, c) for c in range(NF)],
                                         HALF, (s1, s2, sqtA, rowb, psA, psB))),
                   None]
        sc.flush()
        for th in range(2):
            t0 = th * HALF
            lngen = None
            resB = {}
            if th == 0:
                xs1 = [self.xsl(c, HALF, HALF) for c in range(NF)]
                lngen = self.emit_ln_gen(xs1, [l3(1, c) for c in range(NF)],
                                         HALF, (s1, s2, sqtA, rowb, psA2, psB2),
                                         resB, sum_eng="dve", ap2_eng="dve")

            def drive(n):
                nonlocal lngen
                if lngen is None:
                    return
                for _ in range(n):
                    if next(lngen, "END") == "END":
                        lngen = None
                        break

            ps_free = [ln_done[th], ln_done[th]]
            blk_last = [0, 0]
            for ofb in range(4):
                par = ofb % 2
                wsl = wst[:, par * NF * 256:(par + 1) * NF * 256]
                l_w = sc.dma(
                    "sp", wsl.rearrange("p (c f) -> p c f", c=NF),
                    self.bTB.ap()[ofb],
                    wait=[("pe", blk_last[par]),
                          ("lc", self.lc_hist[16 + ofb // 2])],
                    sem=f"q{2 + par}")
                for ofh in range(2):
                    u = (ofb * 2 + ofh) % 2
                    for kc in range(NF):
                        w = []
                        if kc == 0:
                            w = [(f"q{2 + par}", l_w), ps_free[u], ln_done[th]]
                        sc.emit("pe", lambda e, o=ps[u][:, 0:HALF],
                                a=wsl[:, kc * 256 + ofh * 128:
                                      kc * 256 + ofh * 128 + 128],
                                b=l3(th, kc),
                                s=(kc == 0), t=(kc == NF - 1):
                                e.matmul(o, a, b, start=s, stop=t),
                                wait=w, inc="pe" if kc == NF - 1 else None)
                    tmm = sc.cnt["pe"]
                    blk_last[par] = tmm
                    oc = ofb * 2 + ofh
                    a_ev = sc.emit("act",
                                   lambda e, o=t1b[:, oc * HALF:(oc + 1) * HALF],
                                   i=ps[u][:, 0:HALF]: e.activation(o, i, AF.Tanh),
                                   wait=[("pe", tmm)], inc="act")
                    ps_free[u] = ("act", a_ev)
                    drive(4)
            t1_done = sc.cnt["act"]
            tc_war = [0, 0]
            for ofb in range(8):
                par = ofb % 2
                wsl = wst[:, par * NF * 256:(par + 1) * NF * 256]
                l_w = sc.dma(
                    "sp", wsl[:, 0:8 * 256].rearrange("p (c f) -> p c f", c=8),
                    self.cTB.ap()[ofb],
                    wait=[("pe", blk_last[par]),
                          ("lc", self.lc_hist[18 + ofb // 4])],
                    sem=f"q{2 + par}")
                for ofh in range(2):
                    u = (ofb * 2 + ofh) % 2
                    for kc in range(8):
                        w = []
                        if kc == 0:
                            w = [(f"q{2 + par}", l_w), ps_free[u], ("act", t1_done)]
                        sc.emit("pe", lambda e, o=ps[u][:, 0:HALF],
                                a=wsl[:, kc * 256 + ofh * 128:
                                      kc * 256 + ofh * 128 + 128],
                                b=t1b[:, kc * HALF:(kc + 1) * HALF],
                                s=(kc == 0), t=(kc == 7):
                                e.matmul(o, a, b, start=s, stop=t),
                                wait=w, inc="pe" if kc == 7 else None)
                    tmm = sc.cnt["pe"]
                    blk_last[par] = tmm
                    oc = ofb * 2 + ofh
                    P = oc % 2
                    d_c = sc.emit("dve", lambda e, o=tmpC[:, P * 512:P * 512 + HALF],
                                  a=l3(th, oc),
                                  s=self.dv[:, oc:oc + 1], i=ps[u][:, 0:HALF]:
                                  e.scalar_tensor_tensor(o, a, s, i, ALU.mult,
                                                         ALU.add),
                                  wait=[("pe", tmm), ("pool", tc_war[P])],
                                  inc="dve")
                    ps_free[u] = ("dve", d_c)
                    p_x = sc.emit("pool", lambda e, o=self.xsl(oc, t0, HALF),
                                  i=tmpC[:, P * 512:P * 512 + HALF]:
                                  e.tensor_add(o, o, i),
                                  wait=[("dve", d_c)], inc="pool")
                    tc_war[P] = p_x
                    drive(4)
            if lngen is not None:
                for _ in lngen:
                    pass
            if th == 0:
                ln_done[1] = resB["done"]
            sc.flush()
        st.close()

    def p6(self):
        sc, nc = self.sc, self.nc
        st = ExitStack()
        strip = st.enter_context(self.sbuf("p6_strip", [128, 2 * D], F32))
        psT = [st.enter_context(self.psum(f"p6_psT{i}", [128, 512], F32))
               for i in range(2)]
        strip_free = [0, 0]
        ps_free = [("dve", 0), ("dve", 0)]
        for s in range(8):
            par = s % 2
            sl = strip[:, par * D:(par + 1) * D]
            ev_tgts = []
            for g in range(4):
                gp = g % 2
                for qv in range(4):
                    c = g * 4 + qv
                    w = [ps_free[gp]] if qv == 0 else []
                    sc.emit("pe", lambda e, o=psT[gp][:, qv * 128:(qv + 1) * 128],
                            i=self.xsl(c, s * 128, 128):
                            e.transpose(o, i, self.idf[:, :]),
                            wait=w, inc="pe" if qv == 3 else None)
                tpe = sc.cnt["pe"]
                eng = "dve" if g % 2 == 0 else "act"
                tgt = sc.emit(eng, lambda e, o=sl[:, g * 512:(g + 1) * 512],
                              i=psT[gp][:, 0:512], eng=eng:
                              (e.tensor_copy(o, i) if eng == "dve"
                               else e.activation(o, i, AF.Copy)),
                              wait=[("pe", tpe), (f"q{par}", strip_free[par])],
                              inc="act" if eng == "act" else "dve")
                ps_free[gp] = (eng, tgt)
                ev_tgts.append((eng, tgt))
            strip_free[par] = sc.store(
                self.out.ap()[s * 128:(s + 1) * 128, :], sl, wait=ev_tgts,
                sem=f"q{par}")
        sc.waitonly("sp", [("q0", sc.cnt["q0"]), ("q1", sc.cnt["q1"])])
        sc.flush()
        st.close()

    def build(self):
        self.p0()
        if self.stage >= 2:
            self.p2()
        self.attn_ctx.close()
        self.finish_conv()
        self.sc.flush()
        if self.stage >= 3:
            self.ffn("p3")
        if self.stage >= 4:
            self.p4()
        if self.stage >= 5:
            self.ffn("p5")
        self.p6()
        return self.nc


_prog_cache = {}
_last_results = None


def _get_prog(stage):
    if stage not in _prog_cache:
        _prog_cache[stage] = Builder(stage).build()
    return _prog_cache[stage]


def kernel(L0_attn_a_hidden_states, L0_attn_a_slope_rate, L0_attn_a_w_qkv,
           L0_attn_a_w_gate, L0_attn_a_w_out, L0_attn_a_norm_w,
           L0_attn_b_B, L0_attn_b_C, L0_attn_b_D,
           L0_ffn_gate_w, L0_ffn_gate_scale, L0_ffn_up_w, L0_ffn_up_scale,
           L0_ffn_down_w, L0_ffn_down_scale):
    cc = np.ascontiguousarray
    x = cc(np.asarray(L0_attn_a_hidden_states, np.float32)).reshape(8192, D)
    col = lambda v, n: cc(np.asarray(v, np.float32).reshape(n, 128).T)
    blk = lambda a, nb: cc(a.reshape(16, 128, nb, 256).transpose(2, 1, 0, 3))
    gw_t = np.asarray(L0_ffn_gate_w, np.int32).T
    uw_t = np.asarray(L0_ffn_up_w, np.int32).T
    dw_t = np.asarray(L0_ffn_down_w, np.int32).T
    cT_t = np.asarray(L0_attn_b_C, np.float32).T
    shared = {
        "wqkvT": blk(np.asarray(L0_attn_a_w_qkv, np.float32).T, 24),
        "wgateT": blk(np.asarray(L0_attn_a_w_gate, np.float32).T, 8),
        "woutT": blk(np.asarray(L0_attn_a_w_out, np.float32).T, 8),
        "bT": blk(np.asarray(L0_attn_b_B, np.float32).T, 4),
        "cT": cc(cT_t.reshape(8, 128, 8, 256).transpose(2, 1, 0, 3)),
        "gwT": cc(gw_t.reshape(16, 128, NI, 128).transpose(2, 1, 0, 3)),
        "uwT": cc(uw_t.reshape(16, 128, NI, 128).transpose(2, 1, 0, 3)),
        "dwT": cc(dw_t.reshape(16, 4, 128, 2, 1024).transpose(0, 3, 2, 1, 4)),
        "slope": cc(np.asarray(L0_attn_a_slope_rate, np.float32).reshape(1, H)),
        "nw_c": col(L0_attn_a_norm_w, NF),
        "dv_c": col(L0_attn_b_D, NF),
        "gsc_c": col(L0_ffn_gate_scale, NI),
        "usc_c": col(L0_ffn_up_scale, NI),
        "dsc_c": col(L0_ffn_down_scale, NF),
        "ident": np.eye(128, dtype=np.float32),
    }
    in_maps = []
    for c in range(NCORES):
        m = dict(shared)
        m["xo"] = cc(x[c * TOK:(c + 1) * TOK])
        sel = np.zeros((1, 8), np.float32)
        if c % 2 == 1:
            sel[0, c - 1] = 1.0
        m["kvsel"] = sel
        in_maps.append(m)
    nc = _get_prog(STAGE)
    res = run_bass_kernel_spmd(nc, in_maps, core_ids=list(range(NCORES)))
    global _last_results
    _last_results = res.results
    outs = [np.asarray(r["out"]) for r in res.results]
    return np.concatenate(outs, axis=0).reshape(4, 2048, D)


# version 13
# speedup vs baseline: 1.0331x; 1.0331x over previous
import numpy as np
from contextlib import ExitStack

import concourse.bass as bass
import concourse.mybir as mybir
from concourse.bass_utils import run_bass_kernel_spmd

F32 = mybir.dt.float32
BF16 = mybir.dt.bfloat16
I32 = mybir.dt.int32
U8 = mybir.dt.uint8
AF = mybir.ActivationFunctionType
ALU = mybir.AluOpType

NCORES = 8
D = 2048
NF = 16
H = 16
DH = 128
TOK = 1024
BLK = 256
HALF = 512
FI = 8192
NI = 64
NSTATE = 1024
LN_EPS = 1e-5
RMS_EPS = 1e-6

DBG = None
STAGE = 5

ENGS = ("pe", "act", "dve", "pool", "sp")


class Sched:

    def __init__(self, nc, stack):
        self.nc = nc
        self.q = {e: [] for e in ENGS}
        self.sems = {}
        self.cnt = {}
        for name in ("ld", "lc", "lx", "st", "pe", "act", "dve", "pool",
                     "q0", "q1", "q2", "q3", "q4", "q5", "q6", "q7"):
            self.sems[name] = stack.enter_context(nc.semaphore(name))
            self.cnt[name] = 0

    def emit(self, eng, fn, wait=(), inc=None, n=1):
        w = tuple((s, v) for (s, v) in wait if v and v > 0)
        for s, v in w:
            assert v <= self.cnt[s], f"forward wait {s} {v} > {self.cnt[s]}"
        tgt = None
        if inc is not None:
            self.cnt[inc] += n
            tgt = self.cnt[inc]
        sems = self.sems

        pad = eng in ("dve", "act", "pool") and not w

        def thunk(e, w=w, fn=fn, inc=inc, n=n, pad=pad):
            if pad:
                e.nop()
            for s, v in w:
                e.wait_ge(sems[s], v)
            ins = fn(e)
            if inc is not None:
                ins.then_inc(sems[inc], n)

        self.q[eng].append(thunk)
        return tgt

    def waitonly(self, eng, wait):
        sems = self.sems
        w = tuple((s, v) for (s, v) in wait if v and v > 0)

        def thunk(e, w=w):
            for s, v in w:
                e.wait_ge(sems[s], v)

        self.q[eng].append(thunk)

    def dma(self, q, dst, src, wait=(), sem=None):
        if sem is None:
            sem = "ld" if q == "pool" else "lx"
        eng = q if q in ENGS else "sp"
        return self.emit(eng, lambda e, d=dst, s=src: e.dma_start(d, s),
                         wait=wait, inc=sem, n=16)

    def store(self, dst, src, wait=(), sem="st"):
        return self.emit("sp", lambda e, d=dst, s=src: e.dma_start(d, s),
                         wait=wait, inc=sem, n=16)

    def flush(self):
        q = self.q
        self.q = {e: [] for e in ENGS}
        if not any(q.values()):
            return
        with self.nc.Block() as blk:
            m = {"pe": blk.tensor, "act": blk.scalar, "dve": blk.vector,
                 "pool": blk.gpsimd, "sp": blk.sync}
            for e in ENGS:
                if q[e]:
                    m[e](lambda eng, lst=q[e]: [t(eng) for t in lst])


class Builder:
    def __init__(self, stage=STAGE):
        self.stage = stage
        nc = bass.Bass("TRN2", target_bir_lowering=False)
        self.nc = nc
        self.stack = ExitStack()
        self.sc = Sched(nc, self.stack)

        dt = nc.dram_tensor
        self.xo = dt("xo", [TOK, D], F32, kind="ExternalInput")
        self.kvsel = dt("kvsel", [1, 8], F32, kind="ExternalInput")
        self.kvsnd = dt("kvsnd", [128, H * DH], BF16, kind="Internal")
        self.kvrcv = dt("kvrcv", [8 * 128, H * DH], BF16, kind="Internal")
        self.aoD = dt("aoD", [4, 128, H * BLK], BF16, kind="Internal")
        self.qhD = dt("qhD", [4, 128, H * BLK], BF16, kind="Internal")
        self.gateD = dt("gateD", [2, 128, NF * HALF], BF16, kind="Internal")
        self.wqkvT = dt("wqkvT", [24, 128, NF, 256], F32, kind="ExternalInput")
        self.wgateT = dt("wgateT", [8, 128, NF, 256], F32, kind="ExternalInput")
        self.woutT = dt("woutT", [8, 128, NF, 256], F32, kind="ExternalInput")
        self.bT = dt("bT", [4, 128, NF, 256], F32, kind="ExternalInput")
        self.cT = dt("cT", [8, 128, 8, 256], F32, kind="ExternalInput")
        self.gwT = dt("gwT", [NI, 128, 16, 128], I32, kind="ExternalInput")
        self.uwT = dt("uwT", [NI, 128, 16, 128], I32, kind="ExternalInput")
        self.dwT = dt("dwT", [16, 2, 128, 4, 1024], I32, kind="ExternalInput")
        self.slope = dt("slope", [1, H], F32, kind="ExternalInput")
        self.nw_c = dt("nw_c", [128, NF], F32, kind="ExternalInput")
        self.dv_c = dt("dv_c", [128, NF], F32, kind="ExternalInput")
        self.gsc_c = dt("gsc_c", [128, NI], F32, kind="ExternalInput")
        self.usc_c = dt("usc_c", [128, NI], F32, kind="ExternalInput")
        self.dsc_c = dt("dsc_c", [128, NF], F32, kind="ExternalInput")
        self.ident = dt("ident", [128, 128], F32, kind="ExternalInput")
        self.out = dt("out", [TOK, D], F32, kind="ExternalOutput")
        self.dbgt = dt("dbg", [128, 16384], F32, kind="ExternalOutput")
        self.u8g = dt("u8g", [NI, 128, 16, 128], U8, kind="Internal")
        self.wqkvB = dt("wqkvB", [24, 128, NF, 256], BF16, kind="Internal")
        self.wgateB = dt("wgateB", [8, 128, NF, 256], BF16, kind="Internal")
        self.woutB = dt("woutB", [8, 128, NF, 256], BF16, kind="Internal")
        self.bTB = dt("bTB", [4, 128, NF, 256], BF16, kind="Internal")
        self.cTB = dt("cTB", [8, 128, 8, 256], BF16, kind="Internal")
        self.u8u = dt("u8u", [NI, 128, 16, 128], U8, kind="Internal")
        self.u8d = dt("u8d", [16, 2, 128, 4, 1024], U8, kind="Internal")

        al = nc.alloc_sbuf_tensor
        self.x_sb = al("x_sb", [128, NF * TOK], F32)
        self.idf = al("idf", [128, 128], F32)
        self.selr = al("selr", [1, 8], F32)
        self.selb = al("selb", [128, 8], F32)
        self.ones_r = al("ones_r", [1, 128], F32)
        self.ones_cf = al("ones_cf", [128, 1], F32)
        self.ones_cb = al("ones_cb", [128, 1], BF16)
        self.m128_r = al("m128_r", [1, 128], BF16)
        self.slope_sb = al("slope_sb", [1, H], F32)
        self.nw = al("nw", [128, NF], F32)
        self.dv = al("dv", [128, NF], F32)
        self.gsc = al("gsc", [128, NI], F32)
        self.usc = al("usc", [128, NI], F32)
        self.dsc = al("dsc", [128, NF], F32)
        self.epsv = al("epsv", [1, 2], F32)
        self.attn_ctx = ExitStack()
        ac = lambda n_, sh, dt_: self.attn_ctx.enter_context(
            nc.sbuf_tensor(n_, sh, dt_))
        self.kv = ac("kv", [128, H * DH], F32)
        self.idb = ac("idb", [128, 128], BF16)
        self.SF = ac("SF", [128, H], F32)
        self.SFn = ac("SFn", [128, H], F32)
        self.KD = ac("KD", [128, 2 * H], F32)
        self.BD = ac("BD", [128, H], F32)
        self.BDg = ac("BDg", [128, 4 * H], F32)

        self.conv_jobs = []
        self.lc_kvjob = []
        if self.stage >= 2:
            for j in range(8):
                self.conv_jobs.append((
                    self.wqkvB.ap()[8 + 2 * j: 8 + 2 * (j + 1)],
                    self.wqkvT.ap()[8 + 2 * j: 8 + 2 * (j + 1)]))
            for j in range(4):
                self.conv_jobs.append((
                    self.wgateB.ap()[2 * j: 2 * (j + 1)],
                    self.wgateT.ap()[2 * j: 2 * (j + 1)]))
            for j in range(4):
                self.conv_jobs.append((
                    self.woutB.ap()[2 * j: 2 * (j + 1)],
                    self.woutT.ap()[2 * j: 2 * (j + 1)]))
            for j in range(2):
                self.conv_jobs.append((
                    self.bTB.ap()[2 * j: 2 * (j + 1)],
                    self.bT.ap()[2 * j: 2 * (j + 1)]))
            for j in range(2):
                self.conv_jobs.append((
                    self.cTB.ap()[4 * j: 4 * (j + 1)],
                    self.cT.ap()[4 * j: 4 * (j + 1)]))
            for j in range(4):
                self.conv_jobs.append((
                    self.wqkvB.ap()[2 * j: 2 * (j + 1)],
                    self.wqkvT.ap()[2 * j: 2 * (j + 1)]))
        self.n_attn_conv = len(self.conv_jobs)
        if self.stage >= 3:
            for ib in range(NI):
                self.conv_jobs.append((self.u8g.ap()[ib], self.gwT.ap()[ib]))
                self.conv_jobs.append((self.u8u.ap()[ib], self.uwT.ap()[ib]))
            for r in range(16):
                self.conv_jobs.append((self.u8d.ap()[r], self.dwT.ap()[r]))
        self.conv_done = 0
        self.lc_hist = []
        self.kv_stt = [0] * H
        self._ao_war = 0
        self._qh_war = 0
        self.p0_pool = 0
        self.bdg_done = []

    def sbuf(self, name, shape, dtype):
        self._uid = getattr(self, "_uid", 0) + 1
        return self.nc.sbuf_tensor(f"{name}_{self._uid}", shape, dtype)

    def psum(self, name, shape, dtype):
        self._uid = getattr(self, "_uid", 0) + 1
        return self.nc.psum_tensor(f"{name}_{self._uid}", shape, dtype)

    def maybe_conv(self, k=1):
        for _ in range(k):
            if self.conv_jobs:
                dst, src = self.conv_jobs.pop(0)
                self.conv_done = self.sc.dma("pool", dst, src, sem="lc")
                self.lc_hist.append(self.conv_done)

    def finish_conv(self):
        while self.conv_jobs:
            self.maybe_conv()

    def dump(self, tag, p, region_ap, width):
        if DBG != (tag, p):
            return
        sc = self.sc
        sc.flush()
        t = sc.dma("pool", self.dbgt.ap()[:, 0:width], region_ap)
        sc.waitonly("pool", [("ld", t)])
        sc.flush()

    def xsl(self, c, t0, L):
        return self.x_sb[:, c * TOK + t0: c * TOK + t0 + L]

    def emit_ln_gen(self, xs, outs, L, scr, res, sum_eng="pool",
                    ap2_eng="pool", war=()):
        sc = self.sc
        s1, s2, sqtA, rowb, psA, psB = scr
        n = len(xs)
        inv = 1.0 / (128.0 * n)
        row = lambda i: rowb[0:1, i * 512: i * 512 + L]

        sc.emit("act", lambda e, o=s2[:, 0:L], i=xs[0]:
                e.activation(o, i, AF.Square), wait=list(war), inc="act")
        d_s2 = 0
        sq_war = [0, 0]
        for c in range(1, n):
            P = c % 2
            a = sc.emit("act", lambda e, o=sqtA[:, P * 512:P * 512 + L], i=xs[c]:
                        e.activation(o, i, AF.Square),
                        wait=[("dve", sq_war[P])], inc="act")
            d_s2 = sc.emit("dve",
                           lambda e, o=s2[:, 0:L], i=sqtA[:, P * 512:P * 512 + L]:
                           e.tensor_add(o, o, i), wait=[("act", a)], inc="dve")
            sq_war[P] = d_s2
            yield
        p_s1 = sc.emit(sum_eng, lambda e, o=s1[:, 0:L], i=xs[0]:
                       e.tensor_copy(o, i), wait=list(war), inc=sum_eng)
        for c in range(1, n):
            p_s1 = sc.emit(sum_eng, lambda e, o=s1[:, 0:L], i=xs[c]:
                           e.tensor_add(o, o, i), inc=sum_eng)
            if c % 2 == 0:
                yield
        t1 = sc.emit("pe", lambda e, o=psA[0:1, 0:L], a=self.ones_cf[:, :],
                     b=s1[:, 0:L]: e.matmul(o, a, b, start=True, stop=True),
                     wait=[(sum_eng, p_s1)], inc="pe")
        t2 = sc.emit("pe", lambda e, o=psB[0:1, 0:L], a=self.ones_cf[:, :],
                     b=s2[:, 0:L]: e.matmul(o, a, b, start=True, stop=True),
                     wait=[("dve", d_s2)], inc="pe")
        yield
        sc.emit("dve", lambda e, o=row(0), i=psA[0:1, 0:L]:
                e.tensor_scalar(o, i, inv, None, ALU.mult),
                wait=[("pe", t1)], inc="dve")
        sc.emit("dve", lambda e, o=row(1), i=psB[0:1, 0:L]:
                e.tensor_scalar(o, i, inv, None, ALU.mult),
                wait=[("pe", t2)], inc="dve")
        sc.emit("dve", lambda e, o=row(2), a=row(0): e.tensor_mul(o, a, a),
                inc="dve")
        d = sc.emit("dve", lambda e, o=row(1), a=row(1), b=row(2):
                    e.tensor_sub(o, a, b), inc="dve")
        a_sd = sc.emit("act", lambda e, o=row(2), i=row(1):
                       e.activation(o, i, AF.Sqrt, bias=self.epsv[0:1, 0:1]),
                       wait=[("dve", d)], inc="act")
        d_a = sc.emit("dve", lambda e, o=row(3), i=row(2):
                      e.reciprocal(o, i), wait=[("act", a_sd)], inc="dve")
        yield
        tA = sc.emit("pe", lambda e, o=psA[:, 0:L], a=self.ones_r[:, :], b=row(3):
                     e.matmul(o, a, b, start=True, stop=True),
                     wait=[("dve", d_a)], inc="pe")
        dA = sc.emit("dve", lambda e, o=s1[:, 0:L], i=psA[:, 0:L]:
                     e.tensor_copy(o, i), wait=[("pe", tA)], inc="dve")
        d_b = sc.emit("dve", lambda e, o=row(2), a=row(0), b=row(3):
                      e.scalar_tensor_tensor(o, a, -1.0, b, ALU.mult, ALU.mult),
                      inc="dve")
        tB = sc.emit("pe", lambda e, o=psB[:, 0:L], a=self.ones_r[:, :], b=row(2):
                     e.matmul(o, a, b, start=True, stop=True),
                     wait=[("dve", d_b)], inc="pe")
        dB = sc.emit("dve", lambda e, o=s2[:, 0:L], i=psB[:, 0:L]:
                     e.tensor_copy(o, i), wait=[("pe", tB)], inc="dve")
        yield
        ap_war = [0, 0]
        p_last = 0
        for c in range(n):
            P = c % 2
            dm = sc.emit("dve", lambda e, o=sqtA[:, P * 512:P * 512 + L],
                         a=xs[c], b=s1[:, 0:L]: e.tensor_mul(o, a, b),
                         wait=[("dve", dA), (ap2_eng, ap_war[P])], inc="dve")
            p_last = sc.emit(ap2_eng, lambda e, o=outs[c],
                             a=sqtA[:, P * 512:P * 512 + L], b=s2[:, 0:L]:
                             e.tensor_add(o, a, b),
                             wait=[("dve", max(dm, dB))], inc=ap2_eng)
            ap_war[P] = p_last
            res["done"] = (ap2_eng, p_last)
            yield

    def emit_ln(self, xs, outs, L, scr):
        res = {}
        for _ in self.emit_ln_gen(xs, outs, L, scr, res):
            pass
        return res["done"][1]

    def load_feature_major(self, src_dram, row0, nrows, strip, psT, dst_xsl,
                           extra_wait=(), dma_extra=()):
        sc = self.sc
        nstrips = nrows // 128
        strip_war = [0, 0]
        ps_war = [("dve", 0), ("dve", 0)]
        lx = [0, 0]
        for s in range(nstrips):
            par = s % 2
            sl = strip[:, par * D:(par + 1) * D]
            w_d = [("pe", strip_war[par])]
            if s < 2:
                w_d += list(dma_extra)
            lx[par] = sc.dma(
                "sp", sl,
                src_dram.ap()[row0 + s * 128: row0 + (s + 1) * 128, :],
                wait=w_d, sem=f"q{par}")
            for g in range(4):
                gp = g % 2
                for qv in range(4):
                    c = g * 4 + qv
                    w = []
                    if qv == 0:
                        w = [(f"q{par}", lx[par]), ps_war[gp]]
                        if s == 0 and g == 0:
                            w += list(extra_wait)
                    sc.emit("pe", lambda e, o=psT[gp][:, qv * 128:(qv + 1) * 128],
                            i=sl[:, c * 128:(c + 1) * 128]:
                            e.transpose(o, i, self.idf[:, :]), wait=w,
                            inc="pe" if qv == 3 else None)
                tpe = sc.cnt["pe"]
                strip_war[par] = tpe
                eng = "dve" if (s + g) % 2 == 0 else "act"
                tgt = 0
                for qv2 in range(4):
                    w = [("pe", tpe)] if qv2 == 0 else []
                    tgt = sc.emit(eng, lambda e, o=dst_xsl(g * 4 + qv2, s * 128, 128),
                                  i=psT[gp][:, qv2 * 128:(qv2 + 1) * 128], eng=eng:
                                  (e.tensor_copy(o, i) if eng == "dve"
                                   else e.activation(o, i, AF.Copy)),
                                  wait=w, inc="act" if eng == "act" else "dve")
                ps_war[gp] = (eng, tgt)

    def p0(self):
        sc, nc = self.sc, self.nc
        st = ExitStack()
        strip = st.enter_context(self.sbuf("p0_strip", [128, 2 * D], F32))
        kcol = st.enter_context(self.sbuf("p0_kcol", [128, 2], F32))
        psT = [st.enter_context(self.psum(f"p0_psT{i}", [128, 512], F32))
               for i in range(2)]
        psW = st.enter_context(self.psum("p0_psW", [128, 128], F32))

        l_sl = sc.dma("sp", self.slope_sb[:, :], self.slope.ap())
        sc.dma("sp", self.selr[:, :], self.kvsel.ap())
        for dst, src in ((self.nw, self.nw_c), (self.dv, self.dv_c),
                         (self.gsc, self.gsc_c), (self.usc, self.usc_c),
                         (self.dsc, self.dsc_c), (self.idf, self.ident)):
            sc.dma("sp", dst[:, :], src.ap())
        sc.dma("pool", self.idb[:, :], self.ident.ap())
        d_ms = 0
        for buf, val in ((self.ones_r, 1.0), (self.ones_cf, 1.0),
                         (self.ones_cb, 1.0), (self.m128_r, -128.0)):
            d_ms = sc.emit("dve", lambda e, b=buf, v=val: e.memset(b[:, :], v),
                           inc="dve")
        d_ms = sc.emit("dve", lambda e: e.memset(self.kv[:, :], 0.0), inc="dve")
        self.kv_ms = d_ms
        d_ms = sc.emit("dve", lambda e: e.memset(self.epsv[0:1, 0:1], LN_EPS),
                       inc="dve")
        d_ms = sc.emit("dve", lambda e: e.memset(self.epsv[0:1, 1:2], RMS_EPS),
                       inc="dve")
        t_sf = sc.emit("pe", lambda e: e.matmul(psW[:, 0:H], self.ones_r[:, :],
                                                self.slope_sb[:, :],
                                                start=True, stop=True),
                       wait=[("lx", sc.cnt["lx"]), ("dve", d_ms)], inc="pe")
        d_sfc = sc.emit("dve", lambda e: e.tensor_copy(self.SF[:, :], psW[:, 0:H]),
                        wait=[("pe", t_sf)], inc="dve")
        d_sf = sc.emit("dve", lambda e: e.tensor_scalar(
            self.SFn[:, :], self.SF[:, :], -1.0, None, ALU.mult), inc="dve")
        p_io = sc.emit("pool", lambda e: e.iota(
            kcol[:, 0:1], [[0, 1]], base=-255, channel_multiplier=1,
            allow_small_or_imprecise_dtypes=True), inc="pool")
        p_io = sc.emit("pool", lambda e: e.iota(
            kcol[:, 1:2], [[0, 1]], base=-127, channel_multiplier=1,
            allow_small_or_imprecise_dtypes=True), inc="pool")
        t_se = sc.emit("pe", lambda e: e.matmul(psW[:, 32:40], self.ones_r[:, :],
                                                self.selr[:, :],
                                                start=True, stop=True),
                       wait=[("dve", d_sfc)], inc="pe")
        sc.emit("dve", lambda e: e.tensor_copy(self.selb[:, :], psW[:, 32:40]),
                wait=[("pe", t_se)], inc="dve")
        a_bd = 0
        for h in range(H):
            for c in range(2):
                sc.emit("act", lambda e, h=h, c=c:
                        e.activation(self.KD[:, h * 2 + c:h * 2 + c + 1],
                                     kcol[:, c:c + 1], AF.Exp,
                                     scale=self.SF[:, h:h + 1]),
                        wait=[("dve", d_sf), ("pool", p_io)], inc="act")
            a_bd = sc.emit("act", lambda e, h=h:
                           e.activation(self.BD[:, h:h + 1], self.SF[:, h:h + 1],
                                        AF.Exp, scale=-256.0), inc="act")
        p_b0 = sc.emit("pool", lambda e: e.memset(self.BDg[:, 0:H], 1.0),
                       inc="pool")
        d_b = sc.emit("dve", lambda e: e.tensor_copy(self.BDg[:, H:2 * H],
                                                     self.BD[:, :]),
                      wait=[("act", a_bd)], inc="dve")
        d_b = sc.emit("dve", lambda e: e.tensor_mul(
            self.BDg[:, 2 * H:3 * H], self.BDg[:, H:2 * H], self.BD[:, :]),
            inc="dve")
        d_b = sc.emit("dve", lambda e: e.tensor_mul(
            self.BDg[:, 3 * H:4 * H], self.BDg[:, 2 * H:3 * H], self.BD[:, :]),
            inc="dve")
        self.bdg_done = [("dve", d_b), ("pool", p_b0)]
        self.load_feature_major(self.xo, 0, TOK, strip, psT,
                                lambda c, tt, L: self.xsl(c, tt, L),
                                extra_wait=[("lx", sc.cnt["lx"])])
        self.maybe_conv(12)
        sc.waitonly("pe", [("dve", d_ms)])
        sc.waitonly("act", [("dve", d_ms)])
        sc.waitonly("pool", [("dve", d_ms)])
        self.p0_pool = sc.cnt["pool"]
        sc.waitonly("sp", [("lx", sc.cnt["lx"])])
        sc.waitonly("pool", [("ld", sc.cnt["ld"])])
        sc.flush()
        st.close()

    def p2(self):
        sc, nc = self.sc, self.nc
        st = ExitStack()
        qkvctx = ExitStack()
        lnctx = ExitStack()
        s1 = st.enter_context(self.sbuf("p2_s1", [128, 512], F32))
        s2 = st.enter_context(self.sbuf("p2_s2", [128, 512], F32))
        rgt = st.enter_context(self.sbuf("p2_rgt", [128, 256], BF16))
        qb = qkvctx.enter_context(self.sbuf("p2_q", [128, NF * HALF], BF16))
        kb = qkvctx.enter_context(self.sbuf("p2_k", [128, NF * HALF], BF16))
        vT = qkvctx.enter_context(self.sbuf("p2_vT", [128, 4 * D], BF16))
        ln0 = lnctx.enter_context(self.sbuf("p2_ln0", [128, NF * HALF], BF16))
        ln1 = lnctx.enter_context(self.sbuf("p2_ln1", [128, NF * HALF], BF16))
        rowb = None
        sqtA = None
        psS = [st.enter_context(self.psum(f"p2_psS{i}", [128, 512], F32))
               for i in range(2)]
        psS_free = [("dve", 0), ("dve", 0)]
        blk_last = [0, 0]

        def lcw(j):
            assert j < len(self.lc_hist), f"conv job {j} not yet emitted"
            return ("lc", self.lc_hist[j])

        blk_cv = [0, 0]

        def std_block(wdram, blk0, ofb, dst, act_fn, wst, ln_t, ln_done, lc_j,
                      evac_war=None, preload=None):
            par = ofb % 2
            wsl = wst[:, par * NF * 256:(par + 1) * NF * 256]
            if preload is None:
                l_w = sc.dma(
                    "pool", wsl.rearrange("p (c f) -> p c f", c=NF),
                    wdram.ap()[blk0 + ofb],
                    wait=[("pe", blk_last[par]), lcw(lc_j)], sem=f"q{2 + par}")
                wrdy = (f"q{2 + par}", l_w)
            else:
                wrdy = preload
            self.maybe_conv(2)
            for ofh in range(2):
                u = (ofb * 2 + ofh) % 2
                for kc in range(NF):
                    w = []
                    if kc == 0:
                        w = [wrdy, psS_free[u], ln_done]
                    sc.emit("pe", lambda e, o=psS[u][:, 0:HALF],
                            a=wsl[:, kc * 256 + ofh * 128:
                                  kc * 256 + ofh * 128 + 128],
                            b=ln_t[:, kc * HALF:(kc + 1) * HALF],
                            s=(kc == 0), t=(kc == NF - 1):
                            e.matmul(o, a, b, start=s, stop=t),
                            wait=w, inc="pe" if kc == NF - 1 else None)
                tmm = sc.cnt["pe"]
                blk_last[par] = tmm
                oc = ofb * 2 + ofh
                w2 = [("pe", tmm)]
                if evac_war is not None:
                    w2.append(evac_war)
                a_ev = sc.emit("act",
                               lambda e, o=dst[:, oc * HALF:(oc + 1) * HALF],
                               i=psS[u][:, 0:HALF], f=act_fn:
                               e.activation(o, i, f),
                               wait=w2, inc="act")
                psS_free[u] = ("act", a_ev)

        def v_block(ofb, wst, ln_t, ln_done, evac_war=None):
            par = ofb % 2
            wsl = wst[:, par * NF * 256:(par + 1) * NF * 256]
            l_w = sc.dma(
                "pool", wsl.rearrange("p (c f) -> p c f", c=NF),
                self.wqkvB.ap()[16 + ofb],
                wait=[("pe", blk_last[par]), lcw(4 + ofb // 2)],
                sem=f"q{2 + par}")
            self.maybe_conv(2)
            for mc in range(4):
                u = mc % 2
                for kc in range(NF):
                    w = []
                    if kc == 0:
                        w = [(f"q{2 + par}", l_w), psS_free[u], ln_done]
                    sc.emit("pe", lambda e, o=psS[u][:, 0:256],
                            a=ln_t[:, kc * HALF + mc * 128:
                                  kc * HALF + mc * 128 + 128],
                            b=wsl[:, kc * 256:(kc + 1) * 256],
                            s=(kc == 0), t=(kc == NF - 1):
                            e.matmul(o, a, b, start=s, stop=t),
                            wait=w, inc="pe" if kc == NF - 1 else None)
                tmm = sc.cnt["pe"]
                blk_last[par] = tmm
                w2 = [("pe", tmm)]
                if evac_war is not None:
                    w2.append(evac_war)
                a_ev = sc.emit("act", lambda e,
                               o=vT[:, mc * D + ofb * 256: mc * D + (ofb + 1) * 256],
                               i=psS[u][:, 0:256]: e.activation(o, i, AF.Silu),
                               wait=w2, inc="act")
                psS_free[u] = ("act", a_ev)

        def gate_block(ofb, wst, gateS, ld0, ld1):
            par = ofb % 2
            wsl = wst[:, par * NF * 256:(par + 1) * NF * 256]
            l_w = sc.dma(
                "pool", wsl.rearrange("p (c f) -> p c f", c=NF),
                self.wgateB.ap()[ofb],
                wait=[("pe", blk_last[par]), lcw(8 + ofb // 2)],
                sem=f"q{2 + par}")
            self.maybe_conv(2)
            for si, (ln_t, ldone) in enumerate(((ln0, ld0), (ln1, ld1))):
                for ofh in range(2):
                    u = (si * 2 + ofh) % 2
                    for kc in range(NF):
                        w = []
                        if kc == 0:
                            w = [(f"q{2 + par}", l_w), psS_free[u], ldone]
                        sc.emit("pe", lambda e, o=psS[u][:, 0:HALF],
                                a=wsl[:, kc * 256 + ofh * 128:
                                      kc * 256 + ofh * 128 + 128],
                                b=ln_t[:, kc * HALF:(kc + 1) * HALF],
                                s=(kc == 0), t=(kc == NF - 1):
                                e.matmul(o, a, b, start=s, stop=t),
                                wait=w, inc="pe" if kc == NF - 1 else None)
                    tmm = sc.cnt["pe"]
                    blk_last[par] = tmm
                    oc = ofb * 2 + ofh
                    dsl = gateS[si][:, oc * HALF:(oc + 1) * HALF]
                    a_ev = sc.emit("act",
                                   lambda e, o=dsl, i=psS[u][:, 0:HALF]:
                                   e.activation(o, i, AF.Sigmoid),
                                   wait=[("pe", tmm)], inc="act")
                    psS_free[u] = ("act", a_ev)
                    self._g_war = sc.dma(
                        "sp", self.gateD.ap()[si, :, oc * HALF:(oc + 1) * HALF],
                        dsl, wait=[("act", a_ev)], sem="q7")

        def emit_decay(stX, war):
            QD = stX.enter_context(self.sbuf("p2_QD", [128, H * 256], BF16))
            DDt = stX.enter_context(self.sbuf("p2_DD", [128, 2 * H * 256], BF16))
            it = stX.enter_context(self.sbuf("p2_it", [128, 256], F32))
            p_io = sc.emit("pool", lambda e: e.iota(
                it[:, 0:256], [[1, 256]], base=1, channel_multiplier=0,
                allow_small_or_imprecise_dtypes=True), wait=list(war),
                inc="pool")
            a_qd = 0
            for h in range(H):
                a_qd = sc.emit("act", lambda e, h=h:
                               e.activation(QD[:, h * 256:(h + 1) * 256],
                                            it[:, 0:256], AF.Exp,
                                            scale=self.SFn[:, h:h + 1]),
                               wait=[("pool", p_io)], inc="act")
            a_dd = 0
            for c in range(2):
                p_io = sc.emit("pool", lambda e, c=c: e.iota(
                    it[:, 0:256], [[-1, 256]], base=128 * c, channel_multiplier=1,
                    allow_small_or_imprecise_dtypes=True),
                    wait=[("act", a_qd if c == 0 else a_dd)], inc="pool")
                for h in range(H):
                    a_dd = sc.emit("act", lambda e, h=h, c=c:
                                   e.activation(
                                       DDt[:, (h * 2 + c) * 256:(h * 2 + c + 1) * 256],
                                       it[:, 0:256], AF.Exp,
                                       scale=self.SF[:, h:h + 1]),
                                   wait=[("pool", p_io)], inc="act")
            p_dd = 0
            for h in range(H):
                for c in range(2):
                    p_dd = sc.emit("pool", lambda e, h=h, c=c:
                                   e.affine_select(
                                       DDt[:, (h * 2 + c) * 256:(h * 2 + c + 1) * 256],
                                       DDt[:, (h * 2 + c) * 256:(h * 2 + c + 1) * 256],
                                       [[1, 256]], ALU.is_ge, 0.0,
                                       base=-(128 * c), channel_multiplier=-1),
                                   wait=[("act", a_dd)], inc="pool")
            return QD, DDt, a_qd, p_dd

        def emit_scan2(p, sb):
            QD, DDt, a_qd, p_dd = sb["decay"]
            aow, qhw = sb["aow"], sb["qhw"]
            khat, kvb, sS, qh = sb["khat"], sb["kvb"], sb["sS"], sb["qh"]
            psTr, psO, psKV = sb["psTr"], sb["psO"], sb["psKV"]
            b_done, kvmm_done = sb["b_done"], sb["kvmm"]
            g_done, j_done, i_done = sb["g"], sb["j"], sb["i"]
            for jj in range(2):
                g = 2 * p + jj
                tc = jj * 256
                for h in range(H):
                    if h % 2 == 0:
                        self.maybe_conv()
                    P = h % 2
                    for c in range(2):
                        w = [("dve", b_done[P])] if c == 0 else []
                        sc.emit("pe", lambda e, o=psTr[P][:, c * 128:(c + 1) * 128],
                                i=kb[:, h * HALF + tc + c * 128:
                                     h * HALF + tc + c * 128 + 128]:
                                e.transpose(o, i, self.idb[:, :]), wait=w,
                                inc="pe" if c == 1 else None)
                    ta = sc.cnt["pe"]
                    for c in range(2):
                        w = [("pe", ta), ("pe", kvmm_done[P])] if c == 0 else []
                        b_done[P] = sc.emit(
                            "dve", lambda e,
                            o=khat[:, P * 256 + c * 128: P * 256 + c * 128 + 128],
                            i=psTr[P][:, c * 128:(c + 1) * 128],
                            s=self.KD[:, h * 2 + c:h * 2 + c + 1]:
                            e.tensor_scalar(o, i, s, None, ALU.mult),
                            wait=w, inc="dve")
                    for c in range(2):
                        w = [psS_free[P]] if c == 0 else []
                        sc.emit("pe", lambda e, o=psS[P][:, c * 256:(c + 1) * 256],
                                a=kb[:, h * HALF + tc + c * 128:
                                     h * HALF + tc + c * 128 + 128],
                                b=qb[:, h * HALF + tc: h * HALF + tc + 256]:
                                e.matmul(o, a, b, start=True, stop=True),
                                wait=w, inc="pe" if c == 1 else None)
                    tcc = sc.cnt["pe"]
                    dd = 0
                    for c in range(2):
                        w = []
                        if c == 0:
                            w = [("pe", tcc), ("pe", g_done[P]), ("pool", p_dd)]
                        dd = sc.emit("dve", lambda e,
                                     o=sS[:, P * 512 + c * 256: P * 512 + c * 256 + 256],
                                     a=psS[P][:, c * 256:(c + 1) * 256],
                                     b=DDt[:, (h * 2 + c) * 256:(h * 2 + c + 1) * 256]:
                                     e.tensor_mul(o, a, b), wait=w, inc="dve")
                    psS_free[P] = ("dve", dd)
                    ve = sc.emit("pool", lambda e, o=qh[:, P * 256:P * 256 + 256],
                                 a=qb[:, h * HALF + tc: h * HALF + tc + 256],
                                 b=QD[:, h * 256:(h + 1) * 256]:
                                 e.tensor_mul(o, a, b),
                                 wait=[("act", a_qd), ("pe", g_done[P])],
                                 inc="pool")
                    wq = []
                    if h < 2:
                        wq = [("q5", self._qh_war)]
                        if g == 0:
                            wq += self.bdg_done
                    sc.emit("pool", lambda e, o=qhw[:, h * 256:(h + 1) * 256],
                            a=qh[:, P * 256:P * 256 + 256],
                            s=self.BDg[:, g * H + h:g * H + h + 1]:
                            e.tensor_scalar(o, a, s, None, ALU.mult),
                            wait=wq, inc="pool")
                    wv = [("dve", max(self.kv_stt[h], self.kv_ms)),
                          ("pe", g_done[P])]
                    vf = sc.emit("act", lambda e, o=kvb[:, P * 128:P * 128 + 128],
                                 i=self.kv[:, h * DH:(h + 1) * DH]:
                                 e.activation(o, i, AF.Copy),
                                 wait=wv, inc="act")
                    sc.emit("pe", lambda e, o=psO[P][:, :],
                            a=kvb[:, P * 128:P * 128 + 128],
                            b=qh[:, P * 256:P * 256 + 256]:
                            e.matmul(o, a, b, start=True, stop=False),
                            wait=[("act", vf), ("act", j_done[P]),
                                  ("pool", ve), ("dve", dd)])
                    for c in range(2):
                        sc.emit("pe", lambda e, o=psO[P][:, :],
                                a=vT[:, (jj * 2 + c) * D + h * 128:
                                     (jj * 2 + c) * D + h * 128 + 128],
                                b=sS[:, P * 512 + c * 256: P * 512 + c * 256 + 256],
                                t=(c == 1):
                                e.matmul(o, a, b, start=False, stop=t),
                                inc="pe" if c == 1 else None)
                    tg = sc.cnt["pe"]
                    g_done[P] = tg
                    for c in range(2):
                        w = []
                        if c == 0:
                            w = [("dve", b_done[P]), ("dve", i_done[P])]
                        sc.emit("pe", lambda e, o=psKV[P][:, :],
                                a=khat[:, P * 256 + c * 128: P * 256 + c * 128 + 128],
                                b=vT[:, (jj * 2 + c) * D + h * 128:
                                     (jj * 2 + c) * D + h * 128 + 128],
                                s=(c == 0), t=(c == 1):
                                e.matmul(o, a, b, start=s, stop=t),
                                wait=w, inc="pe" if c == 1 else None)
                    th = sc.cnt["pe"]
                    kvmm_done[P] = th
                    di = sc.emit("dve", lambda e, o=self.kv[:, h * DH:(h + 1) * DH],
                                 s=self.BD[:, h:h + 1], i=psKV[P][:, :]:
                                 e.scalar_tensor_tensor(o, o, s, i, ALU.mult, ALU.add),
                                 wait=[("pe", th), ("act", vf)], inc="dve")
                    self.kv_stt[h] = di
                    i_done[P] = di
                    w3 = [("pe", tg)]
                    if h < 2:
                        w3.append(("q4", self._ao_war))
                    j_done[P] = sc.emit("act",
                                        lambda e, o=aow[:, h * 256:(h + 1) * 256],
                                        i=psO[P][:, :]: e.activation(o, i, AF.Copy),
                                        wait=w3, inc="act")
                q_last = sc.cnt["pool"]
                self._ao_war = sc.dma("sp", self.aoD.ap()[g], aow[:, :],
                                      wait=[("act", max(j_done))], sem="q4")
                self._qh_war = sc.dma("sp", self.qhD.ap()[g], qhw[:, :],
                                      wait=[("pool", q_last)], sem="q5")

        def emit_rg(aot, coff, gate_t, tcg, first_wait):
            aos = [aot[:, coff + c * 256: coff + (c + 1) * 256]
                   for c in range(NF)]
            sc.emit("act", lambda e, o=s2[:, 0:256], i=aos[0]:
                    e.activation(o, i, AF.Square),
                    wait=first_wait, inc="act")
            d_s2 = 0
            sq_war = [0, 0]
            for c in range(1, NF):
                P2 = c % 2
                a = sc.emit("act", lambda e, o=sqtA[:, P2 * 512:P2 * 512 + 256],
                            i=aos[c]: e.activation(o, i, AF.Square),
                            wait=[("dve", sq_war[P2])], inc="act")
                d_s2 = sc.emit("dve", lambda e, o=s2[:, 0:256],
                               i=sqtA[:, P2 * 512:P2 * 512 + 256]:
                               e.tensor_add(o, o, i),
                               wait=[("act", a)], inc="dve")
                sq_war[P2] = d_s2
            t1 = sc.emit("pe", lambda e, o=psS[0][0:1, 0:256],
                         a=self.ones_cf[:, :], b=s2[:, 0:256]:
                         e.matmul(o, a, b, start=True, stop=True),
                         wait=[("dve", d_s2), psS_free[0]], inc="pe")
            d = sc.emit("dve", lambda e, o=rowb[0:1, 0:256],
                        i=psS[0][0:1, 0:256]:
                        e.tensor_scalar(o, i, 1.0 / 2048.0, None, ALU.mult),
                        wait=[("pe", t1)], inc="dve")
            a_sd = sc.emit("act", lambda e, o=rowb[0:1, 512:768],
                           i=rowb[0:1, 0:256]:
                           e.activation(o, i, AF.Sqrt, bias=self.epsv[0:1, 1:2]),
                           wait=[("dve", d)], inc="act")
            d_rs = sc.emit("dve", lambda e, o=rowb[0:1, 1024:1280],
                           i=rowb[0:1, 512:768]: e.reciprocal(o, i),
                           wait=[("act", a_sd)], inc="dve")
            tB = sc.emit("pe", lambda e, o=psS[1][:, 0:256], a=self.ones_r[:, :],
                         b=rowb[0:1, 1024:1280]:
                         e.matmul(o, a, b, start=True, stop=True),
                         wait=[("dve", d_rs), psS_free[1]], inc="pe")
            d_bc = sc.emit("dve", lambda e, o=s1[:, 0:256], i=psS[1][:, 0:256]:
                           e.tensor_copy(o, i), wait=[("pe", tB)], inc="dve")
            psS_free[0] = ("dve", d)
            psS_free[1] = ("dve", d_bc)
            rg_war = 0
            for c in range(NF):
                dm = sc.emit("dve", lambda e, o=rgt[:, 0:256],
                             a=aos[c], s=self.nw[:, c:c + 1], b=s1[:, 0:256]:
                             e.scalar_tensor_tensor(o, a, s, b, ALU.mult, ALU.mult),
                             wait=[("dve", d_bc), ("pool", rg_war)],
                             inc="dve")
                rg_war = sc.emit("pool", lambda e,
                                 o=gate_t[:, c * HALF + tcg: c * HALF + tcg + 256],
                                 a=rgt[:, 0:256]:
                                 e.tensor_mul(o, a, o),
                                 wait=[("dve", dm)], inc="pool")
            return sc.cnt["pool"]

        def emit_wout(p, rg_done, wst, gate_t, x_rdy):
            t0 = p * HALF
            for ofb in range(8):
                par = ofb % 2
                wsl = wst[:, par * NF * 256:(par + 1) * NF * 256]
                l_w = sc.dma(
                    "pool", wsl.rearrange("p (c f) -> p c f", c=NF),
                    self.woutB.ap()[ofb],
                    wait=[("pe", blk_last[par]), lcw(12 + ofb // 2)],
                    sem=f"q{2 + par}")
                self.maybe_conv(2)
                for ofh in range(2):
                    u = (ofb * 2 + ofh) % 2
                    for kc in range(NF):
                        w = []
                        if kc == 0:
                            w = [(f"q{2 + par}", l_w), psS_free[u],
                                 ("pool", rg_done)]
                        sc.emit("pe", lambda e, o=psS[u][:, 0:HALF],
                                a=wsl[:, kc * 256 + ofh * 128:
                                      kc * 256 + ofh * 128 + 128],
                                b=gate_t[:, kc * HALF:(kc + 1) * HALF],
                                s=(kc == 0), t=(kc == NF - 1):
                                e.matmul(o, a, b, start=s, stop=t),
                                wait=w, inc="pe" if kc == NF - 1 else None)
                    tmm = sc.cnt["pe"]
                    blk_last[par] = tmm
                    oc = ofb * 2 + ofh
                    d_x = sc.emit("dve", lambda e, o=self.xsl(oc, t0, HALF),
                                  i=psS[u][:, 0:HALF]: e.tensor_add(o, o, i),
                                  wait=[("pe", tmm)] + x_rdy, inc="dve")
                    psS_free[u] = ("dve", d_x)

        stA = ExitStack()
        wstA = stA.enter_context(self.sbuf("p2_wstA", [128, 2 * NF * 256], BF16))
        rowb = stA.enter_context(self.sbuf("p2_rowbA", [1, 2048], F32))
        sqtA = stA.enter_context(self.sbuf("p2_sqtAA", [128, 1024], F32))
        psA = stA.enter_context(self.psum("p2_psA", [128, 512], F32))
        psB = stA.enter_context(self.psum("p2_psB", [128, 512], F32))
        l0s = [ln0[:, c * HALF:(c + 1) * HALF] for c in range(NF)]
        l1s = [ln1[:, c * HALF:(c + 1) * HALF] for c in range(NF)]
        ln_t0 = self.emit_ln([self.xsl(c, 0, HALF) for c in range(NF)],
                             l0s, HALF, (s1, s2, sqtA, rowb, psA, psB))
        ln_d0 = ("pool", ln_t0)
        res1 = {}
        lngen = self.emit_ln_gen([self.xsl(c, HALF, HALF) for c in range(NF)],
                                 l1s, HALF, (s1, s2, sqtA, rowb, psA, psB),
                                 res1, sum_eng="dve", ap2_eng="dve",
                                 war=[("pool", ln_t0)])

        def drive(n):
            nonlocal lngen
            if lngen is None:
                return
            for _ in range(n):
                if next(lngen, "END") == "END":
                    lngen = None
                    break

        qstg = stA.enter_context(self.sbuf("p2_qstg", [128, 4096], F32))
        for ofb in range(8):
            par = ofb % 2
            wsl = wstA[:, par * NF * 256:(par + 1) * NF * 256]
            l_f = sc.dma("sp", qstg[:, :].rearrange("p (c f) -> p c f", c=NF),
                         self.wqkvT.ap()[ofb],
                         wait=[("dve", blk_cv[0])], sem=f"q{2 + par}")
            cv = sc.emit("dve", lambda e, o=wsl, i=qstg[:, :]:
                         e.tensor_copy(o, i),
                         wait=[(f"q{2 + par}", l_f), ("pe", blk_last[par])],
                         inc="dve")
            blk_cv[0] = cv
            std_block(self.wqkvB, 0, ofb, qb, AF.Silu, wstA, ln0, ln_d0,
                      None, preload=("dve", cv))
            drive(3)
        for ofb in range(8):
            std_block(self.wqkvB, 8, ofb, kb, AF.Silu, wstA, ln0, ln_d0,
                      ofb // 2)
            drive(3)
        for ofb in range(8):
            v_block(ofb, wstA, ln0, ln_d0)
            drive(3)
        drive(100)
        ln_d1 = res1["done"]
        sc.flush()
        stA.close()

        def mk_scanctx(war, trk):
            ctx = ExitStack()
            sb = {"decay": emit_decay(ctx, war)}
            sb["aow"] = ctx.enter_context(self.sbuf("p2_aow", [128, H * BLK], BF16))
            sb["qhw"] = ctx.enter_context(self.sbuf("p2_qhw", [128, H * BLK], BF16))
            sb["khat"] = ctx.enter_context(self.sbuf("p2_khat", [128, 2 * 256], BF16))
            sb["kvb"] = ctx.enter_context(self.sbuf("p2_kvb", [128, 2 * 128], BF16))
            sb["sS"] = ctx.enter_context(self.sbuf("p2_sS", [128, 2 * 512], BF16))
            sb["qh"] = ctx.enter_context(self.sbuf("p2_qh", [128, 2 * 256], BF16))
            sb["psTr"] = [ctx.enter_context(self.psum(f"p2_psTr{i}", [128, 256], BF16))
                          for i in range(2)]
            sb["psO"] = [ctx.enter_context(self.psum(f"p2_psO{i}", [128, 256], F32))
                         for i in range(2)]
            sb["psKV"] = [ctx.enter_context(self.psum(f"p2_psKV{i}", [128, 128], F32))
                          for i in range(2)]
            for k, v in trk.items():
                sb[k] = v
            return ctx, sb
        trk = {"b_done": [0, 0], "kvmm": [0, 0], "g": [0, 0],
               "j": [0, 0], "i": [0, 0]}
        scanB, sb = mk_scanctx([ln_d1], trk)
        emit_scan2(0, sb)
        sc.flush()
        scanB.close()

        stC = ExitStack()
        wstC = stC.enter_context(self.sbuf("p2_wstC", [128, 2 * NF * 256], BF16))
        gateS = [stC.enter_context(self.sbuf(f"p2_gS{i}", [128, NF * HALF], BF16))
                 for i in range(2)]
        pool_snap = sc.cnt["pool"]
        for ofb in range(8):
            std_block(self.wqkvB, 0, ofb, qb, AF.Silu, wstC, ln1, ln_d1,
                      20 + ofb // 2,
                      evac_war=("pool", pool_snap) if ofb < 2 else None)
        for ofb in range(8):
            std_block(self.wqkvB, 8, ofb, kb, AF.Silu, wstC, ln1, ln_d1,
                      ofb // 2)
        for ofb in range(8):
            v_block(ofb, wstC, ln1, ln_d1)
        for ofb in range(8):
            gate_block(ofb, wstC, gateS, ln_d0, ln_d1)
        sc.flush()
        stC.close()
        lnctx.close()

        scanD, sb = mk_scanctx([("q7", self._g_war), ("q4", self._ao_war),
                                ("q5", self._qh_war)], trk)
        kvbS = scanD.enter_context(self.sbuf("p2_kvbS", [128, H * DH], BF16))
        emit_scan2(1, sb)
        kvw = 0
        for h in range(H):
            kvw = max(kvw, self.kv_stt[h])
        a_ks = sc.emit("act", lambda e: e.activation(kvbS[:, :], self.kv[:, :],
                                                     AF.Copy),
                       wait=[("dve", kvw)], inc="act")
        t_snd = sc.dma("sp", self.kvsnd.ap(), kvbS[:, :],
                       wait=[("act", a_ks)], sem="q6")
        cc_t = sc.emit("pool", lambda e: e.collective_compute(
            "AllGather", ALU.bypass, replica_groups=[list(range(8))],
            ins=[self.kvsnd.ap().opt()], outs=[self.kvrcv.ap().opt()]),
            wait=[("q6", t_snd)], inc="pool")
        sc.flush()
        scanD.close()
        qkvctx.close()

        stE = ExitStack()
        x_rdy = []
        rowb = stE.enter_context(self.sbuf("p2_rowbE", [1, 2048], F32))
        sqtA = stE.enter_context(self.sbuf("p2_sqtAE", [128, 1024], F32))
        ktmp = stE.enter_context(self.sbuf("p2e_ktmp", [128, 2 * 2048], BF16))
        kvinB = stE.enter_context(self.sbuf("p2e_kvinB", [128, 2048], BF16))

        def emit_kvinit():
            d_acc = 0
            d_prev = 0
            for i, r in enumerate((0, 2, 4, 6)):
                kt = ktmp[:, (i % 2) * 2048:(i % 2) * 2048 + 2048]
                t_l = sc.dma("sp", kt,
                             self.kvrcv.ap()[r * 128:(r + 1) * 128, :],
                             wait=[("pool", cc_t), ("dve", d_prev)], sem="q6")
                d_prev = d_acc
                if i == 0:
                    d_acc = sc.emit("dve", lambda e, kt=kt: e.tensor_scalar(
                        kvinB[:, :], kt, self.selb[:, 0:1], None, ALU.mult),
                        wait=[("q6", t_l)], inc="dve")
                else:
                    d_acc = sc.emit(
                        "dve", lambda e, kt=kt, r=r: e.scalar_tensor_tensor(
                            kvinB[:, :], kt, self.selb[:, r:r + 1], kvinB[:, :],
                            ALU.mult, ALU.add), wait=[("q6", t_l)], inc="dve")
            return d_acc
        a_kb = None

        psC = [stE.enter_context(self.psum(f"p2e_psC{i}", [128, 256], F32))
               for i in range(4)]
        wstE = stE.enter_context(self.sbuf("p2e_wst", [128, 2 * NF * 256], BF16))
        for p in range(2):
            stP = ExitStack()
            aor = stP.enter_context(self.sbuf("p2e_aor", [128, 2 * H * BLK], BF16))
            qhr = stP.enter_context(self.sbuf("p2e_qhr", [128, 2 * H * BLK], BF16))
            gate_t = stP.enter_context(self.sbuf("p2e_gt", [128, NF * HALF], BF16))
            snap = [("dve", sc.cnt["dve"]), ("act", sc.cnt["act"]),
                    ("pool", sc.cnt["pool"]), ("pe", sc.cnt["pe"])]
            rst = []
            for jj in range(2):
                g = 2 * p + jj
                rst.append((
                    sc.dma("sp", aor[:, jj * 4096:(jj + 1) * 4096],
                           self.aoD.ap()[g],
                           wait=[("q4", self._ao_war)] + snap, sem="q4"),
                    sc.dma("sp", qhr[:, jj * 4096:(jj + 1) * 4096],
                           self.qhD.ap()[g],
                           wait=[("q5", self._qh_war)] + snap, sem="q5")))
            t_g = sc.dma("sp", gate_t[:, :], self.gateD.ap()[p],
                         wait=[("q7", self._g_war)] + snap, sem="q7")
            sc.waitonly("pool", [("q7", t_g)])
            if p == 0:
                a_kb = emit_kvinit()
            corr_war = [("dve", a_kb)] * 4
            rgd = 0
            for jj in range(2):
                tcg = jj * 256
                t_ao, t_qh = rst[jj]
                d_last = 0
                for h in range(H):
                    Pc = h % 4
                    w = [corr_war[Pc]]
                    if h < 4:
                        w += [("dve", a_kb), ("q5", t_qh)] + x_rdy
                    t_mm = sc.emit("pe", lambda e, o=psC[Pc][:, :],
                                   a=kvinB[:, h * DH:(h + 1) * DH],
                                   b=qhr[:, jj * 4096 + h * 256:
                                         jj * 4096 + (h + 1) * 256]:
                                   e.matmul(o, a, b, start=True, stop=True),
                                   wait=w, inc="pe")
                    wd = [("pe", t_mm)]
                    if h < 4:
                        wd.append(("q4", t_ao))
                    d_c = sc.emit("dve", lambda e,
                                  o=aor[:, jj * 4096 + h * 256:
                                        jj * 4096 + (h + 1) * 256],
                                  i=psC[Pc][:, :]:
                                  e.tensor_add(o, o, i),
                                  wait=wd, inc="dve")
                    corr_war[Pc] = ("dve", d_c)
                    d_last = d_c
                rgd = emit_rg(aor, jj * 4096, gate_t, tcg, [("dve", d_last)])
            emit_wout(p, rgd, wstE, gate_t, x_rdy)
            sc.flush()
            stP.close()
        sc.flush()
        stE.close()
        st.close()

    def ffn(self, tag):
        sc, nc = self.sc, self.nc
        st = ExitStack()
        ln2 = st.enter_context(self.sbuf(f"{tag}_ln2", [128, 2 * NF * HALF], BF16))
        hfull = st.enter_context(self.sbuf(f"{tag}_h", [128, NI * HALF], BF16))
        rowb = st.enter_context(self.sbuf(f"{tag}_rowb", [1, 2048], F32))
        shilo = st.enter_context(self.sbuf(f"{tag}_shl", [1, 1024], BF16))
        s1 = st.enter_context(self.sbuf(f"{tag}_s1", [128, 512], F32))
        s2 = st.enter_context(self.sbuf(f"{tag}_s2", [128, 512], F32))
        sqtA = st.enter_context(self.sbuf(f"{tag}_sqtA", [128, 1024], F32))
        tmpG = st.enter_context(self.sbuf(f"{tag}_tG", [128, 512], F32))
        tmpU = st.enter_context(self.sbuf(f"{tag}_tU", [128, 512], F32))
        l2 = lambda t, c: ln2[:, (t * NF + c) * HALF:(t * NF + c + 1) * HALF]
        srow_f = s1[0:1, 0:HALF]
        srow_t = s2[0:1, 0:HALF]

        with ExitStack() as stl:
            psA = stl.enter_context(self.psum(f"{tag}_psA", [128, 512], F32))
            psB = stl.enter_context(self.psum(f"{tag}_psB", [128, 512], F32))
            xs0 = [self.xsl(c, 0, HALF) for c in range(NF)]
            ln_done = [("pool", self.emit_ln(xs0, [l2(0, c) for c in range(NF)],
                                             HALF, (s1, s2, sqtA, rowb, psA, psB))),
                       None]
            sc.flush()

        for th in range(2):
            t0 = th * HALF
            stG = ExitStack()
            psG = [stG.enter_context(self.psum(f"{tag}_psG{i}", [128, 512], F32))
                   for i in range(2)]
            psU = [stG.enter_context(self.psum(f"{tag}_psU{i}", [128, 512], F32))
                   for i in range(2)]
            psR = stG.enter_context(self.psum(f"{tag}_psR", [128, 512], F32))
            wg = stG.enter_context(self.sbuf(f"{tag}_wg", [128, 2 * NF * 128], BF16))
            wu = stG.enter_context(self.sbuf(f"{tag}_wu", [128, 2 * NF * 128], BF16))
            lngen = None
            resB = {}
            if th == 0:
                psA2 = stG.enter_context(self.psum(f"{tag}_psA2", [128, 512], F32))
                psB2 = stG.enter_context(self.psum(f"{tag}_psB2", [128, 512], F32))
                xs1 = [self.xsl(c, HALF, HALF) for c in range(NF)]
                lngen = self.emit_ln_gen(xs1, [l2(1, c) for c in range(NF)],
                                         HALF, (s1, s2, sqtA, rowb, psA2, psB2),
                                         resB, sum_eng="dve", ap2_eng="dve")
            if self.conv_done:
                sc.waitonly("pool", [("lc", self.conv_done)])
                self.conv_done = 0
            hmul_last = 0
            gevac = [0, 0]
            uevac = [0, 0]
            blk_last = [0, 0]
            tsr = 0
            pend_sr = []

            def emit_sr(ic, dh):
                nonlocal tsr
                tsr = sc.emit("pe", lambda e, o=psR[0:1, 0:HALF],
                              a=self.ones_cb[:, :],
                              b=hfull[:, ic * HALF:(ic + 1) * HALF],
                              s=(ic == 0), t=(ic == NI - 1):
                              e.matmul(o, a, b, start=s, stop=t),
                              wait=[("dve", dh)], inc="pe")
            for ic in range(NI):
                P = ic % 2
                wgs = wg[:, P * NF * 128:(P + 1) * NF * 128]
                wus = wu[:, P * NF * 128:(P + 1) * NF * 128]
                lg = sc.dma("pool", wgs.rearrange("p (c f) -> p c f", c=NF),
                            self.u8g.ap()[ic], wait=[("pe", blk_last[P])],
                            sem=f"q{4 + P}")
                lu = sc.dma("pool", wus.rearrange("p (c f) -> p c f", c=NF),
                            self.u8u.ap()[ic], wait=[("pe", blk_last[P])],
                            sem=f"q{6 + P}")
                for kc in range(NF):
                    w = []
                    if kc == 0:
                        w = [(f"q{4 + P}", lg), (f"q{6 + P}", lu),
                             ("act", gevac[P]), ln_done[th]]
                    sc.emit("pe", lambda e, o=psG[P][:, 0:HALF],
                            a=wgs[:, kc * 128:(kc + 1) * 128],
                            b=l2(th, kc),
                            s=(kc == 0), t=(kc == NF - 1):
                            e.matmul(o, a, b, start=s, stop=t),
                            wait=w, inc="pe" if kc == NF - 1 else None)
                tgG = sc.cnt["pe"]
                for kc in range(NF):
                    w = [("act", uevac[P])] if kc == 0 else []
                    sc.emit("pe", lambda e, o=psU[P][:, 0:HALF],
                            a=wus[:, kc * 128:(kc + 1) * 128],
                            b=l2(th, kc),
                            s=(kc == 0), t=(kc == NF - 1):
                            e.matmul(o, a, b, start=s, stop=t),
                            wait=w, inc="pe" if kc == NF - 1 else None)
                tgU = sc.cnt["pe"]
                blk_last[P] = tgU
                ag = sc.emit("act", lambda e, o=tmpG[:, :],
                             i=psG[P][:, 0:HALF], s=self.gsc[:, ic:ic + 1]:
                             e.activation(o, i, AF.Silu, scale=s),
                             wait=[("pe", tgG), ("dve", hmul_last)], inc="act")
                gevac[P] = ag
                au = sc.emit("act", lambda e, o=tmpU[:, :],
                             i=psU[P][:, 0:HALF], s=self.usc[:, ic:ic + 1]:
                             e.activation(o, i, AF.Copy, scale=s),
                             wait=[("pe", tgU), ("dve", hmul_last)], inc="act")
                uevac[P] = au
                dh = sc.emit("dve", lambda e, o=hfull[:, ic * HALF:(ic + 1) * HALF],
                             a=tmpG[:, :], b=tmpU[:, :]:
                             e.tensor_mul(o, a, b), wait=[("act", au)], inc="dve")
                hmul_last = dh
                pend_sr.append((ic, dh))
                if len(pend_sr) > 2:
                    emit_sr(*pend_sr.pop(0))
                if lngen is not None:
                    for _ in range(2):
                        if next(lngen, "END") == "END":
                            lngen = None
                            break
            if lngen is not None:
                for _ in lngen:
                    pass
            if th == 0:
                ln_done[1] = resB["done"]
            for args in pend_sr:
                emit_sr(*args)
            d_sr = sc.emit("dve", lambda e, o=srow_f, i=psR[0:1, 0:HALF]:
                           e.tensor_copy(o, i), wait=[("pe", tsr)], inc="dve")
            a_hi = sc.emit("act", lambda e, o=shilo[0:1, 0:HALF], i=srow_f:
                           e.activation(o, i, AF.Copy),
                           wait=[("dve", d_sr)], inc="act")
            d_lo = sc.emit("dve", lambda e, o=srow_t, a=srow_f,
                           b=shilo[0:1, 0:HALF]:
                           e.tensor_sub(o, a, b), wait=[("act", a_hi)], inc="dve")
            a_lo = sc.emit("act", lambda e, o=shilo[0:1, 512:512 + HALF],
                           i=srow_t: e.activation(o, i, AF.Copy),
                           wait=[("dve", d_lo)], inc="act")
            h_done = hmul_last
            sc.flush()
            stG.close()

            stD = ExitStack()
            psD = [stD.enter_context(self.psum(f"{tag}_psD{i}", [128, 512], F32))
                   for i in range(8)]
            wd = stD.enter_context(self.sbuf(f"{tag}_wd", [128, 2 * 4096], BF16))
            psD_free = [0] * 8
            xadd = [0, 0]
            wd_last = [0, 0]
            tmpE = [tmpG, tmpU]
            for dhh in range(2):
                for i4 in range(16):
                    par = i4 % 2
                    wds = wd[:, par * 4096:(par + 1) * 4096]
                    lw = sc.dma(
                        "pool", wds.rearrange("p (c f) -> p c f", c=4),
                        self.u8d.ap()[i4, dhh],
                        wait=[("pe", wd_last[par])], sem=f"q{4 + par}")
                    for c in range(4):
                        ic = i4 * 4 + c
                        for dc in range(8):
                            w = []
                            if dc == 0 and c == 0:
                                w.append((f"q{4 + par}", lw))
                                if dhh == 0 and i4 == 0:
                                    w.append(("dve", h_done))
                            if ic == 0:
                                w.append(("dve", psD_free[dc]))
                            sc.emit("pe", lambda e, o=psD[dc][:, :],
                                    a=wds[:, c * 1024 + dc * 128:
                                          c * 1024 + dc * 128 + 128],
                                    b=hfull[:, ic * HALF:(ic + 1) * HALF],
                                    s=(ic == 0):
                                    e.matmul(o, a, b, start=s, stop=False),
                                    wait=w, inc="pe" if dc == 7 else None)
                        wd_last[par] = sc.cnt["pe"]
                for dc in range(8):
                    w = [("act", a_lo)] if dc == 0 else []
                    sc.emit("pe", lambda e, o=psD[dc][:, :]:
                            e.matmul(o, self.m128_r[0:1, :], shilo[0:1, 0:HALF],
                                     start=False, stop=False), wait=w)
                    tcor = sc.emit("pe", lambda e, o=psD[dc][:, :]:
                                   e.matmul(o, self.m128_r[0:1, :],
                                            shilo[0:1, 512:512 + HALF],
                                            start=False, stop=True), inc="pe")
                    dcg = dhh * 8 + dc
                    P = dc % 2
                    a_ev = sc.emit("act",
                                   lambda e, o=tmpE[P][:, :],
                                   i=psD[dc][:, :], s=self.dsc[:, dcg:dcg + 1]:
                                   e.activation(o, i, AF.Copy, scale=s),
                                   wait=[("pe", tcor), ("dve", xadd[P])],
                                   inc="act")
                    d_x = sc.emit("dve", lambda e, o=self.xsl(dcg, t0, HALF),
                                  i=tmpE[P][:, :]: e.tensor_add(o, o, i),
                                  wait=[("act", a_ev)], inc="dve")
                    xadd[P] = d_x
                    psD_free[dc] = d_x
            sc.flush()
            stD.close()
        st.close()

    def p4(self):
        sc, nc = self.sc, self.nc
        st = ExitStack()
        ln3 = st.enter_context(self.sbuf("p4_ln3", [128, 2 * NF * HALF], BF16))
        t1b = st.enter_context(self.sbuf("p4_t1", [128, 8 * HALF], BF16))
        wst = st.enter_context(self.sbuf("p4_wst", [128, 2 * NF * 256], BF16))
        s1 = st.enter_context(self.sbuf("p4_s1", [128, 512], F32))
        s2 = st.enter_context(self.sbuf("p4_s2", [128, 512], F32))
        sqtA = st.enter_context(self.sbuf("p4_sqtA", [128, 1024], F32))
        rowb = st.enter_context(self.sbuf("p4_rowb", [1, 2048], F32))
        tmpC = st.enter_context(self.sbuf("p4_tmpC", [128, 1024], F32))
        psA = st.enter_context(self.psum("p4_psA", [128, 512], F32))
        psB = st.enter_context(self.psum("p4_psB", [128, 512], F32))
        psA2 = st.enter_context(self.psum("p4_psA2", [128, 512], F32))
        psB2 = st.enter_context(self.psum("p4_psB2", [128, 512], F32))
        ps = [psA, psB]
        l3 = lambda t, c: ln3[:, (t * NF + c) * HALF:(t * NF + c + 1) * HALF]

        xs0 = [self.xsl(c, 0, HALF) for c in range(NF)]
        ln_done = [("pool", self.emit_ln(xs0, [l3(0, c) for c in range(NF)],
                                         HALF, (s1, s2, sqtA, rowb, psA, psB))),
                   None]
        sc.flush()
        for th in range(2):
            t0 = th * HALF
            lngen = None
            resB = {}
            if th == 0:
                xs1 = [self.xsl(c, HALF, HALF) for c in range(NF)]
                lngen = self.emit_ln_gen(xs1, [l3(1, c) for c in range(NF)],
                                         HALF, (s1, s2, sqtA, rowb, psA2, psB2),
                                         resB, sum_eng="dve", ap2_eng="dve")

            def drive(n):
                nonlocal lngen
                if lngen is None:
                    return
                for _ in range(n):
                    if next(lngen, "END") == "END":
                        lngen = None
                        break

            ps_free = [ln_done[th], ln_done[th]]
            blk_last = [0, 0]
            for ofb in range(4):
                par = ofb % 2
                wsl = wst[:, par * NF * 256:(par + 1) * NF * 256]
                l_w = sc.dma(
                    "sp", wsl.rearrange("p (c f) -> p c f", c=NF),
                    self.bTB.ap()[ofb],
                    wait=[("pe", blk_last[par]),
                          ("lc", self.lc_hist[16 + ofb // 2])],
                    sem=f"q{2 + par}")
                for ofh in range(2):
                    u = (ofb * 2 + ofh) % 2
                    for kc in range(NF):
                        w = []
                        if kc == 0:
                            w = [(f"q{2 + par}", l_w), ps_free[u], ln_done[th]]
                        sc.emit("pe", lambda e, o=ps[u][:, 0:HALF],
                                a=wsl[:, kc * 256 + ofh * 128:
                                      kc * 256 + ofh * 128 + 128],
                                b=l3(th, kc),
                                s=(kc == 0), t=(kc == NF - 1):
                                e.matmul(o, a, b, start=s, stop=t),
                                wait=w, inc="pe" if kc == NF - 1 else None)
                    tmm = sc.cnt["pe"]
                    blk_last[par] = tmm
                    oc = ofb * 2 + ofh
                    a_ev = sc.emit("act",
                                   lambda e, o=t1b[:, oc * HALF:(oc + 1) * HALF],
                                   i=ps[u][:, 0:HALF]: e.activation(o, i, AF.Tanh),
                                   wait=[("pe", tmm)], inc="act")
                    ps_free[u] = ("act", a_ev)
                    drive(4)
            t1_done = sc.cnt["act"]
            tc_war = [0, 0]
            for ofb in range(8):
                par = ofb % 2
                wsl = wst[:, par * NF * 256:(par + 1) * NF * 256]
                l_w = sc.dma(
                    "sp", wsl[:, 0:8 * 256].rearrange("p (c f) -> p c f", c=8),
                    self.cTB.ap()[ofb],
                    wait=[("pe", blk_last[par]),
                          ("lc", self.lc_hist[18 + ofb // 4])],
                    sem=f"q{2 + par}")
                for ofh in range(2):
                    u = (ofb * 2 + ofh) % 2
                    for kc in range(8):
                        w = []
                        if kc == 0:
                            w = [(f"q{2 + par}", l_w), ps_free[u], ("act", t1_done)]
                        sc.emit("pe", lambda e, o=ps[u][:, 0:HALF],
                                a=wsl[:, kc * 256 + ofh * 128:
                                      kc * 256 + ofh * 128 + 128],
                                b=t1b[:, kc * HALF:(kc + 1) * HALF],
                                s=(kc == 0), t=(kc == 7):
                                e.matmul(o, a, b, start=s, stop=t),
                                wait=w, inc="pe" if kc == 7 else None)
                    tmm = sc.cnt["pe"]
                    blk_last[par] = tmm
                    oc = ofb * 2 + ofh
                    P = oc % 2
                    d_c = sc.emit("dve", lambda e, o=tmpC[:, P * 512:P * 512 + HALF],
                                  a=l3(th, oc),
                                  s=self.dv[:, oc:oc + 1], i=ps[u][:, 0:HALF]:
                                  e.scalar_tensor_tensor(o, a, s, i, ALU.mult,
                                                         ALU.add),
                                  wait=[("pe", tmm), ("pool", tc_war[P])],
                                  inc="dve")
                    ps_free[u] = ("dve", d_c)
                    p_x = sc.emit("pool", lambda e, o=self.xsl(oc, t0, HALF),
                                  i=tmpC[:, P * 512:P * 512 + HALF]:
                                  e.tensor_add(o, o, i),
                                  wait=[("dve", d_c)], inc="pool")
                    tc_war[P] = p_x
                    drive(4)
            if lngen is not None:
                for _ in lngen:
                    pass
            if th == 0:
                ln_done[1] = resB["done"]
            sc.flush()
        st.close()

    def p6(self):
        sc, nc = self.sc, self.nc
        st = ExitStack()
        strip = st.enter_context(self.sbuf("p6_strip", [128, 2 * D], F32))
        psT = [st.enter_context(self.psum(f"p6_psT{i}", [128, 512], F32))
               for i in range(2)]
        strip_free = [0, 0]
        ps_free = [("dve", 0), ("dve", 0)]
        for s in range(8):
            par = s % 2
            sl = strip[:, par * D:(par + 1) * D]
            ev_tgts = []
            for g in range(4):
                gp = g % 2
                for qv in range(4):
                    c = g * 4 + qv
                    w = [ps_free[gp]] if qv == 0 else []
                    sc.emit("pe", lambda e, o=psT[gp][:, qv * 128:(qv + 1) * 128],
                            i=self.xsl(c, s * 128, 128):
                            e.transpose(o, i, self.idf[:, :]),
                            wait=w, inc="pe" if qv == 3 else None)
                tpe = sc.cnt["pe"]
                eng = "dve" if g % 2 == 0 else "act"
                tgt = sc.emit(eng, lambda e, o=sl[:, g * 512:(g + 1) * 512],
                              i=psT[gp][:, 0:512], eng=eng:
                              (e.tensor_copy(o, i) if eng == "dve"
                               else e.activation(o, i, AF.Copy)),
                              wait=[("pe", tpe), (f"q{par}", strip_free[par])],
                              inc="act" if eng == "act" else "dve")
                ps_free[gp] = (eng, tgt)
                ev_tgts.append((eng, tgt))
            strip_free[par] = sc.store(
                self.out.ap()[s * 128:(s + 1) * 128, :], sl, wait=ev_tgts,
                sem=f"q{par}")
        sc.waitonly("sp", [("q0", sc.cnt["q0"]), ("q1", sc.cnt["q1"])])
        sc.flush()
        st.close()

    def build(self):
        self.p0()
        if self.stage >= 2:
            self.p2()
        self.attn_ctx.close()
        self.finish_conv()
        self.sc.flush()
        if self.stage >= 3:
            self.ffn("p3")
        if self.stage >= 4:
            self.p4()
        if self.stage >= 5:
            self.ffn("p5")
        self.p6()
        return self.nc


_prog_cache = {}
_last_results = None


def _get_prog(stage):
    if stage not in _prog_cache:
        _prog_cache[stage] = Builder(stage).build()
    return _prog_cache[stage]


def kernel(L0_attn_a_hidden_states, L0_attn_a_slope_rate, L0_attn_a_w_qkv,
           L0_attn_a_w_gate, L0_attn_a_w_out, L0_attn_a_norm_w,
           L0_attn_b_B, L0_attn_b_C, L0_attn_b_D,
           L0_ffn_gate_w, L0_ffn_gate_scale, L0_ffn_up_w, L0_ffn_up_scale,
           L0_ffn_down_w, L0_ffn_down_scale):
    cc = np.ascontiguousarray
    x = cc(np.asarray(L0_attn_a_hidden_states, np.float32)).reshape(8192, D)
    col = lambda v, n: cc(np.asarray(v, np.float32).reshape(n, 128).T)
    blk = lambda a, nb: cc(a.reshape(16, 128, nb, 256).transpose(2, 1, 0, 3))
    gw_t = np.asarray(L0_ffn_gate_w, np.int32).T
    uw_t = np.asarray(L0_ffn_up_w, np.int32).T
    dw_t = np.asarray(L0_ffn_down_w, np.int32).T
    cT_t = np.asarray(L0_attn_b_C, np.float32).T
    shared = {
        "wqkvT": blk(np.asarray(L0_attn_a_w_qkv, np.float32).T, 24),
        "wgateT": blk(np.asarray(L0_attn_a_w_gate, np.float32).T, 8),
        "woutT": blk(np.asarray(L0_attn_a_w_out, np.float32).T, 8),
        "bT": blk(np.asarray(L0_attn_b_B, np.float32).T, 4),
        "cT": cc(cT_t.reshape(8, 128, 8, 256).transpose(2, 1, 0, 3)),
        "gwT": cc(gw_t.reshape(16, 128, NI, 128).transpose(2, 1, 0, 3)),
        "uwT": cc(uw_t.reshape(16, 128, NI, 128).transpose(2, 1, 0, 3)),
        "dwT": cc(dw_t.reshape(16, 4, 128, 2, 1024).transpose(0, 3, 2, 1, 4)),
        "slope": cc(np.asarray(L0_attn_a_slope_rate, np.float32).reshape(1, H)),
        "nw_c": col(L0_attn_a_norm_w, NF),
        "dv_c": col(L0_attn_b_D, NF),
        "gsc_c": col(L0_ffn_gate_scale, NI),
        "usc_c": col(L0_ffn_up_scale, NI),
        "dsc_c": col(L0_ffn_down_scale, NF),
        "ident": np.eye(128, dtype=np.float32),
    }
    in_maps = []
    for c in range(NCORES):
        m = dict(shared)
        m["xo"] = cc(x[c * TOK:(c + 1) * TOK])
        sel = np.zeros((1, 8), np.float32)
        if c % 2 == 1:
            sel[0, c - 1] = 1.0
        m["kvsel"] = sel
        in_maps.append(m)
    nc = _get_prog(STAGE)
    res = run_bass_kernel_spmd(nc, in_maps, core_ids=list(range(NCORES)))
    global _last_results
    _last_results = res.results
    outs = [np.asarray(r["out"]) for r in res.results]
    return np.concatenate(outs, axis=0).reshape(4, 2048, D)


# version 14
# speedup vs baseline: 1.1303x; 1.0941x over previous
import numpy as np
from contextlib import ExitStack

import concourse.bass as bass
import concourse.mybir as mybir
from concourse.bass_utils import run_bass_kernel_spmd

F32 = mybir.dt.float32
BF16 = mybir.dt.bfloat16
I32 = mybir.dt.int32
U8 = mybir.dt.uint8
AF = mybir.ActivationFunctionType
ALU = mybir.AluOpType

NCORES = 8
D = 2048
NF = 16
H = 16
DH = 128
TOK = 1024
BLK = 256
HALF = 512
FI = 8192
NI = 64
NSTATE = 1024
LN_EPS = 1e-5
RMS_EPS = 1e-6

DBG = None
STAGE = 5

ENGS = ("pe", "act", "dve", "pool", "sp")


class Sched:

    def __init__(self, nc, stack):
        self.nc = nc
        self.q = {e: [] for e in ENGS}
        self.sems = {}
        self.cnt = {}
        for name in ("ld", "lc", "lx", "st", "pe", "act", "dve", "pool",
                     "q0", "q1", "q2", "q3", "q4", "q5", "q6", "q7"):
            self.sems[name] = stack.enter_context(nc.semaphore(name))
            self.cnt[name] = 0

    def emit(self, eng, fn, wait=(), inc=None, n=1):
        w = tuple((s, v) for (s, v) in wait if v and v > 0)
        for s, v in w:
            assert v <= self.cnt[s], f"forward wait {s} {v} > {self.cnt[s]}"
        tgt = None
        if inc is not None:
            self.cnt[inc] += n
            tgt = self.cnt[inc]
        sems = self.sems

        pad = eng in ("dve", "act", "pool") and not w

        def thunk(e, w=w, fn=fn, inc=inc, n=n, pad=pad):
            if pad:
                e.nop()
            for s, v in w:
                e.wait_ge(sems[s], v)
            ins = fn(e)
            if inc is not None:
                ins.then_inc(sems[inc], n)

        self.q[eng].append(thunk)
        return tgt

    def waitonly(self, eng, wait):
        sems = self.sems
        w = tuple((s, v) for (s, v) in wait if v and v > 0)

        def thunk(e, w=w):
            for s, v in w:
                e.wait_ge(sems[s], v)

        self.q[eng].append(thunk)

    def dma(self, q, dst, src, wait=(), sem=None):
        if sem is None:
            sem = "ld" if q == "pool" else "lx"
        return self.emit(q if q == "pool" else "sp",
                         lambda e, d=dst, s=src: e.dma_start(d, s),
                         wait=wait, inc=sem, n=16)

    def store(self, dst, src, wait=(), sem="st"):
        return self.emit("sp", lambda e, d=dst, s=src: e.dma_start(d, s),
                         wait=wait, inc=sem, n=16)

    def flush(self):
        q = self.q
        self.q = {e: [] for e in ENGS}
        if not any(q.values()):
            return
        with self.nc.Block() as blk:
            m = {"pe": blk.tensor, "act": blk.scalar, "dve": blk.vector,
                 "pool": blk.gpsimd, "sp": blk.sync}
            for e in ENGS:
                if q[e]:
                    m[e](lambda eng, lst=q[e]: [t(eng) for t in lst])


class Builder:
    def __init__(self, stage=STAGE):
        self.stage = stage
        nc = bass.Bass("TRN2", target_bir_lowering=False)
        self.nc = nc
        self.stack = ExitStack()
        self.sc = Sched(nc, self.stack)

        dt = nc.dram_tensor
        self.xo = dt("xo", [TOK, D], F32, kind="ExternalInput")
        self.xp = dt("xp", [TOK, D], F32, kind="ExternalInput")
        self.wqkvT = dt("wqkvT", [24, 128, NF, 256], F32, kind="ExternalInput")
        self.wgateT = dt("wgateT", [8, 128, NF, 256], F32, kind="ExternalInput")
        self.woutT = dt("woutT", [8, 128, NF, 256], F32, kind="ExternalInput")
        self.bT = dt("bT", [4, 128, NF, 256], F32, kind="ExternalInput")
        self.cT = dt("cT", [8, 128, 8, 256], F32, kind="ExternalInput")
        self.gwT = dt("gwT", [NI, 128, 16, 128], I32, kind="ExternalInput")
        self.uwT = dt("uwT", [NI, 128, 16, 128], I32, kind="ExternalInput")
        self.dwT = dt("dwT", [16, 2, 128, 4, 1024], I32, kind="ExternalInput")
        self.slope = dt("slope", [1, H], F32, kind="ExternalInput")
        self.nw_c = dt("nw_c", [128, NF], F32, kind="ExternalInput")
        self.dv_c = dt("dv_c", [128, NF], F32, kind="ExternalInput")
        self.gsc_c = dt("gsc_c", [128, NI], F32, kind="ExternalInput")
        self.usc_c = dt("usc_c", [128, NI], F32, kind="ExternalInput")
        self.dsc_c = dt("dsc_c", [128, NF], F32, kind="ExternalInput")
        self.ident = dt("ident", [128, 128], F32, kind="ExternalInput")
        self.out = dt("out", [TOK, D], F32, kind="ExternalOutput")
        self.dbgt = dt("dbg", [128, 16384], F32, kind="ExternalOutput")
        self.u8g = dt("u8g", [NI, 128, 16, 128], U8, kind="Internal")
        self.wqkvB = dt("wqkvB", [24, 128, NF, 256], BF16, kind="Internal")
        self.wgateB = dt("wgateB", [8, 128, NF, 256], BF16, kind="Internal")
        self.woutB = dt("woutB", [8, 128, NF, 256], BF16, kind="Internal")
        self.bTB = dt("bTB", [4, 128, NF, 256], BF16, kind="Internal")
        self.cTB = dt("cTB", [8, 128, 8, 256], BF16, kind="Internal")
        self.u8u = dt("u8u", [NI, 128, 16, 128], U8, kind="Internal")
        self.u8d = dt("u8d", [16, 2, 128, 4, 1024], U8, kind="Internal")

        al = nc.alloc_sbuf_tensor
        self.x_sb = al("x_sb", [128, NF * TOK], F32)
        self.idf = al("idf", [128, 128], F32)
        self.ones_r = al("ones_r", [1, 128], F32)
        self.ones_cf = al("ones_cf", [128, 1], F32)
        self.ones_cb = al("ones_cb", [128, 1], BF16)
        self.m128_r = al("m128_r", [1, 128], BF16)
        self.slope_sb = al("slope_sb", [1, H], F32)
        self.nw = al("nw", [128, NF], F32)
        self.dv = al("dv", [128, NF], F32)
        self.gsc = al("gsc", [128, NI], F32)
        self.usc = al("usc", [128, NI], F32)
        self.dsc = al("dsc", [128, NF], F32)
        self.epsv = al("epsv", [1, 2], F32)
        self.attn_ctx = ExitStack()
        ac = lambda n_, sh, dt_: self.attn_ctx.enter_context(
            nc.sbuf_tensor(n_, sh, dt_))
        self.kv = ac("kv", [128, H * DH], F32)
        self.idb = ac("idb", [128, 128], BF16)
        self.SF = ac("SF", [128, H], F32)
        self.SFn = ac("SFn", [128, H], F32)
        self.KD = ac("KD", [128, 2 * H], F32)
        self.BD = ac("BD", [128, H], F32)

        self.conv_jobs = []
        self.lc_kvjob = []
        if self.stage >= 2:
            for j in range(8):
                self.conv_jobs.append((
                    self.wqkvB.ap()[8 + 2 * j: 8 + 2 * (j + 1)],
                    self.wqkvT.ap()[8 + 2 * j: 8 + 2 * (j + 1)]))
            for j in range(4):
                self.conv_jobs.append((
                    self.wqkvB.ap()[2 * j: 2 * (j + 1)],
                    self.wqkvT.ap()[2 * j: 2 * (j + 1)]))
            for j in range(4):
                self.conv_jobs.append((
                    self.wgateB.ap()[2 * j: 2 * (j + 1)],
                    self.wgateT.ap()[2 * j: 2 * (j + 1)]))
            for j in range(4):
                self.conv_jobs.append((
                    self.woutB.ap()[2 * j: 2 * (j + 1)],
                    self.woutT.ap()[2 * j: 2 * (j + 1)]))
            for j in range(2):
                self.conv_jobs.append((
                    self.bTB.ap()[2 * j: 2 * (j + 1)],
                    self.bT.ap()[2 * j: 2 * (j + 1)]))
            for j in range(2):
                self.conv_jobs.append((
                    self.cTB.ap()[4 * j: 4 * (j + 1)],
                    self.cT.ap()[4 * j: 4 * (j + 1)]))
        self.n_attn_conv = len(self.conv_jobs)
        if self.stage >= 3:
            for ib in range(NI):
                self.conv_jobs.append((self.u8g.ap()[ib], self.gwT.ap()[ib]))
                self.conv_jobs.append((self.u8u.ap()[ib], self.uwT.ap()[ib]))
            for r in range(16):
                self.conv_jobs.append((self.u8d.ap()[r], self.dwT.ap()[r]))
        self.conv_done = 0
        self.lc_hist = []
        self.kv_stt = [0] * H

    def sbuf(self, name, shape, dtype):
        self._uid = getattr(self, "_uid", 0) + 1
        return self.nc.sbuf_tensor(f"{name}_{self._uid}", shape, dtype)

    def psum(self, name, shape, dtype):
        self._uid = getattr(self, "_uid", 0) + 1
        return self.nc.psum_tensor(f"{name}_{self._uid}", shape, dtype)

    def maybe_conv(self, k=1):
        for _ in range(k):
            if self.conv_jobs:
                dst, src = self.conv_jobs.pop(0)
                self.conv_done = self.sc.dma("pool", dst, src, sem="lc")
                self.lc_hist.append(self.conv_done)

    def finish_conv(self):
        while self.conv_jobs:
            self.maybe_conv()

    def dump(self, tag, p, region_ap, width):
        if DBG != (tag, p):
            return
        sc = self.sc
        sc.flush()
        t = sc.dma("pool", self.dbgt.ap()[:, 0:width], region_ap)
        sc.waitonly("pool", [("ld", t)])
        sc.flush()

    def xsl(self, c, t0, L):
        return self.x_sb[:, c * TOK + t0: c * TOK + t0 + L]

    def emit_ln_gen(self, xs, outs, L, scr, res, sum_eng="pool",
                    ap2_eng="pool"):
        sc = self.sc
        s1, s2, sqtA, rowb, psA, psB = scr
        n = len(xs)
        inv = 1.0 / (128.0 * n)
        row = lambda i: rowb[0:1, i * 512: i * 512 + L]

        sc.emit("act", lambda e, o=s2[:, 0:L], i=xs[0]:
                e.activation(o, i, AF.Square), inc="act")
        d_s2 = 0
        sq_war = [0, 0]
        for c in range(1, n):
            P = c % 2
            a = sc.emit("act", lambda e, o=sqtA[:, P * 512:P * 512 + L], i=xs[c]:
                        e.activation(o, i, AF.Square),
                        wait=[("dve", sq_war[P])], inc="act")
            d_s2 = sc.emit("dve",
                           lambda e, o=s2[:, 0:L], i=sqtA[:, P * 512:P * 512 + L]:
                           e.tensor_add(o, o, i), wait=[("act", a)], inc="dve")
            sq_war[P] = d_s2
            yield
        p_s1 = sc.emit(sum_eng, lambda e, o=s1[:, 0:L], i=xs[0]:
                       e.tensor_copy(o, i), inc=sum_eng)
        for c in range(1, n):
            p_s1 = sc.emit(sum_eng, lambda e, o=s1[:, 0:L], i=xs[c]:
                           e.tensor_add(o, o, i), inc=sum_eng)
            if c % 2 == 0:
                yield
        t1 = sc.emit("pe", lambda e, o=psA[0:1, 0:L], a=self.ones_cf[:, :],
                     b=s1[:, 0:L]: e.matmul(o, a, b, start=True, stop=True),
                     wait=[(sum_eng, p_s1)], inc="pe")
        t2 = sc.emit("pe", lambda e, o=psB[0:1, 0:L], a=self.ones_cf[:, :],
                     b=s2[:, 0:L]: e.matmul(o, a, b, start=True, stop=True),
                     wait=[("dve", d_s2)], inc="pe")
        yield
        sc.emit("dve", lambda e, o=row(0), i=psA[0:1, 0:L]:
                e.tensor_scalar(o, i, inv, None, ALU.mult),
                wait=[("pe", t1)], inc="dve")
        sc.emit("dve", lambda e, o=row(1), i=psB[0:1, 0:L]:
                e.tensor_scalar(o, i, inv, None, ALU.mult),
                wait=[("pe", t2)], inc="dve")
        sc.emit("dve", lambda e, o=row(2), a=row(0): e.tensor_mul(o, a, a),
                inc="dve")
        d = sc.emit("dve", lambda e, o=row(1), a=row(1), b=row(2):
                    e.tensor_sub(o, a, b), inc="dve")
        a_sd = sc.emit("act", lambda e, o=row(2), i=row(1):
                       e.activation(o, i, AF.Sqrt, bias=self.epsv[0:1, 0:1]),
                       wait=[("dve", d)], inc="act")
        d_a = sc.emit("dve", lambda e, o=row(3), i=row(2):
                      e.reciprocal(o, i), wait=[("act", a_sd)], inc="dve")
        yield
        tA = sc.emit("pe", lambda e, o=psA[:, 0:L], a=self.ones_r[:, :], b=row(3):
                     e.matmul(o, a, b, start=True, stop=True),
                     wait=[("dve", d_a)], inc="pe")
        dA = sc.emit("dve", lambda e, o=s1[:, 0:L], i=psA[:, 0:L]:
                     e.tensor_copy(o, i), wait=[("pe", tA)], inc="dve")
        d_b = sc.emit("dve", lambda e, o=row(2), a=row(0), b=row(3):
                      e.scalar_tensor_tensor(o, a, -1.0, b, ALU.mult, ALU.mult),
                      inc="dve")
        tB = sc.emit("pe", lambda e, o=psB[:, 0:L], a=self.ones_r[:, :], b=row(2):
                     e.matmul(o, a, b, start=True, stop=True),
                     wait=[("dve", d_b)], inc="pe")
        dB = sc.emit("dve", lambda e, o=s2[:, 0:L], i=psB[:, 0:L]:
                     e.tensor_copy(o, i), wait=[("pe", tB)], inc="dve")
        yield
        ap_war = [0, 0]
        p_last = 0
        for c in range(n):
            P = c % 2
            dm = sc.emit("dve", lambda e, o=sqtA[:, P * 512:P * 512 + L],
                         a=xs[c], b=s1[:, 0:L]: e.tensor_mul(o, a, b),
                         wait=[("dve", dA), (ap2_eng, ap_war[P])], inc="dve")
            p_last = sc.emit(ap2_eng, lambda e, o=outs[c],
                             a=sqtA[:, P * 512:P * 512 + L], b=s2[:, 0:L]:
                             e.tensor_add(o, a, b),
                             wait=[("dve", max(dm, dB))], inc=ap2_eng)
            ap_war[P] = p_last
            res["done"] = (ap2_eng, p_last)
            yield

    def emit_ln(self, xs, outs, L, scr):
        res = {}
        for _ in self.emit_ln_gen(xs, outs, L, scr, res):
            pass
        return res["done"][1]

    def load_feature_major(self, src_dram, row0, nrows, strip, psT, dst_xsl,
                           extra_wait=()):
        sc = self.sc
        nstrips = nrows // 128
        strip_war = [0, 0]
        ps_war = [("dve", 0), ("dve", 0)]
        lx = [0, 0]
        for s in range(nstrips):
            par = s % 2
            sl = strip[:, par * D:(par + 1) * D]
            lx[par] = sc.dma(
                "sp", sl,
                src_dram.ap()[row0 + s * 128: row0 + (s + 1) * 128, :],
                wait=[("pe", strip_war[par])], sem=f"q{par}")
            for g in range(4):
                gp = g % 2
                for qv in range(4):
                    c = g * 4 + qv
                    w = []
                    if qv == 0:
                        w = [(f"q{par}", lx[par]), ps_war[gp]]
                        if s == 0 and g == 0:
                            w += list(extra_wait)
                    sc.emit("pe", lambda e, o=psT[gp][:, qv * 128:(qv + 1) * 128],
                            i=sl[:, c * 128:(c + 1) * 128]:
                            e.transpose(o, i, self.idf[:, :]), wait=w,
                            inc="pe" if qv == 3 else None)
                tpe = sc.cnt["pe"]
                strip_war[par] = tpe
                eng = "dve" if (s + g) % 2 == 0 else "act"
                tgt = 0
                for qv2 in range(4):
                    w = [("pe", tpe)] if qv2 == 0 else []
                    tgt = sc.emit(eng, lambda e, o=dst_xsl(g * 4 + qv2, s * 128, 128),
                                  i=psT[gp][:, qv2 * 128:(qv2 + 1) * 128], eng=eng:
                                  (e.tensor_copy(o, i) if eng == "dve"
                                   else e.activation(o, i, AF.Copy)),
                                  wait=w, inc="act" if eng == "act" else "dve")
                ps_war[gp] = (eng, tgt)

    def p0(self):
        sc, nc = self.sc, self.nc
        st = ExitStack()
        strip = st.enter_context(self.sbuf("p0_strip", [128, 2 * D], F32))
        kcol = st.enter_context(self.sbuf("p0_kcol", [128, 2], F32))
        psT = [st.enter_context(self.psum(f"p0_psT{i}", [128, 512], F32))
               for i in range(2)]
        psW = st.enter_context(self.psum("p0_psW", [128, 128], F32))

        l_sl = sc.dma("sp", self.slope_sb[:, :], self.slope.ap())
        for dst, src in ((self.nw, self.nw_c), (self.dv, self.dv_c),
                         (self.gsc, self.gsc_c), (self.usc, self.usc_c),
                         (self.dsc, self.dsc_c), (self.idf, self.ident)):
            sc.dma("sp", dst[:, :], src.ap())
        sc.dma("pool", self.idb[:, :], self.ident.ap())
        d_ms = 0
        for buf, val in ((self.ones_r, 1.0), (self.ones_cf, 1.0),
                         (self.ones_cb, 1.0), (self.m128_r, -128.0)):
            d_ms = sc.emit("dve", lambda e, b=buf, v=val: e.memset(b[:, :], v),
                           inc="dve")
        d_ms = sc.emit("dve", lambda e: e.memset(self.kv[:, :], 0.0), inc="dve")
        d_ms = sc.emit("dve", lambda e: e.memset(self.epsv[0:1, 0:1], LN_EPS),
                       inc="dve")
        d_ms = sc.emit("dve", lambda e: e.memset(self.epsv[0:1, 1:2], RMS_EPS),
                       inc="dve")
        t_sf = sc.emit("pe", lambda e: e.matmul(psW[:, 0:H], self.ones_r[:, :],
                                                self.slope_sb[:, :],
                                                start=True, stop=True),
                       wait=[("lx", sc.cnt["lx"]), ("dve", d_ms)], inc="pe")
        sc.emit("dve", lambda e: e.tensor_copy(self.SF[:, :], psW[:, 0:H]),
                wait=[("pe", t_sf)], inc="dve")
        d_sf = sc.emit("dve", lambda e: e.tensor_scalar(
            self.SFn[:, :], self.SF[:, :], -1.0, None, ALU.mult), inc="dve")
        p_io = sc.emit("pool", lambda e: e.iota(
            kcol[:, 0:1], [[0, 1]], base=-255, channel_multiplier=1,
            allow_small_or_imprecise_dtypes=True), inc="pool")
        p_io = sc.emit("pool", lambda e: e.iota(
            kcol[:, 1:2], [[0, 1]], base=-127, channel_multiplier=1,
            allow_small_or_imprecise_dtypes=True), inc="pool")
        for h in range(H):
            for c in range(2):
                sc.emit("act", lambda e, h=h, c=c:
                        e.activation(self.KD[:, h * 2 + c:h * 2 + c + 1],
                                     kcol[:, c:c + 1], AF.Exp,
                                     scale=self.SF[:, h:h + 1]),
                        wait=[("dve", d_sf), ("pool", p_io)], inc="act")
            sc.emit("act", lambda e, h=h:
                    e.activation(self.BD[:, h:h + 1], self.SF[:, h:h + 1],
                                 AF.Exp, scale=-256.0), inc="act")
        self.load_feature_major(self.xo, 0, TOK, strip, psT,
                                lambda c, tt, L: self.xsl(c, tt, L),
                                extra_wait=[("lx", sc.cnt["lx"])])
        self.maybe_conv(12)
        sc.waitonly("pe", [("dve", d_ms)])
        sc.waitonly("act", [("dve", d_ms)])
        sc.waitonly("pool", [("dve", d_ms)])
        sc.waitonly("sp", [("lx", sc.cnt["lx"])])
        sc.waitonly("pool", [("ld", sc.cnt["ld"])])
        sc.flush()
        st.close()

    def p1(self):
        sc, nc = self.sc, self.nc
        for hp in range(2):
            st = ExitStack()
            xpf = st.enter_context(self.sbuf("p1_xpf", [128, NF * HALF], F32))
            lnp = st.enter_context(self.sbuf("p1_lnp", [128, NF * HALF], BF16))
            s1 = st.enter_context(self.sbuf("p1_s1", [128, 512], F32))
            s2 = st.enter_context(self.sbuf("p1_s2", [128, 512], F32))
            sqtA = st.enter_context(self.sbuf("p1_sqtA", [128, 1024], F32))
            rowb = st.enter_context(self.sbuf("p1_rowb", [1, 2048], F32))
            khat = st.enter_context(self.sbuf("p1_khat", [128, 2 * 256], BF16))
            psA = st.enter_context(self.psum("p1_psA", [128, 512], F32))
            psB = st.enter_context(self.psum("p1_psB", [128, 512], F32))
            psKV = [st.enter_context(self.psum(f"p1_psKV{i}", [128, 128], F32))
                    for i in range(2)]

            stA = ExitStack()
            strip = stA.enter_context(self.sbuf("p1_strip", [128, 2 * D], F32))
            psT = [stA.enter_context(self.psum(f"p1_psT{i}", [128, 512], F32))
                   for i in range(2)]
            self.load_feature_major(
                self.xp, hp * HALF, HALF, strip, psT,
                lambda c, tt, L: xpf[:, c * HALF + tt: c * HALF + tt + L])
            sc.flush()
            stA.close()

            stB = ExitStack()
            kvT = stB.enter_context(self.sbuf("p1_kvT", [128, 4 * 4096], BF16))
            wst = stB.enter_context(self.sbuf("p1_wst", [128, 2 * NF * 256], BF16))

            xs = [xpf[:, c * HALF:(c + 1) * HALF] for c in range(NF)]
            outs = [lnp[:, c * HALF:(c + 1) * HALF] for c in range(NF)]
            ln_done = self.emit_ln(xs, outs, HALF, (s1, s2, sqtA, rowb, psA, psB))

            ps = [psA, psB]
            evac_done = [0, 0]
            blk_last = [0, 0]
            for ofb in range(16):
                par = ofb % 2
                wsl = wst[:, par * NF * 256:(par + 1) * NF * 256]
                jb = ofb // 2
                cw = self.lc_hist[jb] if jb < len(self.lc_hist) else 0
                l_w = sc.dma(
                    "pool", wsl.rearrange("p (c f) -> p c f", c=NF),
                    self.wqkvB.ap()[8 + ofb],
                    wait=[("pe", blk_last[par]), ("lc", cw)], sem=f"q{2 + par}")
                self.maybe_conv()
                for mc in range(4):
                    u = (ofb * 4 + mc) % 2
                    for kc in range(NF):
                        w = []
                        if kc == 0:
                            w = [(f"q{2 + par}", l_w), ("act", evac_done[u]),
                                 ("pool", ln_done)]
                        sc.emit("pe", lambda e, o=ps[u][:, 0:256],
                                a=lnp[:, kc * HALF + mc * 128:
                                      kc * HALF + mc * 128 + 128],
                                b=wsl[:, kc * 256:(kc + 1) * 256],
                                s=(kc == 0), t=(kc == NF - 1):
                                e.matmul(o, a, b, start=s, stop=t),
                                wait=w, inc="pe" if kc == NF - 1 else None)
                    tmm = sc.cnt["pe"]
                    blk_last[par] = tmm
                    evac_done[u] = sc.emit(
                        "act", lambda e,
                        o=kvT[:, mc * 4096 + ofb * 256: mc * 4096 + (ofb + 1) * 256],
                        i=ps[u][:, 0:256]: e.activation(o, i, AF.Silu),
                        wait=[("pe", tmm)], inc="act")

            khat_war = [0, 0]
            pskv_free = [0, 0]
            for j in range(2):
                for h in range(H):
                    P = h % 2
                    dv = 0
                    for c in range(2):
                        w = []
                        if c == 0:
                            w = [("pe", khat_war[P]), ("act", evac_done[0]),
                                 ("act", evac_done[1])]
                        dv = sc.emit(
                            "dve", lambda e,
                            o=khat[:, P * 256 + c * 128: P * 256 + c * 128 + 128],
                            i=kvT[:, (j * 2 + c) * 4096 + h * 128:
                                  (j * 2 + c) * 4096 + h * 128 + 128],
                            s=self.KD[:, h * 2 + c:h * 2 + c + 1]:
                            e.tensor_scalar(o, i, s, None, ALU.mult),
                            wait=w, inc="dve")
                    for c in range(2):
                        w = []
                        if c == 0:
                            w = [("dve", dv), ("dve", pskv_free[P])]
                        sc.emit("pe", lambda e, o=psKV[P][:, :],
                                a=khat[:, P * 256 + c * 128: P * 256 + c * 128 + 128],
                                b=kvT[:, (j * 2 + c) * 4096 + 2048 + h * 128:
                                      (j * 2 + c) * 4096 + 2048 + h * 128 + 128],
                                s=(c == 0), t=(c == 1):
                                e.matmul(o, a, b, start=s, stop=t),
                                wait=w, inc="pe" if c == 1 else None)
                    th = sc.cnt["pe"]
                    khat_war[P] = th
                    di = sc.emit("dve", lambda e, o=self.kv[:, h * DH:(h + 1) * DH],
                                 s=self.BD[:, h:h + 1], i=psKV[P][:, :]:
                                 e.scalar_tensor_tensor(o, o, s, i, ALU.mult, ALU.add),
                                 wait=[("pe", th), ("dve", self.kv_stt[h])],
                                 inc="dve")
                    self.kv_stt[h] = di
                    pskv_free[P] = di
            sc.flush()
            stB.close()
            st.close()

    def p2(self):
        sc, nc = self.sc, self.nc
        st = ExitStack()
        qb = st.enter_context(self.sbuf("p2_q", [128, NF * HALF], BF16))
        kb = st.enter_context(self.sbuf("p2_k", [128, NF * HALF], BF16))
        vT = st.enter_context(self.sbuf("p2_vT", [128, 4 * D], BF16))
        gate = st.enter_context(self.sbuf("p2_gate", [128, NF * HALF], BF16))
        ao = st.enter_context(self.sbuf("p2_ao", [128, H * BLK], BF16))
        s1 = st.enter_context(self.sbuf("p2_s1", [128, 512], F32))
        s2 = st.enter_context(self.sbuf("p2_s2", [128, 512], F32))
        sqtA = st.enter_context(self.sbuf("p2_sqtA", [128, 1024], F32))
        rowb = st.enter_context(self.sbuf("p2_rowb", [1, 2048], F32))
        khat = st.enter_context(self.sbuf("p2_khat", [128, 2 * 256], BF16))
        kvb = st.enter_context(self.sbuf("p2_kvb", [128, 2 * 128], BF16))
        sS = st.enter_context(self.sbuf("p2_sS", [128, 2 * 512], BF16))
        qh = st.enter_context(self.sbuf("p2_qh", [128, 2 * 256], BF16))
        rgt = st.enter_context(self.sbuf("p2_rgt", [128, 256], BF16))
        ln1 = st.enter_context(self.sbuf("p2_ln1", [128, NF * HALF], BF16))
        psS = [st.enter_context(self.psum(f"p2_psS{i}", [128, 512], F32))
               for i in range(2)]
        psS_free = [("dve", 0), ("dve", 0)]
        blk_last = [0, 0]

        lc_attn = (self.lc_hist[self.n_attn_conv - 1]
                   if len(self.lc_hist) >= self.n_attn_conv else 0)

        def std_block(wdram, col0, ofb, dst, act_fn, wst, ln_done, evac_war=None):
            par = ofb % 2
            wsl = wst[:, par * NF * 256:(par + 1) * NF * 256]
            l_w = sc.dma(
                "pool", wsl.rearrange("p (c f) -> p c f", c=NF),
                wdram.ap()[col0 // 256 + ofb],
                wait=[("pe", blk_last[par]), ("lc", lc_attn)], sem=f"q{2 + par}")
            self.maybe_conv(2)
            for ofh in range(2):
                u = (ofb * 2 + ofh) % 2
                for kc in range(NF):
                    w = []
                    if kc == 0:
                        w = [(f"q{2 + par}", l_w), psS_free[u], ln_done]
                    sc.emit("pe", lambda e, o=psS[u][:, 0:HALF],
                            a=wsl[:, kc * 256 + ofh * 128:
                                  kc * 256 + ofh * 128 + 128],
                            b=ln1[:, kc * HALF:(kc + 1) * HALF],
                            s=(kc == 0), t=(kc == NF - 1):
                            e.matmul(o, a, b, start=s, stop=t),
                            wait=w, inc="pe" if kc == NF - 1 else None)
                tmm = sc.cnt["pe"]
                blk_last[par] = tmm
                oc = ofb * 2 + ofh
                w2 = [("pe", tmm)]
                if evac_war is not None:
                    w2.append(evac_war)
                a_ev = sc.emit("act",
                               lambda e, o=dst[:, oc * HALF:(oc + 1) * HALF],
                               i=psS[u][:, 0:HALF], f=act_fn:
                               e.activation(o, i, f),
                               wait=w2, inc="act")
                psS_free[u] = ("act", a_ev)

        def v_block(ofb, wst, ln_done):
            par = ofb % 2
            wsl = wst[:, par * NF * 256:(par + 1) * NF * 256]
            l_w = sc.dma(
                "pool", wsl.rearrange("p (c f) -> p c f", c=NF),
                self.wqkvB.ap()[16 + ofb],
                wait=[("pe", blk_last[par]), ("lc", lc_attn)], sem=f"q{2 + par}")
            self.maybe_conv()
            for mc in range(4):
                u = mc % 2
                for kc in range(NF):
                    w = []
                    if kc == 0:
                        w = [(f"q{2 + par}", l_w), psS_free[u], ln_done]
                    sc.emit("pe", lambda e, o=psS[u][:, 0:256],
                            a=ln1[:, kc * HALF + mc * 128:
                                  kc * HALF + mc * 128 + 128],
                            b=wsl[:, kc * 256:(kc + 1) * 256],
                            s=(kc == 0), t=(kc == NF - 1):
                            e.matmul(o, a, b, start=s, stop=t),
                            wait=w, inc="pe" if kc == NF - 1 else None)
                tmm = sc.cnt["pe"]
                blk_last[par] = tmm
                a_ev = sc.emit("act", lambda e,
                               o=vT[:, mc * D + ofb * 256: mc * D + (ofb + 1) * 256],
                               i=psS[u][:, 0:256]: e.activation(o, i, AF.Silu),
                               wait=[("pe", tmm)], inc="act")
                psS_free[u] = ("act", a_ev)

        def emit_decay(stX):
            QD = stX.enter_context(self.sbuf("p2_QD", [128, H * 256], BF16))
            DDt = stX.enter_context(self.sbuf("p2_DD", [128, 2 * H * 256], BF16))
            it = sqtA
            p_io = sc.emit("pool", lambda e: e.iota(
                it[:, 0:256], [[1, 256]], base=1, channel_multiplier=0,
                allow_small_or_imprecise_dtypes=True), inc="pool")
            a_qd = 0
            for h in range(H):
                a_qd = sc.emit("act", lambda e, h=h:
                               e.activation(QD[:, h * 256:(h + 1) * 256],
                                            it[:, 0:256], AF.Exp,
                                            scale=self.SFn[:, h:h + 1]),
                               wait=[("pool", p_io)], inc="act")
            a_dd = 0
            for c in range(2):
                p_io = sc.emit("pool", lambda e, c=c: e.iota(
                    it[:, 0:256], [[-1, 256]], base=128 * c, channel_multiplier=1,
                    allow_small_or_imprecise_dtypes=True),
                    wait=[("act", a_qd if c == 0 else a_dd)], inc="pool")
                for h in range(H):
                    a_dd = sc.emit("act", lambda e, h=h, c=c:
                                   e.activation(
                                       DDt[:, (h * 2 + c) * 256:(h * 2 + c + 1) * 256],
                                       it[:, 0:256], AF.Exp,
                                       scale=self.SF[:, h:h + 1]),
                                   wait=[("pool", p_io)], inc="act")
            p_dd = 0
            for h in range(H):
                for c in range(2):
                    p_dd = sc.emit("pool", lambda e, h=h, c=c:
                                   e.affine_select(
                                       DDt[:, (h * 2 + c) * 256:(h * 2 + c + 1) * 256],
                                       DDt[:, (h * 2 + c) * 256:(h * 2 + c + 1) * 256],
                                       [[1, 256]], ALU.is_ge, 0.0,
                                       base=-(128 * c), channel_multiplier=-1),
                                   wait=[("act", a_dd)], inc="pool")
            return QD, DDt, a_qd, p_dd

        def emit_scan(p, stX):
            QD, DDt, a_qd, p_dd = emit_decay(stX)
            psTr = [stX.enter_context(self.psum(f"p2_psTr{i}", [128, 256], BF16))
                    for i in range(2)]
            psO = [stX.enter_context(self.psum(f"p2_psO{i}", [128, 256], F32))
                   for i in range(2)]
            psKV = [stX.enter_context(self.psum(f"p2_psKV{i}", [128, 128], F32))
                    for i in range(2)]
            b_done = [0, 0]
            kvmm_done = [0, 0]
            g_done = [0, 0]
            j_done = [0, 0]
            i_done = [0, 0]
            rg_done = 0
            for jj in range(2):
                tc = jj * 256
                for h in range(H):
                    if h % 2 == 0:
                        self.maybe_conv()
                    P = h % 2
                    for c in range(2):
                        w = [("dve", b_done[P])] if c == 0 else []
                        sc.emit("pe", lambda e, o=psTr[P][:, c * 128:(c + 1) * 128],
                                i=kb[:, h * HALF + tc + c * 128:
                                     h * HALF + tc + c * 128 + 128]:
                                e.transpose(o, i, self.idb[:, :]), wait=w,
                                inc="pe" if c == 1 else None)
                    ta = sc.cnt["pe"]
                    for c in range(2):
                        w = [("pe", ta), ("pe", kvmm_done[P])] if c == 0 else []
                        b_done[P] = sc.emit(
                            "dve", lambda e,
                            o=khat[:, P * 256 + c * 128: P * 256 + c * 128 + 128],
                            i=psTr[P][:, c * 128:(c + 1) * 128],
                            s=self.KD[:, h * 2 + c:h * 2 + c + 1]:
                            e.tensor_scalar(o, i, s, None, ALU.mult),
                            wait=w, inc="dve")
                    for c in range(2):
                        w = [psS_free[P]] if c == 0 else []
                        sc.emit("pe", lambda e, o=psS[P][:, c * 256:(c + 1) * 256],
                                a=kb[:, h * HALF + tc + c * 128:
                                     h * HALF + tc + c * 128 + 128],
                                b=qb[:, h * HALF + tc: h * HALF + tc + 256]:
                                e.matmul(o, a, b, start=True, stop=True),
                                wait=w, inc="pe" if c == 1 else None)
                    tcc = sc.cnt["pe"]
                    dd = 0
                    for c in range(2):
                        w = []
                        if c == 0:
                            w = [("pe", tcc), ("pe", g_done[P]), ("pool", p_dd)]
                        dd = sc.emit("dve", lambda e,
                                     o=sS[:, P * 512 + c * 256: P * 512 + c * 256 + 256],
                                     a=psS[P][:, c * 256:(c + 1) * 256],
                                     b=DDt[:, (h * 2 + c) * 256:(h * 2 + c + 1) * 256]:
                                     e.tensor_mul(o, a, b), wait=w, inc="dve")
                    psS_free[P] = ("dve", dd)
                    ve = sc.emit("pool", lambda e, o=qh[:, P * 256:P * 256 + 256],
                                 a=qb[:, h * HALF + tc: h * HALF + tc + 256],
                                 b=QD[:, h * 256:(h + 1) * 256]:
                                 e.tensor_mul(o, a, b),
                                 wait=[("act", a_qd), ("pe", g_done[P])],
                                 inc="pool")
                    vf = sc.emit("act", lambda e, o=kvb[:, P * 128:P * 128 + 128],
                                 i=self.kv[:, h * DH:(h + 1) * DH]:
                                 e.activation(o, i, AF.Copy),
                                 wait=[("dve", self.kv_stt[h]), ("pe", g_done[P])],
                                 inc="act")
                    sc.emit("pe", lambda e, o=psO[P][:, :],
                            a=kvb[:, P * 128:P * 128 + 128],
                            b=qh[:, P * 256:P * 256 + 256]:
                            e.matmul(o, a, b, start=True, stop=False),
                            wait=[("act", vf), ("act", j_done[P]),
                                  ("pool", ve), ("dve", dd)])
                    for c in range(2):
                        sc.emit("pe", lambda e, o=psO[P][:, :],
                                a=vT[:, (jj * 2 + c) * D + h * 128:
                                     (jj * 2 + c) * D + h * 128 + 128],
                                b=sS[:, P * 512 + c * 256: P * 512 + c * 256 + 256],
                                t=(c == 1):
                                e.matmul(o, a, b, start=False, stop=t),
                                inc="pe" if c == 1 else None)
                    tg = sc.cnt["pe"]
                    g_done[P] = tg
                    for c in range(2):
                        w = []
                        if c == 0:
                            w = [("dve", b_done[P]), ("dve", i_done[P])]
                        sc.emit("pe", lambda e, o=psKV[P][:, :],
                                a=khat[:, P * 256 + c * 128: P * 256 + c * 128 + 128],
                                b=vT[:, (jj * 2 + c) * D + h * 128:
                                     (jj * 2 + c) * D + h * 128 + 128],
                                s=(c == 0), t=(c == 1):
                                e.matmul(o, a, b, start=s, stop=t),
                                wait=w, inc="pe" if c == 1 else None)
                    th = sc.cnt["pe"]
                    kvmm_done[P] = th
                    di = sc.emit("dve", lambda e, o=self.kv[:, h * DH:(h + 1) * DH],
                                 s=self.BD[:, h:h + 1], i=psKV[P][:, :]:
                                 e.scalar_tensor_tensor(o, o, s, i, ALU.mult, ALU.add),
                                 wait=[("pe", th), ("act", vf)], inc="dve")
                    self.kv_stt[h] = di
                    i_done[P] = di
                    j_done[P] = sc.emit("act",
                                        lambda e, o=ao[:, h * 256:(h + 1) * 256],
                                        i=psO[P][:, :]: e.activation(o, i, AF.Copy),
                                        wait=[("pe", tg)], inc="act")

                aos = [ao[:, c * 256:(c + 1) * 256] for c in range(NF)]
                sc.emit("act", lambda e, o=s2[:, 0:256], i=aos[0]:
                        e.activation(o, i, AF.Square),
                        wait=[("act", max(j_done))], inc="act")
                d_s2 = 0
                sq_war = [0, 0]
                for c in range(1, NF):
                    P2 = c % 2
                    a = sc.emit("act", lambda e, o=sqtA[:, P2 * 512:P2 * 512 + 256],
                                i=aos[c]: e.activation(o, i, AF.Square),
                                wait=[("dve", sq_war[P2])], inc="act")
                    d_s2 = sc.emit("dve", lambda e, o=s2[:, 0:256],
                                   i=sqtA[:, P2 * 512:P2 * 512 + 256]:
                                   e.tensor_add(o, o, i),
                                   wait=[("act", a)], inc="dve")
                    sq_war[P2] = d_s2
                t1 = sc.emit("pe", lambda e, o=psS[0][0:1, 0:256],
                             a=self.ones_cf[:, :], b=s2[:, 0:256]:
                             e.matmul(o, a, b, start=True, stop=True),
                             wait=[("dve", d_s2), psS_free[0]], inc="pe")
                d = sc.emit("dve", lambda e, o=rowb[0:1, 0:256],
                            i=psS[0][0:1, 0:256]:
                            e.tensor_scalar(o, i, 1.0 / 2048.0, None, ALU.mult),
                            wait=[("pe", t1)], inc="dve")
                a_sd = sc.emit("act", lambda e, o=rowb[0:1, 512:768],
                               i=rowb[0:1, 0:256]:
                               e.activation(o, i, AF.Sqrt, bias=self.epsv[0:1, 1:2]),
                               wait=[("dve", d)], inc="act")
                d_rs = sc.emit("dve", lambda e, o=rowb[0:1, 1024:1280],
                               i=rowb[0:1, 512:768]: e.reciprocal(o, i),
                               wait=[("act", a_sd)], inc="dve")
                tB = sc.emit("pe", lambda e, o=psS[1][:, 0:256], a=self.ones_r[:, :],
                             b=rowb[0:1, 1024:1280]:
                             e.matmul(o, a, b, start=True, stop=True),
                             wait=[("dve", d_rs), psS_free[1]], inc="pe")
                d_bc = sc.emit("dve", lambda e, o=s1[:, 0:256], i=psS[1][:, 0:256]:
                               e.tensor_copy(o, i), wait=[("pe", tB)], inc="dve")
                psS_free[0] = ("dve", d)
                psS_free[1] = ("dve", d_bc)
                rg_war = [0, 0]
                for c in range(NF):
                    P2 = c % 2
                    dm = sc.emit("dve", lambda e, o=rgt[:, 0:256],
                                 a=aos[c], s=self.nw[:, c:c + 1], b=s1[:, 0:256]:
                                 e.scalar_tensor_tensor(o, a, s, b, ALU.mult, ALU.mult),
                                 wait=[("dve", d_bc), ("pool", rg_war[0])],
                                 inc="dve")
                    pm = sc.emit("pool", lambda e,
                                 o=gate[:, c * HALF + tc: c * HALF + tc + 256],
                                 a=rgt[:, 0:256]:
                                 e.tensor_mul(o, a, o),
                                 wait=[("dve", dm)], inc="pool")
                    rg_war[0] = pm
                rg_done = sc.cnt["pool"]
            return rg_done

        def emit_wout(p, rg_done, wst, gen=None):
            t0 = p * HALF
            wout_last = 0
            for ofb in range(8):
                par = ofb % 2
                wsl = wst[:, par * NF * 256:(par + 1) * NF * 256]
                l_w = sc.dma(
                    "pool", wsl.rearrange("p (c f) -> p c f", c=NF),
                    self.woutB.ap()[ofb],
                    wait=[("pe", blk_last[par]), ("lc", lc_attn)], sem=f"q{2 + par}")
                self.maybe_conv(2)
                for ofh in range(2):
                    u = (ofb * 2 + ofh) % 2
                    for kc in range(NF):
                        w = []
                        if kc == 0:
                            w = [(f"q{2 + par}", l_w), psS_free[u],
                                 ("pool", rg_done)]
                        sc.emit("pe", lambda e, o=psS[u][:, 0:HALF],
                                a=wsl[:, kc * 256 + ofh * 128:
                                      kc * 256 + ofh * 128 + 128],
                                b=gate[:, kc * HALF:(kc + 1) * HALF],
                                s=(kc == 0), t=(kc == NF - 1):
                                e.matmul(o, a, b, start=s, stop=t),
                                wait=w, inc="pe" if kc == NF - 1 else None)
                    tmm = sc.cnt["pe"]
                    blk_last[par] = tmm
                    wout_last = tmm
                    oc = ofb * 2 + ofh
                    d_x = sc.emit("dve", lambda e, o=self.xsl(oc, t0, HALF),
                                  i=psS[u][:, 0:HALF]: e.tensor_add(o, o, i),
                                  wait=[("pe", tmm)], inc="dve")
                    psS_free[u] = ("dve", d_x)
                    if gen is not None:
                        for _ in range(3):
                            if next(gen, "END") == "END":
                                gen = None
                                break
            if gen is not None:
                for _ in gen:
                    pass
            return wout_last

        stA = ExitStack()
        wstA = stA.enter_context(self.sbuf("p2_wstA", [128, 2 * NF * 256], BF16))
        xs0 = [self.xsl(c, 0, HALF) for c in range(NF)]
        l1s = [ln1[:, c * HALF:(c + 1) * HALF] for c in range(NF)]
        ln_done = ("pool", self.emit_ln(xs0, l1s, HALF,
                                        (s1, s2, sqtA, rowb, psS[0], psS[1])))
        for ofb in range(8):
            std_block(self.wgateB, 0, ofb, gate, AF.Sigmoid, wstA, ln_done)
        for ofb in range(8):
            std_block(self.wqkvB, 0, ofb, qb, AF.Silu, wstA, ln_done)
        for ofb in range(8):
            std_block(self.wqkvB, 2048, ofb, kb, AF.Silu, wstA, ln_done)
        for ofb in range(8):
            v_block(ofb, wstA, ln_done)
        sc.flush()
        stA.close()

        stB = ExitStack()
        rg_done = emit_scan(0, stB)
        sc.flush()
        stB.close()

        stC = ExitStack()
        wstC = stC.enter_context(self.sbuf("p2_wstC", [128, 2 * NF * 256], BF16))
        psA2 = stC.enter_context(self.psum("p2_psA2", [128, 512], F32))
        psB2 = stC.enter_context(self.psum("p2_psB2", [128, 512], F32))
        res1 = {}
        gen = self.emit_ln_gen([self.xsl(c, HALF, HALF) for c in range(NF)],
                               l1s, HALF, (s1, s2, sqtA, rowb, psA2, psB2),
                               res1, sum_eng="dve", ap2_eng="dve")
        wout_last = emit_wout(0, rg_done, wstC, gen=gen)
        ln_done = res1["done"]
        for ofb in range(8):
            std_block(self.wqkvB, 0, ofb, qb, AF.Silu, wstC, ln_done)
        for ofb in range(8):
            std_block(self.wqkvB, 2048, ofb, kb, AF.Silu, wstC, ln_done)
        for ofb in range(8):
            v_block(ofb, wstC, ln_done)
        for ofb in range(8):
            std_block(self.wgateB, 0, ofb, gate, AF.Sigmoid, wstC, ln_done,
                      evac_war=("pe", wout_last))
        sc.flush()
        stC.close()

        stD = ExitStack()
        rg_done = emit_scan(1, stD)
        sc.flush()
        stD.close()

        stE = ExitStack()
        wstE = stE.enter_context(self.sbuf("p2_wstE", [128, 2 * NF * 256], BF16))
        emit_wout(1, rg_done, wstE)
        sc.flush()
        stE.close()
        st.close()

    def ffn(self, tag):
        sc, nc = self.sc, self.nc
        st = ExitStack()
        ln2 = st.enter_context(self.sbuf(f"{tag}_ln2", [128, 2 * NF * HALF], BF16))
        hfull = st.enter_context(self.sbuf(f"{tag}_h", [128, NI * HALF], BF16))
        rowb = st.enter_context(self.sbuf(f"{tag}_rowb", [1, 2048], F32))
        shilo = st.enter_context(self.sbuf(f"{tag}_shl", [1, 1024], BF16))
        s1 = st.enter_context(self.sbuf(f"{tag}_s1", [128, 512], F32))
        s2 = st.enter_context(self.sbuf(f"{tag}_s2", [128, 512], F32))
        sqtA = st.enter_context(self.sbuf(f"{tag}_sqtA", [128, 1024], F32))
        tmpG = st.enter_context(self.sbuf(f"{tag}_tG", [128, 512], F32))
        tmpU = st.enter_context(self.sbuf(f"{tag}_tU", [128, 512], F32))
        l2 = lambda t, c: ln2[:, (t * NF + c) * HALF:(t * NF + c + 1) * HALF]
        srow_f = s1[0:1, 0:HALF]
        srow_t = s2[0:1, 0:HALF]

        with ExitStack() as stl:
            psA = stl.enter_context(self.psum(f"{tag}_psA", [128, 512], F32))
            psB = stl.enter_context(self.psum(f"{tag}_psB", [128, 512], F32))
            xs0 = [self.xsl(c, 0, HALF) for c in range(NF)]
            ln_done = [("pool", self.emit_ln(xs0, [l2(0, c) for c in range(NF)],
                                             HALF, (s1, s2, sqtA, rowb, psA, psB))),
                       None]
            sc.flush()

        for th in range(2):
            t0 = th * HALF
            stG = ExitStack()
            psG = [stG.enter_context(self.psum(f"{tag}_psG{i}", [128, 512], F32))
                   for i in range(2)]
            psU = [stG.enter_context(self.psum(f"{tag}_psU{i}", [128, 512], F32))
                   for i in range(2)]
            psR = stG.enter_context(self.psum(f"{tag}_psR", [128, 512], F32))
            wg = stG.enter_context(self.sbuf(f"{tag}_wg", [128, 2 * NF * 128], BF16))
            wu = stG.enter_context(self.sbuf(f"{tag}_wu", [128, 2 * NF * 128], BF16))
            lngen = None
            resB = {}
            if th == 0:
                psA2 = stG.enter_context(self.psum(f"{tag}_psA2", [128, 512], F32))
                psB2 = stG.enter_context(self.psum(f"{tag}_psB2", [128, 512], F32))
                xs1 = [self.xsl(c, HALF, HALF) for c in range(NF)]
                lngen = self.emit_ln_gen(xs1, [l2(1, c) for c in range(NF)],
                                         HALF, (s1, s2, sqtA, rowb, psA2, psB2),
                                         resB, sum_eng="dve", ap2_eng="dve")
            if self.conv_done:
                sc.waitonly("pool", [("lc", self.conv_done)])
                self.conv_done = 0
            hmul_last = 0
            gevac = [0, 0]
            uevac = [0, 0]
            blk_last = [0, 0]
            tsr = 0
            pend_sr = []

            def emit_sr(ic, dh):
                nonlocal tsr
                tsr = sc.emit("pe", lambda e, o=psR[0:1, 0:HALF],
                              a=self.ones_cb[:, :],
                              b=hfull[:, ic * HALF:(ic + 1) * HALF],
                              s=(ic == 0), t=(ic == NI - 1):
                              e.matmul(o, a, b, start=s, stop=t),
                              wait=[("dve", dh)], inc="pe")
            for ic in range(NI):
                P = ic % 2
                wgs = wg[:, P * NF * 128:(P + 1) * NF * 128]
                wus = wu[:, P * NF * 128:(P + 1) * NF * 128]
                lg = sc.dma("pool", wgs.rearrange("p (c f) -> p c f", c=NF),
                            self.u8g.ap()[ic], wait=[("pe", blk_last[P])],
                            sem=f"q{4 + P}")
                lu = sc.dma("pool", wus.rearrange("p (c f) -> p c f", c=NF),
                            self.u8u.ap()[ic], wait=[("pe", blk_last[P])],
                            sem=f"q{6 + P}")
                for kc in range(NF):
                    w = []
                    if kc == 0:
                        w = [(f"q{4 + P}", lg), (f"q{6 + P}", lu),
                             ("act", gevac[P]), ln_done[th]]
                    sc.emit("pe", lambda e, o=psG[P][:, 0:HALF],
                            a=wgs[:, kc * 128:(kc + 1) * 128],
                            b=l2(th, kc),
                            s=(kc == 0), t=(kc == NF - 1):
                            e.matmul(o, a, b, start=s, stop=t),
                            wait=w, inc="pe" if kc == NF - 1 else None)
                tgG = sc.cnt["pe"]
                for kc in range(NF):
                    w = [("act", uevac[P])] if kc == 0 else []
                    sc.emit("pe", lambda e, o=psU[P][:, 0:HALF],
                            a=wus[:, kc * 128:(kc + 1) * 128],
                            b=l2(th, kc),
                            s=(kc == 0), t=(kc == NF - 1):
                            e.matmul(o, a, b, start=s, stop=t),
                            wait=w, inc="pe" if kc == NF - 1 else None)
                tgU = sc.cnt["pe"]
                blk_last[P] = tgU
                ag = sc.emit("act", lambda e, o=tmpG[:, :],
                             i=psG[P][:, 0:HALF], s=self.gsc[:, ic:ic + 1]:
                             e.activation(o, i, AF.Silu, scale=s),
                             wait=[("pe", tgG), ("dve", hmul_last)], inc="act")
                gevac[P] = ag
                au = sc.emit("act", lambda e, o=tmpU[:, :],
                             i=psU[P][:, 0:HALF], s=self.usc[:, ic:ic + 1]:
                             e.activation(o, i, AF.Copy, scale=s),
                             wait=[("pe", tgU), ("dve", hmul_last)], inc="act")
                uevac[P] = au
                dh = sc.emit("dve", lambda e, o=hfull[:, ic * HALF:(ic + 1) * HALF],
                             a=tmpG[:, :], b=tmpU[:, :]:
                             e.tensor_mul(o, a, b), wait=[("act", au)], inc="dve")
                hmul_last = dh
                pend_sr.append((ic, dh))
                if len(pend_sr) > 2:
                    emit_sr(*pend_sr.pop(0))
                if lngen is not None:
                    for _ in range(2):
                        if next(lngen, "END") == "END":
                            lngen = None
                            break
            if lngen is not None:
                for _ in lngen:
                    pass
            if th == 0:
                ln_done[1] = resB["done"]
            for args in pend_sr:
                emit_sr(*args)
            d_sr = sc.emit("dve", lambda e, o=srow_f, i=psR[0:1, 0:HALF]:
                           e.tensor_copy(o, i), wait=[("pe", tsr)], inc="dve")
            a_hi = sc.emit("act", lambda e, o=shilo[0:1, 0:HALF], i=srow_f:
                           e.activation(o, i, AF.Copy),
                           wait=[("dve", d_sr)], inc="act")
            d_lo = sc.emit("dve", lambda e, o=srow_t, a=srow_f,
                           b=shilo[0:1, 0:HALF]:
                           e.tensor_sub(o, a, b), wait=[("act", a_hi)], inc="dve")
            a_lo = sc.emit("act", lambda e, o=shilo[0:1, 512:512 + HALF],
                           i=srow_t: e.activation(o, i, AF.Copy),
                           wait=[("dve", d_lo)], inc="act")
            h_done = hmul_last
            sc.flush()
            stG.close()

            stD = ExitStack()
            psD = [stD.enter_context(self.psum(f"{tag}_psD{i}", [128, 512], F32))
                   for i in range(8)]
            wd = stD.enter_context(self.sbuf(f"{tag}_wd", [128, 2 * 4096], BF16))
            psD_free = [0] * 8
            xadd = [0, 0]
            wd_last = [0, 0]
            tmpE = [tmpG, tmpU]
            for dhh in range(2):
                for i4 in range(16):
                    par = i4 % 2
                    wds = wd[:, par * 4096:(par + 1) * 4096]
                    lw = sc.dma(
                        "pool", wds.rearrange("p (c f) -> p c f", c=4),
                        self.u8d.ap()[i4, dhh],
                        wait=[("pe", wd_last[par])], sem=f"q{4 + par}")
                    for c in range(4):
                        ic = i4 * 4 + c
                        for dc in range(8):
                            w = []
                            if dc == 0 and c == 0:
                                w.append((f"q{4 + par}", lw))
                                if dhh == 0 and i4 == 0:
                                    w.append(("dve", h_done))
                            if ic == 0:
                                w.append(("dve", psD_free[dc]))
                            sc.emit("pe", lambda e, o=psD[dc][:, :],
                                    a=wds[:, c * 1024 + dc * 128:
                                          c * 1024 + dc * 128 + 128],
                                    b=hfull[:, ic * HALF:(ic + 1) * HALF],
                                    s=(ic == 0):
                                    e.matmul(o, a, b, start=s, stop=False),
                                    wait=w, inc="pe" if dc == 7 else None)
                        wd_last[par] = sc.cnt["pe"]
                for dc in range(8):
                    w = [("act", a_lo)] if dc == 0 else []
                    sc.emit("pe", lambda e, o=psD[dc][:, :]:
                            e.matmul(o, self.m128_r[0:1, :], shilo[0:1, 0:HALF],
                                     start=False, stop=False), wait=w)
                    tcor = sc.emit("pe", lambda e, o=psD[dc][:, :]:
                                   e.matmul(o, self.m128_r[0:1, :],
                                            shilo[0:1, 512:512 + HALF],
                                            start=False, stop=True), inc="pe")
                    dcg = dhh * 8 + dc
                    P = dc % 2
                    a_ev = sc.emit("act",
                                   lambda e, o=tmpE[P][:, :],
                                   i=psD[dc][:, :], s=self.dsc[:, dcg:dcg + 1]:
                                   e.activation(o, i, AF.Copy, scale=s),
                                   wait=[("pe", tcor), ("dve", xadd[P])],
                                   inc="act")
                    d_x = sc.emit("dve", lambda e, o=self.xsl(dcg, t0, HALF),
                                  i=tmpE[P][:, :]: e.tensor_add(o, o, i),
                                  wait=[("act", a_ev)], inc="dve")
                    xadd[P] = d_x
                    psD_free[dc] = d_x
            sc.flush()
            stD.close()
        st.close()

    def p4(self):
        sc, nc = self.sc, self.nc
        st = ExitStack()
        ln3 = st.enter_context(self.sbuf("p4_ln3", [128, 2 * NF * HALF], BF16))
        t1b = st.enter_context(self.sbuf("p4_t1", [128, 8 * HALF], BF16))
        wst = st.enter_context(self.sbuf("p4_wst", [128, 2 * NF * 256], BF16))
        s1 = st.enter_context(self.sbuf("p4_s1", [128, 512], F32))
        s2 = st.enter_context(self.sbuf("p4_s2", [128, 512], F32))
        sqtA = st.enter_context(self.sbuf("p4_sqtA", [128, 1024], F32))
        rowb = st.enter_context(self.sbuf("p4_rowb", [1, 2048], F32))
        tmpC = st.enter_context(self.sbuf("p4_tmpC", [128, 1024], F32))
        psA = st.enter_context(self.psum("p4_psA", [128, 512], F32))
        psB = st.enter_context(self.psum("p4_psB", [128, 512], F32))
        psA2 = st.enter_context(self.psum("p4_psA2", [128, 512], F32))
        psB2 = st.enter_context(self.psum("p4_psB2", [128, 512], F32))
        ps = [psA, psB]
        l3 = lambda t, c: ln3[:, (t * NF + c) * HALF:(t * NF + c + 1) * HALF]

        xs0 = [self.xsl(c, 0, HALF) for c in range(NF)]
        ln_done = [("pool", self.emit_ln(xs0, [l3(0, c) for c in range(NF)],
                                         HALF, (s1, s2, sqtA, rowb, psA, psB))),
                   None]
        sc.flush()
        for th in range(2):
            t0 = th * HALF
            lngen = None
            resB = {}
            if th == 0:
                xs1 = [self.xsl(c, HALF, HALF) for c in range(NF)]
                lngen = self.emit_ln_gen(xs1, [l3(1, c) for c in range(NF)],
                                         HALF, (s1, s2, sqtA, rowb, psA2, psB2),
                                         resB, sum_eng="dve", ap2_eng="dve")

            def drive(n):
                nonlocal lngen
                if lngen is None:
                    return
                for _ in range(n):
                    if next(lngen, "END") == "END":
                        lngen = None
                        break

            ps_free = [ln_done[th], ln_done[th]]
            blk_last = [0, 0]
            for ofb in range(4):
                par = ofb % 2
                wsl = wst[:, par * NF * 256:(par + 1) * NF * 256]
                l_w = sc.dma(
                    "pool", wsl.rearrange("p (c f) -> p c f", c=NF),
                    self.bTB.ap()[ofb],
                    wait=[("pe", blk_last[par])], sem=f"q{2 + par}")
                for ofh in range(2):
                    u = (ofb * 2 + ofh) % 2
                    for kc in range(NF):
                        w = []
                        if kc == 0:
                            w = [(f"q{2 + par}", l_w), ps_free[u], ln_done[th]]
                        sc.emit("pe", lambda e, o=ps[u][:, 0:HALF],
                                a=wsl[:, kc * 256 + ofh * 128:
                                      kc * 256 + ofh * 128 + 128],
                                b=l3(th, kc),
                                s=(kc == 0), t=(kc == NF - 1):
                                e.matmul(o, a, b, start=s, stop=t),
                                wait=w, inc="pe" if kc == NF - 1 else None)
                    tmm = sc.cnt["pe"]
                    blk_last[par] = tmm
                    oc = ofb * 2 + ofh
                    a_ev = sc.emit("act",
                                   lambda e, o=t1b[:, oc * HALF:(oc + 1) * HALF],
                                   i=ps[u][:, 0:HALF]: e.activation(o, i, AF.Tanh),
                                   wait=[("pe", tmm)], inc="act")
                    ps_free[u] = ("act", a_ev)
                    drive(4)
            t1_done = sc.cnt["act"]
            tc_war = [0, 0]
            for ofb in range(8):
                par = ofb % 2
                wsl = wst[:, par * NF * 256:(par + 1) * NF * 256]
                l_w = sc.dma(
                    "pool", wsl[:, 0:8 * 256].rearrange("p (c f) -> p c f", c=8),
                    self.cTB.ap()[ofb],
                    wait=[("pe", blk_last[par])], sem=f"q{2 + par}")
                for ofh in range(2):
                    u = (ofb * 2 + ofh) % 2
                    for kc in range(8):
                        w = []
                        if kc == 0:
                            w = [(f"q{2 + par}", l_w), ps_free[u], ("act", t1_done)]
                        sc.emit("pe", lambda e, o=ps[u][:, 0:HALF],
                                a=wsl[:, kc * 256 + ofh * 128:
                                      kc * 256 + ofh * 128 + 128],
                                b=t1b[:, kc * HALF:(kc + 1) * HALF],
                                s=(kc == 0), t=(kc == 7):
                                e.matmul(o, a, b, start=s, stop=t),
                                wait=w, inc="pe" if kc == 7 else None)
                    tmm = sc.cnt["pe"]
                    blk_last[par] = tmm
                    oc = ofb * 2 + ofh
                    P = oc % 2
                    d_c = sc.emit("dve", lambda e, o=tmpC[:, P * 512:P * 512 + HALF],
                                  a=l3(th, oc),
                                  s=self.dv[:, oc:oc + 1], i=ps[u][:, 0:HALF]:
                                  e.scalar_tensor_tensor(o, a, s, i, ALU.mult,
                                                         ALU.add),
                                  wait=[("pe", tmm), ("pool", tc_war[P])],
                                  inc="dve")
                    ps_free[u] = ("dve", d_c)
                    p_x = sc.emit("pool", lambda e, o=self.xsl(oc, t0, HALF),
                                  i=tmpC[:, P * 512:P * 512 + HALF]:
                                  e.tensor_add(o, o, i),
                                  wait=[("dve", d_c)], inc="pool")
                    tc_war[P] = p_x
                    drive(4)
            if lngen is not None:
                for _ in lngen:
                    pass
            if th == 0:
                ln_done[1] = resB["done"]
            sc.flush()
        st.close()

    def p6(self):
        sc, nc = self.sc, self.nc
        st = ExitStack()
        strip = st.enter_context(self.sbuf("p6_strip", [128, 2 * D], F32))
        psT = [st.enter_context(self.psum(f"p6_psT{i}", [128, 512], F32))
               for i in range(2)]
        strip_free = [0, 0]
        ps_free = [("dve", 0), ("dve", 0)]
        for s in range(8):
            par = s % 2
            sl = strip[:, par * D:(par + 1) * D]
            ev_tgts = []
            for g in range(4):
                gp = g % 2
                for qv in range(4):
                    c = g * 4 + qv
                    w = [ps_free[gp]] if qv == 0 else []
                    sc.emit("pe", lambda e, o=psT[gp][:, qv * 128:(qv + 1) * 128],
                            i=self.xsl(c, s * 128, 128):
                            e.transpose(o, i, self.idf[:, :]),
                            wait=w, inc="pe" if qv == 3 else None)
                tpe = sc.cnt["pe"]
                eng = "dve" if g % 2 == 0 else "act"
                tgt = sc.emit(eng, lambda e, o=sl[:, g * 512:(g + 1) * 512],
                              i=psT[gp][:, 0:512], eng=eng:
                              (e.tensor_copy(o, i) if eng == "dve"
                               else e.activation(o, i, AF.Copy)),
                              wait=[("pe", tpe), (f"q{par}", strip_free[par])],
                              inc="act" if eng == "act" else "dve")
                ps_free[gp] = (eng, tgt)
                ev_tgts.append((eng, tgt))
            strip_free[par] = sc.store(
                self.out.ap()[s * 128:(s + 1) * 128, :], sl, wait=ev_tgts,
                sem=f"q{par}")
        sc.waitonly("sp", [("q0", sc.cnt["q0"]), ("q1", sc.cnt["q1"])])
        sc.flush()
        st.close()

    def build(self):
        self.p0()
        if self.stage >= 2:
            self.p1()
            self.p2()
        self.attn_ctx.close()
        self.finish_conv()
        self.sc.flush()
        if self.stage >= 3:
            self.ffn("p3")
        if self.stage >= 4:
            self.p4()
        if self.stage >= 5:
            self.ffn("p5")
        self.p6()
        return self.nc


_prog_cache = {}
_last_results = None


def _get_prog(stage):
    if stage not in _prog_cache:
        _prog_cache[stage] = Builder(stage).build()
    return _prog_cache[stage]


def kernel(L0_attn_a_hidden_states, L0_attn_a_slope_rate, L0_attn_a_w_qkv,
           L0_attn_a_w_gate, L0_attn_a_w_out, L0_attn_a_norm_w,
           L0_attn_b_B, L0_attn_b_C, L0_attn_b_D,
           L0_ffn_gate_w, L0_ffn_gate_scale, L0_ffn_up_w, L0_ffn_up_scale,
           L0_ffn_down_w, L0_ffn_down_scale):
    cc = np.ascontiguousarray
    x = cc(np.asarray(L0_attn_a_hidden_states, np.float32)).reshape(8192, D)
    col = lambda v, n: cc(np.asarray(v, np.float32).reshape(n, 128).T)
    blk = lambda a, nb: cc(a.reshape(16, 128, nb, 256).transpose(2, 1, 0, 3))
    gw_t = np.asarray(L0_ffn_gate_w, np.int32).T
    uw_t = np.asarray(L0_ffn_up_w, np.int32).T
    dw_t = np.asarray(L0_ffn_down_w, np.int32).T
    cT_t = np.asarray(L0_attn_b_C, np.float32).T
    shared = {
        "wqkvT": blk(np.asarray(L0_attn_a_w_qkv, np.float32).T, 24),
        "wgateT": blk(np.asarray(L0_attn_a_w_gate, np.float32).T, 8),
        "woutT": blk(np.asarray(L0_attn_a_w_out, np.float32).T, 8),
        "bT": blk(np.asarray(L0_attn_b_B, np.float32).T, 4),
        "cT": cc(cT_t.reshape(8, 128, 8, 256).transpose(2, 1, 0, 3)),
        "gwT": cc(gw_t.reshape(16, 128, NI, 128).transpose(2, 1, 0, 3)),
        "uwT": cc(uw_t.reshape(16, 128, NI, 128).transpose(2, 1, 0, 3)),
        "dwT": cc(dw_t.reshape(16, 4, 128, 2, 1024).transpose(0, 3, 2, 1, 4)),
        "slope": cc(np.asarray(L0_attn_a_slope_rate, np.float32).reshape(1, H)),
        "nw_c": col(L0_attn_a_norm_w, NF),
        "dv_c": col(L0_attn_b_D, NF),
        "gsc_c": col(L0_ffn_gate_scale, NI),
        "usc_c": col(L0_ffn_up_scale, NI),
        "dsc_c": col(L0_ffn_down_scale, NF),
        "ident": np.eye(128, dtype=np.float32),
    }
    zeros = np.zeros((TOK, D), np.float32)
    in_maps = []
    for c in range(NCORES):
        m = dict(shared)
        m["xo"] = cc(x[c * TOK:(c + 1) * TOK])
        m["xp"] = cc(x[(c - 1) * TOK:c * TOK]) if c % 2 == 1 else zeros
        in_maps.append(m)
    nc = _get_prog(STAGE)
    res = run_bass_kernel_spmd(nc, in_maps, core_ids=list(range(NCORES)))
    global _last_results
    _last_results = res.results
    outs = [np.asarray(r["out"]) for r in res.results]
    return np.concatenate(outs, axis=0).reshape(4, 2048, D)


# version 15
# speedup vs baseline: 1.1719x; 1.0367x over previous
import numpy as np
from contextlib import ExitStack

import concourse.bass as bass
import concourse.mybir as mybir
from concourse.bass_utils import run_bass_kernel_spmd

F32 = mybir.dt.float32
BF16 = mybir.dt.bfloat16
I32 = mybir.dt.int32
U8 = mybir.dt.uint8
AF = mybir.ActivationFunctionType
ALU = mybir.AluOpType

NCORES = 8
D = 2048
NF = 16
H = 16
DH = 128
TOK = 1024
BLK = 256
HALF = 512
FI = 8192
NI = 64
NSTATE = 1024
LN_EPS = 1e-5
RMS_EPS = 1e-6

DBG = None
STAGE = 5

ENGS = ("pe", "act", "dve", "pool", "sp")


class Sched:

    def __init__(self, nc, stack):
        self.nc = nc
        self.q = {e: [] for e in ENGS}
        self.sems = {}
        self.cnt = {}
        for name in ("ld", "lc", "lx", "st", "pe", "act", "dve", "pool",
                     "q0", "q1", "q2", "q3", "q4", "q5", "q6", "q7"):
            self.sems[name] = stack.enter_context(nc.semaphore(name))
            self.cnt[name] = 0

    def emit(self, eng, fn, wait=(), inc=None, n=1):
        w = tuple((s, v) for (s, v) in wait if v and v > 0)
        for s, v in w:
            assert v <= self.cnt[s], f"forward wait {s} {v} > {self.cnt[s]}"
        tgt = None
        if inc is not None:
            self.cnt[inc] += n
            tgt = self.cnt[inc]
        sems = self.sems

        pad = eng in ("dve", "act", "pool") and not w

        def thunk(e, w=w, fn=fn, inc=inc, n=n, pad=pad):
            if pad:
                e.nop()
            for s, v in w:
                e.wait_ge(sems[s], v)
            ins = fn(e)
            if inc is not None:
                ins.then_inc(sems[inc], n)

        self.q[eng].append(thunk)
        return tgt

    def waitonly(self, eng, wait):
        sems = self.sems
        w = tuple((s, v) for (s, v) in wait if v and v > 0)

        def thunk(e, w=w):
            for s, v in w:
                e.wait_ge(sems[s], v)

        self.q[eng].append(thunk)

    def dma(self, q, dst, src, wait=(), sem=None):
        if sem is None:
            sem = "ld" if q == "pool" else "lx"
        return self.emit(q if q == "pool" else "sp",
                         lambda e, d=dst, s=src: e.dma_start(d, s),
                         wait=wait, inc=sem, n=16)

    def store(self, dst, src, wait=(), sem="st"):
        return self.emit("sp", lambda e, d=dst, s=src: e.dma_start(d, s),
                         wait=wait, inc=sem, n=16)

    def flush(self):
        q = self.q
        self.q = {e: [] for e in ENGS}
        if not any(q.values()):
            return
        with self.nc.Block() as blk:
            m = {"pe": blk.tensor, "act": blk.scalar, "dve": blk.vector,
                 "pool": blk.gpsimd, "sp": blk.sync}
            for e in ENGS:
                if q[e]:
                    m[e](lambda eng, lst=q[e]: [t(eng) for t in lst])


class Builder:
    def __init__(self, stage=STAGE):
        self.stage = stage
        nc = bass.Bass("TRN2", target_bir_lowering=False)
        self.nc = nc
        self.stack = ExitStack()
        self.sc = Sched(nc, self.stack)

        dt = nc.dram_tensor
        self.xo = dt("xo", [TOK, D], F32, kind="ExternalInput")
        self.xp = dt("xp", [TOK, D], F32, kind="ExternalInput")
        self.wqkvT = dt("wqkvT", [24, 128, NF, 256], F32, kind="ExternalInput")
        self.wgateT = dt("wgateT", [8, 128, NF, 256], F32, kind="ExternalInput")
        self.woutT = dt("woutT", [8, 128, NF, 256], F32, kind="ExternalInput")
        self.bT = dt("bT", [4, 128, NF, 256], F32, kind="ExternalInput")
        self.cT = dt("cT", [8, 128, 8, 256], F32, kind="ExternalInput")
        self.gwT = dt("gwT", [NI, 128, 16, 128], I32, kind="ExternalInput")
        self.uwT = dt("uwT", [NI, 128, 16, 128], I32, kind="ExternalInput")
        self.dwT = dt("dwT", [16, 2, 128, 4, 1024], I32, kind="ExternalInput")
        self.slope = dt("slope", [1, H], F32, kind="ExternalInput")
        self.nw_c = dt("nw_c", [128, NF], F32, kind="ExternalInput")
        self.dv_c = dt("dv_c", [128, NF], F32, kind="ExternalInput")
        self.gsc_c = dt("gsc_c", [128, NI], F32, kind="ExternalInput")
        self.usc_c = dt("usc_c", [128, NI], F32, kind="ExternalInput")
        self.dsc_c = dt("dsc_c", [128, NF], F32, kind="ExternalInput")
        self.ident = dt("ident", [128, 128], F32, kind="ExternalInput")
        self.out = dt("out", [TOK, D], F32, kind="ExternalOutput")
        self.dbgt = dt("dbg", [128, 16384], F32, kind="ExternalOutput")
        self.u8g = dt("u8g", [NI, 128, 16, 128], U8, kind="Internal")
        self.wqkvB = dt("wqkvB", [24, 128, NF, 256], BF16, kind="Internal")
        self.wgateB = dt("wgateB", [8, 128, NF, 256], BF16, kind="Internal")
        self.woutB = dt("woutB", [8, 128, NF, 256], BF16, kind="Internal")
        self.bTB = dt("bTB", [4, 128, NF, 256], BF16, kind="Internal")
        self.cTB = dt("cTB", [8, 128, 8, 256], BF16, kind="Internal")
        self.u8u = dt("u8u", [NI, 128, 16, 128], U8, kind="Internal")
        self.u8d = dt("u8d", [16, 2, 128, 4, 1024], U8, kind="Internal")

        al = nc.alloc_sbuf_tensor
        self.x_sb = al("x_sb", [128, NF * TOK], F32)
        self.idf = al("idf", [128, 128], F32)
        self.ones_r = al("ones_r", [1, 128], F32)
        self.ones_cf = al("ones_cf", [128, 1], F32)
        self.ones_cb = al("ones_cb", [128, 1], BF16)
        self.m128_r = al("m128_r", [1, 128], BF16)
        self.slope_sb = al("slope_sb", [1, H], F32)
        self.nw = al("nw", [128, NF], F32)
        self.dv = al("dv", [128, NF], F32)
        self.gsc = al("gsc", [128, NI], F32)
        self.usc = al("usc", [128, NI], F32)
        self.dsc = al("dsc", [128, NF], F32)
        self.epsv = al("epsv", [1, 2], F32)
        self.attn_ctx = ExitStack()
        ac = lambda n_, sh, dt_: self.attn_ctx.enter_context(
            nc.sbuf_tensor(n_, sh, dt_))
        self.kv = ac("kv", [128, H * DH], F32)
        self.idb = ac("idb", [128, 128], BF16)
        self.SF = ac("SF", [128, H], F32)
        self.SFn = ac("SFn", [128, H], F32)
        self.KD = ac("KD", [128, 2 * H], F32)
        self.BD = ac("BD", [128, H], F32)

        self.conv_jobs = []
        self.lc_kvjob = []
        if self.stage >= 2:
            for j in range(8):
                self.conv_jobs.append((
                    self.wqkvB.ap()[8 + 2 * j: 8 + 2 * (j + 1)],
                    self.wqkvT.ap()[8 + 2 * j: 8 + 2 * (j + 1)]))
            for j in range(4):
                self.conv_jobs.append((
                    self.wqkvB.ap()[2 * j: 2 * (j + 1)],
                    self.wqkvT.ap()[2 * j: 2 * (j + 1)]))
            for j in range(4):
                self.conv_jobs.append((
                    self.wgateB.ap()[2 * j: 2 * (j + 1)],
                    self.wgateT.ap()[2 * j: 2 * (j + 1)]))
            for j in range(4):
                self.conv_jobs.append((
                    self.woutB.ap()[2 * j: 2 * (j + 1)],
                    self.woutT.ap()[2 * j: 2 * (j + 1)]))
            for j in range(2):
                self.conv_jobs.append((
                    self.bTB.ap()[2 * j: 2 * (j + 1)],
                    self.bT.ap()[2 * j: 2 * (j + 1)]))
            for j in range(2):
                self.conv_jobs.append((
                    self.cTB.ap()[4 * j: 4 * (j + 1)],
                    self.cT.ap()[4 * j: 4 * (j + 1)]))
        self.n_attn_conv = len(self.conv_jobs)
        if self.stage >= 3:
            for ib in range(NI):
                self.conv_jobs.append((self.u8g.ap()[ib], self.gwT.ap()[ib]))
                self.conv_jobs.append((self.u8u.ap()[ib], self.uwT.ap()[ib]))
            for r in range(16):
                self.conv_jobs.append((self.u8d.ap()[r], self.dwT.ap()[r]))
        self.conv_done = 0
        self.lc_hist = []
        self.kv_stt = [0] * H

    def sbuf(self, name, shape, dtype):
        self._uid = getattr(self, "_uid", 0) + 1
        return self.nc.sbuf_tensor(f"{name}_{self._uid}", shape, dtype)

    def psum(self, name, shape, dtype):
        self._uid = getattr(self, "_uid", 0) + 1
        return self.nc.psum_tensor(f"{name}_{self._uid}", shape, dtype)

    def maybe_conv(self, k=1):
        for _ in range(k):
            if self.conv_jobs:
                dst, src = self.conv_jobs.pop(0)
                self.conv_done = self.sc.dma("pool", dst, src, sem="lc")
                self.lc_hist.append(self.conv_done)

    def finish_conv(self):
        while self.conv_jobs:
            self.maybe_conv()

    def dump(self, tag, p, region_ap, width):
        if DBG != (tag, p):
            return
        sc = self.sc
        sc.flush()
        t = sc.dma("pool", self.dbgt.ap()[:, 0:width], region_ap)
        sc.waitonly("pool", [("ld", t)])
        sc.flush()

    def xsl(self, c, t0, L):
        return self.x_sb[:, c * TOK + t0: c * TOK + t0 + L]

    def emit_ln_gen(self, xs, outs, L, scr, res, sum_eng="pool",
                    ap2_eng="pool"):
        sc = self.sc
        s1, s2, sqtA, rowb, psA, psB = scr
        n = len(xs)
        inv = 1.0 / (128.0 * n)
        row = lambda i: rowb[0:1, i * 512: i * 512 + L]

        sc.emit("act", lambda e, o=s2[:, 0:L], i=xs[0]:
                e.activation(o, i, AF.Square), inc="act")
        d_s2 = 0
        sq_war = [0, 0]
        for c in range(1, n):
            P = c % 2
            a = sc.emit("act", lambda e, o=sqtA[:, P * 512:P * 512 + L], i=xs[c]:
                        e.activation(o, i, AF.Square),
                        wait=[("dve", sq_war[P])], inc="act")
            d_s2 = sc.emit("dve",
                           lambda e, o=s2[:, 0:L], i=sqtA[:, P * 512:P * 512 + L]:
                           e.tensor_add(o, o, i), wait=[("act", a)], inc="dve")
            sq_war[P] = d_s2
            yield
        p_s1 = sc.emit(sum_eng, lambda e, o=s1[:, 0:L], i=xs[0]:
                       e.tensor_copy(o, i), inc=sum_eng)
        for c in range(1, n):
            p_s1 = sc.emit(sum_eng, lambda e, o=s1[:, 0:L], i=xs[c]:
                           e.tensor_add(o, o, i), inc=sum_eng)
            if c % 2 == 0:
                yield
        t1 = sc.emit("pe", lambda e, o=psA[0:1, 0:L], a=self.ones_cf[:, :],
                     b=s1[:, 0:L]: e.matmul(o, a, b, start=True, stop=True),
                     wait=[(sum_eng, p_s1)], inc="pe")
        t2 = sc.emit("pe", lambda e, o=psB[0:1, 0:L], a=self.ones_cf[:, :],
                     b=s2[:, 0:L]: e.matmul(o, a, b, start=True, stop=True),
                     wait=[("dve", d_s2)], inc="pe")
        yield
        sc.emit("dve", lambda e, o=row(0), i=psA[0:1, 0:L]:
                e.tensor_scalar(o, i, inv, None, ALU.mult),
                wait=[("pe", t1)], inc="dve")
        sc.emit("dve", lambda e, o=row(1), i=psB[0:1, 0:L]:
                e.tensor_scalar(o, i, inv, None, ALU.mult),
                wait=[("pe", t2)], inc="dve")
        sc.emit("dve", lambda e, o=row(2), a=row(0): e.tensor_mul(o, a, a),
                inc="dve")
        d = sc.emit("dve", lambda e, o=row(1), a=row(1), b=row(2):
                    e.tensor_sub(o, a, b), inc="dve")
        a_sd = sc.emit("act", lambda e, o=row(2), i=row(1):
                       e.activation(o, i, AF.Sqrt, bias=self.epsv[0:1, 0:1]),
                       wait=[("dve", d)], inc="act")
        d_a = sc.emit("dve", lambda e, o=row(3), i=row(2):
                      e.reciprocal(o, i), wait=[("act", a_sd)], inc="dve")
        yield
        tA = sc.emit("pe", lambda e, o=psA[:, 0:L], a=self.ones_r[:, :], b=row(3):
                     e.matmul(o, a, b, start=True, stop=True),
                     wait=[("dve", d_a)], inc="pe")
        dA = sc.emit("dve", lambda e, o=s1[:, 0:L], i=psA[:, 0:L]:
                     e.tensor_copy(o, i), wait=[("pe", tA)], inc="dve")
        d_b = sc.emit("dve", lambda e, o=row(2), a=row(0), b=row(3):
                      e.scalar_tensor_tensor(o, a, -1.0, b, ALU.mult, ALU.mult),
                      inc="dve")
        tB = sc.emit("pe", lambda e, o=psB[:, 0:L], a=self.ones_r[:, :], b=row(2):
                     e.matmul(o, a, b, start=True, stop=True),
                     wait=[("dve", d_b)], inc="pe")
        dB = sc.emit("dve", lambda e, o=s2[:, 0:L], i=psB[:, 0:L]:
                     e.tensor_copy(o, i), wait=[("pe", tB)], inc="dve")
        yield
        ap_war = [0, 0]
        p_last = 0
        for c in range(n):
            P = c % 2
            dm = sc.emit("dve", lambda e, o=sqtA[:, P * 512:P * 512 + L],
                         a=xs[c], b=s1[:, 0:L]: e.tensor_mul(o, a, b),
                         wait=[("dve", dA), (ap2_eng, ap_war[P])], inc="dve")
            p_last = sc.emit(ap2_eng, lambda e, o=outs[c],
                             a=sqtA[:, P * 512:P * 512 + L], b=s2[:, 0:L]:
                             e.tensor_add(o, a, b),
                             wait=[("dve", max(dm, dB))], inc=ap2_eng)
            ap_war[P] = p_last
            res["done"] = (ap2_eng, p_last)
            yield

    def emit_ln(self, xs, outs, L, scr):
        res = {}
        for _ in self.emit_ln_gen(xs, outs, L, scr, res):
            pass
        return res["done"][1]

    def load_feature_major(self, src_dram, row0, nrows, strip, psT, dst_xsl,
                           extra_wait=()):
        sc = self.sc
        nstrips = nrows // 128
        strip_war = [0, 0]
        ps_war = [("dve", 0), ("dve", 0)]
        lx = [0, 0]
        for s in range(nstrips):
            par = s % 2
            sl = strip[:, par * D:(par + 1) * D]
            lx[par] = sc.dma(
                "sp", sl,
                src_dram.ap()[row0 + s * 128: row0 + (s + 1) * 128, :],
                wait=[("pe", strip_war[par])], sem=f"q{par}")
            for g in range(4):
                gp = g % 2
                for qv in range(4):
                    c = g * 4 + qv
                    w = []
                    if qv == 0:
                        w = [(f"q{par}", lx[par]), ps_war[gp]]
                        if s == 0 and g == 0:
                            w += list(extra_wait)
                    sc.emit("pe", lambda e, o=psT[gp][:, qv * 128:(qv + 1) * 128],
                            i=sl[:, c * 128:(c + 1) * 128]:
                            e.transpose(o, i, self.idf[:, :]), wait=w,
                            inc="pe" if qv == 3 else None)
                tpe = sc.cnt["pe"]
                strip_war[par] = tpe
                eng = "dve" if (s + g) % 2 == 0 else "act"
                tgt = 0
                for qv2 in range(4):
                    w = [("pe", tpe)] if qv2 == 0 else []
                    tgt = sc.emit(eng, lambda e, o=dst_xsl(g * 4 + qv2, s * 128, 128),
                                  i=psT[gp][:, qv2 * 128:(qv2 + 1) * 128], eng=eng:
                                  (e.tensor_copy(o, i) if eng == "dve"
                                   else e.activation(o, i, AF.Copy)),
                                  wait=w, inc="act" if eng == "act" else "dve")
                ps_war[gp] = (eng, tgt)

    def p0(self):
        sc, nc = self.sc, self.nc
        st = ExitStack()
        strip = st.enter_context(self.sbuf("p0_strip", [128, 2 * D], F32))
        kcol = st.enter_context(self.sbuf("p0_kcol", [128, 2], F32))
        psT = [st.enter_context(self.psum(f"p0_psT{i}", [128, 512], F32))
               for i in range(2)]
        psW = st.enter_context(self.psum("p0_psW", [128, 128], F32))

        l_sl = sc.dma("sp", self.slope_sb[:, :], self.slope.ap())
        for dst, src in ((self.nw, self.nw_c), (self.dv, self.dv_c),
                         (self.gsc, self.gsc_c), (self.usc, self.usc_c),
                         (self.dsc, self.dsc_c), (self.idf, self.ident)):
            sc.dma("sp", dst[:, :], src.ap())
        sc.dma("pool", self.idb[:, :], self.ident.ap())
        p_ms = 0
        for buf, val in ((self.ones_r, 1.0), (self.ones_cf, 1.0),
                         (self.ones_cb, 1.0), (self.m128_r, -128.0)):
            p_ms = sc.emit("pool", lambda e, b=buf, v=val: e.memset(b[:, :], v),
                           inc="pool")
        p_ms = sc.emit("pool", lambda e: e.memset(self.kv[:, :], 0.0), inc="pool")
        p_ms = sc.emit("pool", lambda e: e.memset(self.epsv[0:1, 0:1], LN_EPS),
                       inc="pool")
        p_ms = sc.emit("pool", lambda e: e.memset(self.epsv[0:1, 1:2], RMS_EPS),
                       inc="pool")
        t_sf = sc.emit("pe", lambda e: e.matmul(psW[:, 0:H], self.ones_r[:, :],
                                                self.slope_sb[:, :],
                                                start=True, stop=True),
                       wait=[("lx", sc.cnt["lx"]), ("pool", p_ms)], inc="pe")
        sc.emit("dve", lambda e: e.tensor_copy(self.SF[:, :], psW[:, 0:H]),
                wait=[("pe", t_sf)], inc="dve")
        d_sf = sc.emit("dve", lambda e: e.tensor_scalar(
            self.SFn[:, :], self.SF[:, :], -1.0, None, ALU.mult), inc="dve")
        p_io = sc.emit("pool", lambda e: e.iota(
            kcol[:, 0:1], [[0, 1]], base=-255, channel_multiplier=1,
            allow_small_or_imprecise_dtypes=True), inc="pool")
        p_io = sc.emit("pool", lambda e: e.iota(
            kcol[:, 1:2], [[0, 1]], base=-127, channel_multiplier=1,
            allow_small_or_imprecise_dtypes=True), inc="pool")
        for h in range(H):
            for c in range(2):
                sc.emit("act", lambda e, h=h, c=c:
                        e.activation(self.KD[:, h * 2 + c:h * 2 + c + 1],
                                     kcol[:, c:c + 1], AF.Exp,
                                     scale=self.SF[:, h:h + 1]),
                        wait=[("dve", d_sf), ("pool", p_io)], inc="act")
            sc.emit("act", lambda e, h=h:
                    e.activation(self.BD[:, h:h + 1], self.SF[:, h:h + 1],
                                 AF.Exp, scale=-256.0), inc="act")
        self.load_feature_major(self.xo, 0, TOK, strip, psT,
                                lambda c, tt, L: self.xsl(c, tt, L),
                                extra_wait=[("lx", sc.cnt["lx"])])
        self.maybe_conv(12)
        sc.waitonly("sp", [("lx", sc.cnt["lx"])])
        sc.waitonly("pool", [("ld", sc.cnt["ld"])])
        sc.flush()
        st.close()

    def p1(self):
        sc, nc = self.sc, self.nc
        for hp in range(2):
            st = ExitStack()
            xpf = st.enter_context(self.sbuf("p1_xpf", [128, NF * HALF], F32))
            lnp = st.enter_context(self.sbuf("p1_lnp", [128, NF * HALF], BF16))
            s1 = st.enter_context(self.sbuf("p1_s1", [128, 512], F32))
            s2 = st.enter_context(self.sbuf("p1_s2", [128, 512], F32))
            sqtA = st.enter_context(self.sbuf("p1_sqtA", [128, 1024], F32))
            rowb = st.enter_context(self.sbuf("p1_rowb", [1, 2048], F32))
            khat = st.enter_context(self.sbuf("p1_khat", [128, 2 * 256], BF16))
            psA = st.enter_context(self.psum("p1_psA", [128, 512], F32))
            psB = st.enter_context(self.psum("p1_psB", [128, 512], F32))
            psKV = [st.enter_context(self.psum(f"p1_psKV{i}", [128, 128], F32))
                    for i in range(2)]

            stA = ExitStack()
            strip = stA.enter_context(self.sbuf("p1_strip", [128, 2 * D], F32))
            psT = [stA.enter_context(self.psum(f"p1_psT{i}", [128, 512], F32))
                   for i in range(2)]
            self.load_feature_major(
                self.xp, hp * HALF, HALF, strip, psT,
                lambda c, tt, L: xpf[:, c * HALF + tt: c * HALF + tt + L])
            sc.flush()
            stA.close()

            stB = ExitStack()
            kvT = stB.enter_context(self.sbuf("p1_kvT", [128, 4 * 4096], BF16))
            wst = stB.enter_context(self.sbuf("p1_wst", [128, 2 * NF * 256], BF16))

            xs = [xpf[:, c * HALF:(c + 1) * HALF] for c in range(NF)]
            outs = [lnp[:, c * HALF:(c + 1) * HALF] for c in range(NF)]
            ln_done = self.emit_ln(xs, outs, HALF, (s1, s2, sqtA, rowb, psA, psB))

            ps = [psA, psB]
            evac_done = [0, 0]
            blk_last = [0, 0]
            for ofb in range(16):
                par = ofb % 2
                wsl = wst[:, par * NF * 256:(par + 1) * NF * 256]
                jb = ofb // 2
                cw = self.lc_hist[jb] if jb < len(self.lc_hist) else 0
                l_w = sc.dma(
                    "pool", wsl.rearrange("p (c f) -> p c f", c=NF),
                    self.wqkvB.ap()[8 + ofb],
                    wait=[("pe", blk_last[par]), ("lc", cw)], sem=f"q{2 + par}")
                self.maybe_conv()
                for mc in range(4):
                    u = (ofb * 4 + mc) % 2
                    for kc in range(NF):
                        w = []
                        if kc == 0:
                            w = [(f"q{2 + par}", l_w), ("act", evac_done[u]),
                                 ("pool", ln_done)]
                        sc.emit("pe", lambda e, o=ps[u][:, 0:256],
                                a=lnp[:, kc * HALF + mc * 128:
                                      kc * HALF + mc * 128 + 128],
                                b=wsl[:, kc * 256:(kc + 1) * 256],
                                s=(kc == 0), t=(kc == NF - 1):
                                e.matmul(o, a, b, start=s, stop=t),
                                wait=w, inc="pe" if kc == NF - 1 else None)
                    tmm = sc.cnt["pe"]
                    blk_last[par] = tmm
                    evac_done[u] = sc.emit(
                        "act", lambda e,
                        o=kvT[:, mc * 4096 + ofb * 256: mc * 4096 + (ofb + 1) * 256],
                        i=ps[u][:, 0:256]: e.activation(o, i, AF.Silu),
                        wait=[("pe", tmm)], inc="act")

            khat_war = [0, 0]
            pskv_free = [0, 0]
            for j in range(2):
                for h in range(H):
                    P = h % 2
                    dv = 0
                    for c in range(2):
                        w = []
                        if c == 0:
                            w = [("pe", khat_war[P]), ("act", evac_done[0]),
                                 ("act", evac_done[1])]
                        dv = sc.emit(
                            "dve", lambda e,
                            o=khat[:, P * 256 + c * 128: P * 256 + c * 128 + 128],
                            i=kvT[:, (j * 2 + c) * 4096 + h * 128:
                                  (j * 2 + c) * 4096 + h * 128 + 128],
                            s=self.KD[:, h * 2 + c:h * 2 + c + 1]:
                            e.tensor_scalar(o, i, s, None, ALU.mult),
                            wait=w, inc="dve")
                    for c in range(2):
                        w = []
                        if c == 0:
                            w = [("dve", dv), ("dve", pskv_free[P])]
                        sc.emit("pe", lambda e, o=psKV[P][:, :],
                                a=khat[:, P * 256 + c * 128: P * 256 + c * 128 + 128],
                                b=kvT[:, (j * 2 + c) * 4096 + 2048 + h * 128:
                                      (j * 2 + c) * 4096 + 2048 + h * 128 + 128],
                                s=(c == 0), t=(c == 1):
                                e.matmul(o, a, b, start=s, stop=t),
                                wait=w, inc="pe" if c == 1 else None)
                    th = sc.cnt["pe"]
                    khat_war[P] = th
                    di = sc.emit("dve", lambda e, o=self.kv[:, h * DH:(h + 1) * DH],
                                 s=self.BD[:, h:h + 1], i=psKV[P][:, :]:
                                 e.scalar_tensor_tensor(o, o, s, i, ALU.mult, ALU.add),
                                 wait=[("pe", th), ("dve", self.kv_stt[h])],
                                 inc="dve")
                    self.kv_stt[h] = di
                    pskv_free[P] = di
            sc.flush()
            stB.close()
            st.close()

    def p2(self):
        sc, nc = self.sc, self.nc
        st = ExitStack()
        qb = st.enter_context(self.sbuf("p2_q", [128, NF * HALF], BF16))
        kb = st.enter_context(self.sbuf("p2_k", [128, NF * HALF], BF16))
        vT = st.enter_context(self.sbuf("p2_vT", [128, 4 * D], BF16))
        gate = st.enter_context(self.sbuf("p2_gate", [128, NF * HALF], BF16))
        ao = st.enter_context(self.sbuf("p2_ao", [128, H * BLK], BF16))
        s1 = st.enter_context(self.sbuf("p2_s1", [128, 512], F32))
        s2 = st.enter_context(self.sbuf("p2_s2", [128, 512], F32))
        sqtA = st.enter_context(self.sbuf("p2_sqtA", [128, 1024], F32))
        rowb = st.enter_context(self.sbuf("p2_rowb", [1, 2048], F32))
        khat = st.enter_context(self.sbuf("p2_khat", [128, 2 * 256], BF16))
        kvb = st.enter_context(self.sbuf("p2_kvb", [128, 2 * 128], BF16))
        sS = st.enter_context(self.sbuf("p2_sS", [128, 2 * 512], BF16))
        qh = st.enter_context(self.sbuf("p2_qh", [128, 2 * 256], BF16))
        rgt = st.enter_context(self.sbuf("p2_rgt", [128, 256], BF16))
        ln1 = st.enter_context(self.sbuf("p2_ln1", [128, NF * HALF], BF16))
        psS = [st.enter_context(self.psum(f"p2_psS{i}", [128, 512], F32))
               for i in range(2)]
        psS_free = [("dve", 0), ("dve", 0)]
        blk_last = [0, 0]

        lc_attn = (self.lc_hist[self.n_attn_conv - 1]
                   if len(self.lc_hist) >= self.n_attn_conv else 0)

        def std_block(wdram, col0, ofb, dst, act_fn, wst, ln_done, evac_war=None):
            par = ofb % 2
            wsl = wst[:, par * NF * 256:(par + 1) * NF * 256]
            l_w = sc.dma(
                "pool", wsl.rearrange("p (c f) -> p c f", c=NF),
                wdram.ap()[col0 // 256 + ofb],
                wait=[("pe", blk_last[par]), ("lc", lc_attn)], sem=f"q{2 + par}")
            self.maybe_conv(2)
            for ofh in range(2):
                u = (ofb * 2 + ofh) % 2
                for kc in range(NF):
                    w = []
                    if kc == 0:
                        w = [(f"q{2 + par}", l_w), psS_free[u], ln_done]
                    sc.emit("pe", lambda e, o=psS[u][:, 0:HALF],
                            a=wsl[:, kc * 256 + ofh * 128:
                                  kc * 256 + ofh * 128 + 128],
                            b=ln1[:, kc * HALF:(kc + 1) * HALF],
                            s=(kc == 0), t=(kc == NF - 1):
                            e.matmul(o, a, b, start=s, stop=t),
                            wait=w, inc="pe" if kc == NF - 1 else None)
                tmm = sc.cnt["pe"]
                blk_last[par] = tmm
                oc = ofb * 2 + ofh
                w2 = [("pe", tmm)]
                if evac_war is not None:
                    w2.append(evac_war)
                a_ev = sc.emit("act",
                               lambda e, o=dst[:, oc * HALF:(oc + 1) * HALF],
                               i=psS[u][:, 0:HALF], f=act_fn:
                               e.activation(o, i, f),
                               wait=w2, inc="act")
                psS_free[u] = ("act", a_ev)

        def v_block(ofb, wst, ln_done):
            par = ofb % 2
            wsl = wst[:, par * NF * 256:(par + 1) * NF * 256]
            l_w = sc.dma(
                "pool", wsl.rearrange("p (c f) -> p c f", c=NF),
                self.wqkvB.ap()[16 + ofb],
                wait=[("pe", blk_last[par]), ("lc", lc_attn)], sem=f"q{2 + par}")
            self.maybe_conv()
            for mc in range(4):
                u = mc % 2
                for kc in range(NF):
                    w = []
                    if kc == 0:
                        w = [(f"q{2 + par}", l_w), psS_free[u], ln_done]
                    sc.emit("pe", lambda e, o=psS[u][:, 0:256],
                            a=ln1[:, kc * HALF + mc * 128:
                                  kc * HALF + mc * 128 + 128],
                            b=wsl[:, kc * 256:(kc + 1) * 256],
                            s=(kc == 0), t=(kc == NF - 1):
                            e.matmul(o, a, b, start=s, stop=t),
                            wait=w, inc="pe" if kc == NF - 1 else None)
                tmm = sc.cnt["pe"]
                blk_last[par] = tmm
                a_ev = sc.emit("act", lambda e,
                               o=vT[:, mc * D + ofb * 256: mc * D + (ofb + 1) * 256],
                               i=psS[u][:, 0:256]: e.activation(o, i, AF.Silu),
                               wait=[("pe", tmm)], inc="act")
                psS_free[u] = ("act", a_ev)

        def emit_decay(stX):
            QD = stX.enter_context(self.sbuf("p2_QD", [128, H * 256], BF16))
            DDt = stX.enter_context(self.sbuf("p2_DD", [128, 2 * H * 256], BF16))
            it = sqtA
            p_io = sc.emit("pool", lambda e: e.iota(
                it[:, 0:256], [[1, 256]], base=1, channel_multiplier=0,
                allow_small_or_imprecise_dtypes=True), inc="pool")
            a_qd = 0
            for h in range(H):
                a_qd = sc.emit("act", lambda e, h=h:
                               e.activation(QD[:, h * 256:(h + 1) * 256],
                                            it[:, 0:256], AF.Exp,
                                            scale=self.SFn[:, h:h + 1]),
                               wait=[("pool", p_io)], inc="act")
            a_dd = 0
            for c in range(2):
                p_io = sc.emit("pool", lambda e, c=c: e.iota(
                    it[:, 0:256], [[-1, 256]], base=128 * c, channel_multiplier=1,
                    allow_small_or_imprecise_dtypes=True),
                    wait=[("act", a_qd if c == 0 else a_dd)], inc="pool")
                for h in range(H):
                    a_dd = sc.emit("act", lambda e, h=h, c=c:
                                   e.activation(
                                       DDt[:, (h * 2 + c) * 256:(h * 2 + c + 1) * 256],
                                       it[:, 0:256], AF.Exp,
                                       scale=self.SF[:, h:h + 1]),
                                   wait=[("pool", p_io)], inc="act")
            p_dd = 0
            for h in range(H):
                for c in range(2):
                    p_dd = sc.emit("pool", lambda e, h=h, c=c:
                                   e.affine_select(
                                       DDt[:, (h * 2 + c) * 256:(h * 2 + c + 1) * 256],
                                       DDt[:, (h * 2 + c) * 256:(h * 2 + c + 1) * 256],
                                       [[1, 256]], ALU.is_ge, 0.0,
                                       base=-(128 * c), channel_multiplier=-1),
                                   wait=[("act", a_dd)], inc="pool")
            return QD, DDt, a_qd, p_dd

        def emit_scan(p, stX):
            QD, DDt, a_qd, p_dd = emit_decay(stX)
            psTr = [stX.enter_context(self.psum(f"p2_psTr{i}", [128, 256], BF16))
                    for i in range(2)]
            psO = [stX.enter_context(self.psum(f"p2_psO{i}", [128, 256], F32))
                   for i in range(2)]
            psKV = [stX.enter_context(self.psum(f"p2_psKV{i}", [128, 128], F32))
                    for i in range(2)]
            b_done = [0, 0]
            kvmm_done = [0, 0]
            g_done = [0, 0]
            j_done = [0, 0]
            i_done = [0, 0]
            rg_done = 0
            for jj in range(2):
                tc = jj * 256
                for h in range(H):
                    if h % 2 == 0:
                        self.maybe_conv()
                    P = h % 2
                    for c in range(2):
                        w = [("dve", b_done[P])] if c == 0 else []
                        sc.emit("pe", lambda e, o=psTr[P][:, c * 128:(c + 1) * 128],
                                i=kb[:, h * HALF + tc + c * 128:
                                     h * HALF + tc + c * 128 + 128]:
                                e.transpose(o, i, self.idb[:, :]), wait=w,
                                inc="pe" if c == 1 else None)
                    ta = sc.cnt["pe"]
                    for c in range(2):
                        w = [("pe", ta), ("pe", kvmm_done[P])] if c == 0 else []
                        b_done[P] = sc.emit(
                            "dve", lambda e,
                            o=khat[:, P * 256 + c * 128: P * 256 + c * 128 + 128],
                            i=psTr[P][:, c * 128:(c + 1) * 128],
                            s=self.KD[:, h * 2 + c:h * 2 + c + 1]:
                            e.tensor_scalar(o, i, s, None, ALU.mult),
                            wait=w, inc="dve")
                    for c in range(2):
                        w = [psS_free[P]] if c == 0 else []
                        sc.emit("pe", lambda e, o=psS[P][:, c * 256:(c + 1) * 256],
                                a=kb[:, h * HALF + tc + c * 128:
                                     h * HALF + tc + c * 128 + 128],
                                b=qb[:, h * HALF + tc: h * HALF + tc + 256]:
                                e.matmul(o, a, b, start=True, stop=True),
                                wait=w, inc="pe" if c == 1 else None)
                    tcc = sc.cnt["pe"]
                    dd = 0
                    for c in range(2):
                        w = []
                        if c == 0:
                            w = [("pe", tcc), ("pe", g_done[P]), ("pool", p_dd)]
                        dd = sc.emit("dve", lambda e,
                                     o=sS[:, P * 512 + c * 256: P * 512 + c * 256 + 256],
                                     a=psS[P][:, c * 256:(c + 1) * 256],
                                     b=DDt[:, (h * 2 + c) * 256:(h * 2 + c + 1) * 256]:
                                     e.tensor_mul(o, a, b), wait=w, inc="dve")
                    psS_free[P] = ("dve", dd)
                    ve = sc.emit("pool", lambda e, o=qh[:, P * 256:P * 256 + 256],
                                 a=qb[:, h * HALF + tc: h * HALF + tc + 256],
                                 b=QD[:, h * 256:(h + 1) * 256]:
                                 e.tensor_mul(o, a, b),
                                 wait=[("act", a_qd), ("pe", g_done[P])],
                                 inc="pool")
                    vf = sc.emit("act", lambda e, o=kvb[:, P * 128:P * 128 + 128],
                                 i=self.kv[:, h * DH:(h + 1) * DH]:
                                 e.activation(o, i, AF.Copy),
                                 wait=[("dve", self.kv_stt[h]), ("pe", g_done[P])],
                                 inc="act")
                    sc.emit("pe", lambda e, o=psO[P][:, :],
                            a=kvb[:, P * 128:P * 128 + 128],
                            b=qh[:, P * 256:P * 256 + 256]:
                            e.matmul(o, a, b, start=True, stop=False),
                            wait=[("act", vf), ("act", j_done[P]),
                                  ("pool", ve), ("dve", dd)])
                    for c in range(2):
                        sc.emit("pe", lambda e, o=psO[P][:, :],
                                a=vT[:, (jj * 2 + c) * D + h * 128:
                                     (jj * 2 + c) * D + h * 128 + 128],
                                b=sS[:, P * 512 + c * 256: P * 512 + c * 256 + 256],
                                t=(c == 1):
                                e.matmul(o, a, b, start=False, stop=t),
                                inc="pe" if c == 1 else None)
                    tg = sc.cnt["pe"]
                    g_done[P] = tg
                    for c in range(2):
                        w = []
                        if c == 0:
                            w = [("dve", b_done[P]), ("dve", i_done[P])]
                        sc.emit("pe", lambda e, o=psKV[P][:, :],
                                a=khat[:, P * 256 + c * 128: P * 256 + c * 128 + 128],
                                b=vT[:, (jj * 2 + c) * D + h * 128:
                                     (jj * 2 + c) * D + h * 128 + 128],
                                s=(c == 0), t=(c == 1):
                                e.matmul(o, a, b, start=s, stop=t),
                                wait=w, inc="pe" if c == 1 else None)
                    th = sc.cnt["pe"]
                    kvmm_done[P] = th
                    di = sc.emit("dve", lambda e, o=self.kv[:, h * DH:(h + 1) * DH],
                                 s=self.BD[:, h:h + 1], i=psKV[P][:, :]:
                                 e.scalar_tensor_tensor(o, o, s, i, ALU.mult, ALU.add),
                                 wait=[("pe", th), ("act", vf)], inc="dve")
                    self.kv_stt[h] = di
                    i_done[P] = di
                    j_done[P] = sc.emit("act",
                                        lambda e, o=ao[:, h * 256:(h + 1) * 256],
                                        i=psO[P][:, :]: e.activation(o, i, AF.Copy),
                                        wait=[("pe", tg)], inc="act")

                aos = [ao[:, c * 256:(c + 1) * 256] for c in range(NF)]
                sc.emit("act", lambda e, o=s2[:, 0:256], i=aos[0]:
                        e.activation(o, i, AF.Square),
                        wait=[("act", max(j_done))], inc="act")
                d_s2 = 0
                sq_war = [0, 0]
                for c in range(1, NF):
                    P2 = c % 2
                    a = sc.emit("act", lambda e, o=sqtA[:, P2 * 512:P2 * 512 + 256],
                                i=aos[c]: e.activation(o, i, AF.Square),
                                wait=[("dve", sq_war[P2])], inc="act")
                    d_s2 = sc.emit("dve", lambda e, o=s2[:, 0:256],
                                   i=sqtA[:, P2 * 512:P2 * 512 + 256]:
                                   e.tensor_add(o, o, i),
                                   wait=[("act", a)], inc="dve")
                    sq_war[P2] = d_s2
                t1 = sc.emit("pe", lambda e, o=psS[0][0:1, 0:256],
                             a=self.ones_cf[:, :], b=s2[:, 0:256]:
                             e.matmul(o, a, b, start=True, stop=True),
                             wait=[("dve", d_s2), psS_free[0]], inc="pe")
                d = sc.emit("dve", lambda e, o=rowb[0:1, 0:256],
                            i=psS[0][0:1, 0:256]:
                            e.tensor_scalar(o, i, 1.0 / 2048.0, None, ALU.mult),
                            wait=[("pe", t1)], inc="dve")
                a_sd = sc.emit("act", lambda e, o=rowb[0:1, 512:768],
                               i=rowb[0:1, 0:256]:
                               e.activation(o, i, AF.Sqrt, bias=self.epsv[0:1, 1:2]),
                               wait=[("dve", d)], inc="act")
                d_rs = sc.emit("dve", lambda e, o=rowb[0:1, 1024:1280],
                               i=rowb[0:1, 512:768]: e.reciprocal(o, i),
                               wait=[("act", a_sd)], inc="dve")
                tB = sc.emit("pe", lambda e, o=psS[1][:, 0:256], a=self.ones_r[:, :],
                             b=rowb[0:1, 1024:1280]:
                             e.matmul(o, a, b, start=True, stop=True),
                             wait=[("dve", d_rs), psS_free[1]], inc="pe")
                d_bc = sc.emit("dve", lambda e, o=s1[:, 0:256], i=psS[1][:, 0:256]:
                               e.tensor_copy(o, i), wait=[("pe", tB)], inc="dve")
                psS_free[0] = ("dve", d)
                psS_free[1] = ("dve", d_bc)
                rg_war = [0, 0]
                for c in range(NF):
                    P2 = c % 2
                    dm = sc.emit("dve", lambda e, o=rgt[:, 0:256],
                                 a=aos[c], s=self.nw[:, c:c + 1], b=s1[:, 0:256]:
                                 e.scalar_tensor_tensor(o, a, s, b, ALU.mult, ALU.mult),
                                 wait=[("dve", d_bc), ("pool", rg_war[0])],
                                 inc="dve")
                    pm = sc.emit("pool", lambda e,
                                 o=gate[:, c * HALF + tc: c * HALF + tc + 256],
                                 a=rgt[:, 0:256]:
                                 e.tensor_mul(o, a, o),
                                 wait=[("dve", dm)], inc="pool")
                    rg_war[0] = pm
                rg_done = sc.cnt["pool"]
            return rg_done

        def emit_wout(p, rg_done, wst, gen=None):
            t0 = p * HALF
            wout_last = 0
            for ofb in range(8):
                par = ofb % 2
                wsl = wst[:, par * NF * 256:(par + 1) * NF * 256]
                l_w = sc.dma(
                    "pool", wsl.rearrange("p (c f) -> p c f", c=NF),
                    self.woutB.ap()[ofb],
                    wait=[("pe", blk_last[par]), ("lc", lc_attn)], sem=f"q{2 + par}")
                self.maybe_conv(2)
                for ofh in range(2):
                    u = (ofb * 2 + ofh) % 2
                    for kc in range(NF):
                        w = []
                        if kc == 0:
                            w = [(f"q{2 + par}", l_w), psS_free[u],
                                 ("pool", rg_done)]
                        sc.emit("pe", lambda e, o=psS[u][:, 0:HALF],
                                a=wsl[:, kc * 256 + ofh * 128:
                                      kc * 256 + ofh * 128 + 128],
                                b=gate[:, kc * HALF:(kc + 1) * HALF],
                                s=(kc == 0), t=(kc == NF - 1):
                                e.matmul(o, a, b, start=s, stop=t),
                                wait=w, inc="pe" if kc == NF - 1 else None)
                    tmm = sc.cnt["pe"]
                    blk_last[par] = tmm
                    wout_last = tmm
                    oc = ofb * 2 + ofh
                    d_x = sc.emit("dve", lambda e, o=self.xsl(oc, t0, HALF),
                                  i=psS[u][:, 0:HALF]: e.tensor_add(o, o, i),
                                  wait=[("pe", tmm)], inc="dve")
                    psS_free[u] = ("dve", d_x)
                    if gen is not None:
                        for _ in range(3):
                            if next(gen, "END") == "END":
                                gen = None
                                break
            if gen is not None:
                for _ in gen:
                    pass
            return wout_last

        stA = ExitStack()
        wstA = stA.enter_context(self.sbuf("p2_wstA", [128, 2 * NF * 256], BF16))
        xs0 = [self.xsl(c, 0, HALF) for c in range(NF)]
        l1s = [ln1[:, c * HALF:(c + 1) * HALF] for c in range(NF)]
        ln_done = ("pool", self.emit_ln(xs0, l1s, HALF,
                                        (s1, s2, sqtA, rowb, psS[0], psS[1])))
        for ofb in range(8):
            std_block(self.wgateB, 0, ofb, gate, AF.Sigmoid, wstA, ln_done)
        for ofb in range(8):
            std_block(self.wqkvB, 0, ofb, qb, AF.Silu, wstA, ln_done)
        for ofb in range(8):
            std_block(self.wqkvB, 2048, ofb, kb, AF.Silu, wstA, ln_done)
        for ofb in range(8):
            v_block(ofb, wstA, ln_done)
        sc.flush()
        stA.close()

        stB = ExitStack()
        rg_done = emit_scan(0, stB)
        sc.flush()
        stB.close()

        stC = ExitStack()
        wstC = stC.enter_context(self.sbuf("p2_wstC", [128, 2 * NF * 256], BF16))
        psA2 = stC.enter_context(self.psum("p2_psA2", [128, 512], F32))
        psB2 = stC.enter_context(self.psum("p2_psB2", [128, 512], F32))
        res1 = {}
        gen = self.emit_ln_gen([self.xsl(c, HALF, HALF) for c in range(NF)],
                               l1s, HALF, (s1, s2, sqtA, rowb, psA2, psB2),
                               res1, sum_eng="dve", ap2_eng="dve")
        wout_last = emit_wout(0, rg_done, wstC, gen=gen)
        ln_done = res1["done"]
        for ofb in range(8):
            std_block(self.wqkvB, 0, ofb, qb, AF.Silu, wstC, ln_done)
        for ofb in range(8):
            std_block(self.wqkvB, 2048, ofb, kb, AF.Silu, wstC, ln_done)
        for ofb in range(8):
            v_block(ofb, wstC, ln_done)
        for ofb in range(8):
            std_block(self.wgateB, 0, ofb, gate, AF.Sigmoid, wstC, ln_done,
                      evac_war=("pe", wout_last))
        sc.flush()
        stC.close()

        stD = ExitStack()
        rg_done = emit_scan(1, stD)
        sc.flush()
        stD.close()

        stE = ExitStack()
        wstE = stE.enter_context(self.sbuf("p2_wstE", [128, 2 * NF * 256], BF16))
        emit_wout(1, rg_done, wstE)
        sc.flush()
        stE.close()
        st.close()

    def ffn(self, tag):
        sc, nc = self.sc, self.nc
        st = ExitStack()
        ln2 = st.enter_context(self.sbuf(f"{tag}_ln2", [128, 2 * NF * HALF], BF16))
        hfull = st.enter_context(self.sbuf(f"{tag}_h", [128, NI * HALF], BF16))
        rowb = st.enter_context(self.sbuf(f"{tag}_rowb", [1, 2048], F32))
        shilo = st.enter_context(self.sbuf(f"{tag}_shl", [1, 1024], BF16))
        s1 = st.enter_context(self.sbuf(f"{tag}_s1", [128, 512], F32))
        s2 = st.enter_context(self.sbuf(f"{tag}_s2", [128, 512], F32))
        sqtA = st.enter_context(self.sbuf(f"{tag}_sqtA", [128, 1024], F32))
        tmpG = st.enter_context(self.sbuf(f"{tag}_tG", [128, 512], F32))
        tmpU = st.enter_context(self.sbuf(f"{tag}_tU", [128, 512], F32))
        l2 = lambda t, c: ln2[:, (t * NF + c) * HALF:(t * NF + c + 1) * HALF]
        srow_f = s1[0:1, 0:HALF]
        srow_t = s2[0:1, 0:HALF]

        with ExitStack() as stl:
            psA = stl.enter_context(self.psum(f"{tag}_psA", [128, 512], F32))
            psB = stl.enter_context(self.psum(f"{tag}_psB", [128, 512], F32))
            xs0 = [self.xsl(c, 0, HALF) for c in range(NF)]
            res0 = {}
            for _ in self.emit_ln_gen(xs0, [l2(0, c) for c in range(NF)], HALF,
                                      (s1, s2, sqtA, rowb, psA, psB), res0,
                                      sum_eng="dve", ap2_eng="dve"):
                pass
            ln_done = [res0["done"], None]
            sc.flush()

        for th in range(2):
            t0 = th * HALF
            stG = ExitStack()
            psG = [stG.enter_context(self.psum(f"{tag}_psG{i}", [128, 512], F32))
                   for i in range(2)]
            psU = [stG.enter_context(self.psum(f"{tag}_psU{i}", [128, 512], F32))
                   for i in range(2)]
            psR = stG.enter_context(self.psum(f"{tag}_psR", [128, 512], F32))
            wg = stG.enter_context(self.sbuf(f"{tag}_wg", [128, 2 * NF * 128], BF16))
            wu = stG.enter_context(self.sbuf(f"{tag}_wu", [128, 2 * NF * 128], BF16))
            lngen = None
            resB = {}
            if th == 0:
                psA2 = stG.enter_context(self.psum(f"{tag}_psA2", [128, 512], F32))
                psB2 = stG.enter_context(self.psum(f"{tag}_psB2", [128, 512], F32))
                xs1 = [self.xsl(c, HALF, HALF) for c in range(NF)]
                lngen = self.emit_ln_gen(xs1, [l2(1, c) for c in range(NF)],
                                         HALF, (s1, s2, sqtA, rowb, psA2, psB2),
                                         resB, sum_eng="dve", ap2_eng="dve")
            if self.conv_done:
                sc.waitonly("pool", [("lc", self.conv_done)])
                self.conv_done = 0
            hmul_last = 0
            gevac = [0, 0]
            uevac = [0, 0]
            blk_last = [0, 0]
            tsr = 0
            pend_sr = []

            def emit_sr(ic, dh):
                nonlocal tsr
                tsr = sc.emit("pe", lambda e, o=psR[0:1, 0:HALF],
                              a=self.ones_cb[:, :],
                              b=hfull[:, ic * HALF:(ic + 1) * HALF],
                              s=(ic == 0), t=(ic == NI - 1):
                              e.matmul(o, a, b, start=s, stop=t),
                              wait=[("dve", dh)], inc="pe")
            for ic in range(NI):
                P = ic % 2
                wgs = wg[:, P * NF * 128:(P + 1) * NF * 128]
                wus = wu[:, P * NF * 128:(P + 1) * NF * 128]
                lg = sc.dma("pool", wgs.rearrange("p (c f) -> p c f", c=NF),
                            self.u8g.ap()[ic], wait=[("pe", blk_last[P])],
                            sem=f"q{4 + P}")
                lu = sc.dma("pool", wus.rearrange("p (c f) -> p c f", c=NF),
                            self.u8u.ap()[ic], wait=[("pe", blk_last[P])],
                            sem=f"q{6 + P}")
                for kc in range(NF):
                    w = []
                    if kc == 0:
                        w = [(f"q{4 + P}", lg), (f"q{6 + P}", lu),
                             ("act", gevac[P]), ln_done[th]]
                    sc.emit("pe", lambda e, o=psG[P][:, 0:HALF],
                            a=wgs[:, kc * 128:(kc + 1) * 128],
                            b=l2(th, kc),
                            s=(kc == 0), t=(kc == NF - 1):
                            e.matmul(o, a, b, start=s, stop=t),
                            wait=w, inc="pe" if kc == NF - 1 else None)
                tgG = sc.cnt["pe"]
                for kc in range(NF):
                    w = [("act", uevac[P])] if kc == 0 else []
                    sc.emit("pe", lambda e, o=psU[P][:, 0:HALF],
                            a=wus[:, kc * 128:(kc + 1) * 128],
                            b=l2(th, kc),
                            s=(kc == 0), t=(kc == NF - 1):
                            e.matmul(o, a, b, start=s, stop=t),
                            wait=w, inc="pe" if kc == NF - 1 else None)
                tgU = sc.cnt["pe"]
                blk_last[P] = tgU
                ag = sc.emit("act", lambda e, o=tmpG[:, :],
                             i=psG[P][:, 0:HALF], s=self.gsc[:, ic:ic + 1]:
                             e.activation(o, i, AF.Silu, scale=s),
                             wait=[("pe", tgG), ("dve", hmul_last)], inc="act")
                gevac[P] = ag
                au = sc.emit("act", lambda e, o=tmpU[:, :],
                             i=psU[P][:, 0:HALF], s=self.usc[:, ic:ic + 1]:
                             e.activation(o, i, AF.Copy, scale=s),
                             wait=[("pe", tgU), ("dve", hmul_last)], inc="act")
                uevac[P] = au
                dh = sc.emit("dve", lambda e, o=hfull[:, ic * HALF:(ic + 1) * HALF],
                             a=tmpG[:, :], b=tmpU[:, :]:
                             e.tensor_mul(o, a, b), wait=[("act", au)], inc="dve")
                hmul_last = dh
                pend_sr.append((ic, dh))
                if len(pend_sr) > 2:
                    emit_sr(*pend_sr.pop(0))
                if lngen is not None:
                    for _ in range(2):
                        if next(lngen, "END") == "END":
                            lngen = None
                            break
            if lngen is not None:
                for _ in lngen:
                    pass
            if th == 0:
                ln_done[1] = resB["done"]
            for args in pend_sr:
                emit_sr(*args)
            d_sr = sc.emit("dve", lambda e, o=srow_f, i=psR[0:1, 0:HALF]:
                           e.tensor_copy(o, i), wait=[("pe", tsr)], inc="dve")
            a_hi = sc.emit("act", lambda e, o=shilo[0:1, 0:HALF], i=srow_f:
                           e.activation(o, i, AF.Copy),
                           wait=[("dve", d_sr)], inc="act")
            d_lo = sc.emit("dve", lambda e, o=srow_t, a=srow_f,
                           b=shilo[0:1, 0:HALF]:
                           e.tensor_sub(o, a, b), wait=[("act", a_hi)], inc="dve")
            a_lo = sc.emit("act", lambda e, o=shilo[0:1, 512:512 + HALF],
                           i=srow_t: e.activation(o, i, AF.Copy),
                           wait=[("dve", d_lo)], inc="act")
            h_done = hmul_last
            sc.flush()
            stG.close()

            stD = ExitStack()
            psD = [stD.enter_context(self.psum(f"{tag}_psD{i}", [128, 512], F32))
                   for i in range(8)]
            wd = stD.enter_context(self.sbuf(f"{tag}_wd", [128, 2 * 4096], BF16))
            psD_free = [0] * 8
            xadd = [0, 0]
            wd_last = [0, 0]
            tmpE = [tmpG, tmpU]
            for dhh in range(2):
                for i4 in range(16):
                    par = i4 % 2
                    wds = wd[:, par * 4096:(par + 1) * 4096]
                    lw = sc.dma(
                        "pool", wds.rearrange("p (c f) -> p c f", c=4),
                        self.u8d.ap()[i4, dhh],
                        wait=[("pe", wd_last[par])], sem=f"q{4 + par}")
                    for c in range(4):
                        ic = i4 * 4 + c
                        for dc in range(8):
                            w = []
                            if dc == 0 and c == 0:
                                w.append((f"q{4 + par}", lw))
                                if dhh == 0 and i4 == 0:
                                    w.append(("dve", h_done))
                            if ic == 0:
                                w.append(("dve", psD_free[dc]))
                            sc.emit("pe", lambda e, o=psD[dc][:, :],
                                    a=wds[:, c * 1024 + dc * 128:
                                          c * 1024 + dc * 128 + 128],
                                    b=hfull[:, ic * HALF:(ic + 1) * HALF],
                                    s=(ic == 0):
                                    e.matmul(o, a, b, start=s, stop=False),
                                    wait=w, inc="pe" if dc == 7 else None)
                        wd_last[par] = sc.cnt["pe"]
                for dc in range(8):
                    w = [("act", a_lo)] if dc == 0 else []
                    sc.emit("pe", lambda e, o=psD[dc][:, :]:
                            e.matmul(o, self.m128_r[0:1, :], shilo[0:1, 0:HALF],
                                     start=False, stop=False), wait=w)
                    tcor = sc.emit("pe", lambda e, o=psD[dc][:, :]:
                                   e.matmul(o, self.m128_r[0:1, :],
                                            shilo[0:1, 512:512 + HALF],
                                            start=False, stop=True), inc="pe")
                    dcg = dhh * 8 + dc
                    P = dc % 2
                    a_ev = sc.emit("act",
                                   lambda e, o=tmpE[P][:, :],
                                   i=psD[dc][:, :], s=self.dsc[:, dcg:dcg + 1]:
                                   e.activation(o, i, AF.Copy, scale=s),
                                   wait=[("pe", tcor), ("dve", xadd[P])],
                                   inc="act")
                    d_x = sc.emit("dve", lambda e, o=self.xsl(dcg, t0, HALF),
                                  i=tmpE[P][:, :]: e.tensor_add(o, o, i),
                                  wait=[("act", a_ev)], inc="dve")
                    xadd[P] = d_x
                    psD_free[dc] = d_x
            sc.flush()
            stD.close()
        st.close()

    def p4(self):
        sc, nc = self.sc, self.nc
        st = ExitStack()
        ln3 = st.enter_context(self.sbuf("p4_ln3", [128, 2 * NF * HALF], BF16))
        t1b = st.enter_context(self.sbuf("p4_t1", [128, 8 * HALF], BF16))
        wst = st.enter_context(self.sbuf("p4_wst", [128, 2 * NF * 256], BF16))
        s1 = st.enter_context(self.sbuf("p4_s1", [128, 512], F32))
        s2 = st.enter_context(self.sbuf("p4_s2", [128, 512], F32))
        sqtA = st.enter_context(self.sbuf("p4_sqtA", [128, 1024], F32))
        rowb = st.enter_context(self.sbuf("p4_rowb", [1, 2048], F32))
        tmpC = st.enter_context(self.sbuf("p4_tmpC", [128, 1024], F32))
        psA = st.enter_context(self.psum("p4_psA", [128, 512], F32))
        psB = st.enter_context(self.psum("p4_psB", [128, 512], F32))
        psA2 = st.enter_context(self.psum("p4_psA2", [128, 512], F32))
        psB2 = st.enter_context(self.psum("p4_psB2", [128, 512], F32))
        ps = [psA, psB]
        l3 = lambda t, c: ln3[:, (t * NF + c) * HALF:(t * NF + c + 1) * HALF]

        xs0 = [self.xsl(c, 0, HALF) for c in range(NF)]
        res0 = {}
        for _ in self.emit_ln_gen(xs0, [l3(0, c) for c in range(NF)], HALF,
                                  (s1, s2, sqtA, rowb, psA, psB), res0,
                                  sum_eng="dve", ap2_eng="dve"):
            pass
        ln_done = [res0["done"], None]
        sc.flush()
        for th in range(2):
            t0 = th * HALF
            lngen = None
            resB = {}
            if th == 0:
                xs1 = [self.xsl(c, HALF, HALF) for c in range(NF)]
                lngen = self.emit_ln_gen(xs1, [l3(1, c) for c in range(NF)],
                                         HALF, (s1, s2, sqtA, rowb, psA2, psB2),
                                         resB, sum_eng="dve", ap2_eng="dve")

            def drive(n):
                nonlocal lngen
                if lngen is None:
                    return
                for _ in range(n):
                    if next(lngen, "END") == "END":
                        lngen = None
                        break

            ps_free = [ln_done[th], ln_done[th]]
            blk_last = [0, 0]
            for ofb in range(4):
                par = ofb % 2
                wsl = wst[:, par * NF * 256:(par + 1) * NF * 256]
                l_w = sc.dma(
                    "pool", wsl.rearrange("p (c f) -> p c f", c=NF),
                    self.bTB.ap()[ofb],
                    wait=[("pe", blk_last[par])], sem=f"q{2 + par}")
                for ofh in range(2):
                    u = (ofb * 2 + ofh) % 2
                    for kc in range(NF):
                        w = []
                        if kc == 0:
                            w = [(f"q{2 + par}", l_w), ps_free[u], ln_done[th]]
                        sc.emit("pe", lambda e, o=ps[u][:, 0:HALF],
                                a=wsl[:, kc * 256 + ofh * 128:
                                      kc * 256 + ofh * 128 + 128],
                                b=l3(th, kc),
                                s=(kc == 0), t=(kc == NF - 1):
                                e.matmul(o, a, b, start=s, stop=t),
                                wait=w, inc="pe" if kc == NF - 1 else None)
                    tmm = sc.cnt["pe"]
                    blk_last[par] = tmm
                    oc = ofb * 2 + ofh
                    a_ev = sc.emit("act",
                                   lambda e, o=t1b[:, oc * HALF:(oc + 1) * HALF],
                                   i=ps[u][:, 0:HALF]: e.activation(o, i, AF.Tanh),
                                   wait=[("pe", tmm)], inc="act")
                    ps_free[u] = ("act", a_ev)
                    drive(4)
            t1_done = sc.cnt["act"]
            tc_war = [0, 0]
            for ofb in range(8):
                par = ofb % 2
                wsl = wst[:, par * NF * 256:(par + 1) * NF * 256]
                l_w = sc.dma(
                    "pool", wsl[:, 0:8 * 256].rearrange("p (c f) -> p c f", c=8),
                    self.cTB.ap()[ofb],
                    wait=[("pe", blk_last[par])], sem=f"q{2 + par}")
                for ofh in range(2):
                    u = (ofb * 2 + ofh) % 2
                    for kc in range(8):
                        w = []
                        if kc == 0:
                            w = [(f"q{2 + par}", l_w), ps_free[u], ("act", t1_done)]
                        sc.emit("pe", lambda e, o=ps[u][:, 0:HALF],
                                a=wsl[:, kc * 256 + ofh * 128:
                                      kc * 256 + ofh * 128 + 128],
                                b=t1b[:, kc * HALF:(kc + 1) * HALF],
                                s=(kc == 0), t=(kc == 7):
                                e.matmul(o, a, b, start=s, stop=t),
                                wait=w, inc="pe" if kc == 7 else None)
                    tmm = sc.cnt["pe"]
                    blk_last[par] = tmm
                    oc = ofb * 2 + ofh
                    P = oc % 2
                    d_c = sc.emit("dve", lambda e, o=tmpC[:, P * 512:P * 512 + HALF],
                                  a=l3(th, oc),
                                  s=self.dv[:, oc:oc + 1], i=ps[u][:, 0:HALF]:
                                  e.scalar_tensor_tensor(o, a, s, i, ALU.mult,
                                                         ALU.add),
                                  wait=[("pe", tmm), ("pool", tc_war[P])],
                                  inc="dve")
                    ps_free[u] = ("dve", d_c)
                    p_x = sc.emit("pool", lambda e, o=self.xsl(oc, t0, HALF),
                                  i=tmpC[:, P * 512:P * 512 + HALF]:
                                  e.tensor_add(o, o, i),
                                  wait=[("dve", d_c)], inc="pool")
                    tc_war[P] = p_x
                    drive(4)
            if lngen is not None:
                for _ in lngen:
                    pass
            if th == 0:
                ln_done[1] = resB["done"]
            sc.flush()
        st.close()

    def p6(self):
        sc, nc = self.sc, self.nc
        st = ExitStack()
        strip = st.enter_context(self.sbuf("p6_strip", [128, 2 * D], F32))
        psT = [st.enter_context(self.psum(f"p6_psT{i}", [128, 512], F32))
               for i in range(2)]
        strip_free = [0, 0]
        ps_free = [("dve", 0), ("dve", 0)]
        for s in range(8):
            par = s % 2
            sl = strip[:, par * D:(par + 1) * D]
            ev_tgts = []
            for g in range(4):
                gp = g % 2
                for qv in range(4):
                    c = g * 4 + qv
                    w = [ps_free[gp]] if qv == 0 else []
                    sc.emit("pe", lambda e, o=psT[gp][:, qv * 128:(qv + 1) * 128],
                            i=self.xsl(c, s * 128, 128):
                            e.transpose(o, i, self.idf[:, :]),
                            wait=w, inc="pe" if qv == 3 else None)
                tpe = sc.cnt["pe"]
                eng = "dve" if g % 2 == 0 else "act"
                tgt = sc.emit(eng, lambda e, o=sl[:, g * 512:(g + 1) * 512],
                              i=psT[gp][:, 0:512], eng=eng:
                              (e.tensor_copy(o, i) if eng == "dve"
                               else e.activation(o, i, AF.Copy)),
                              wait=[("pe", tpe), (f"q{par}", strip_free[par])],
                              inc="act" if eng == "act" else "dve")
                ps_free[gp] = (eng, tgt)
                ev_tgts.append((eng, tgt))
            strip_free[par] = sc.store(
                self.out.ap()[s * 128:(s + 1) * 128, :], sl, wait=ev_tgts,
                sem=f"q{par}")
        sc.waitonly("sp", [("q0", sc.cnt["q0"]), ("q1", sc.cnt["q1"])])
        sc.flush()
        st.close()

    def build(self):
        self.p0()
        if self.stage >= 2:
            self.p1()
            self.p2()
        self.attn_ctx.close()
        self.finish_conv()
        self.sc.flush()
        if self.stage >= 3:
            self.ffn("p3")
        if self.stage >= 4:
            self.p4()
        if self.stage >= 5:
            self.ffn("p5")
        self.p6()
        return self.nc


_prog_cache = {}
_last_results = None


def _get_prog(stage):
    if stage not in _prog_cache:
        _prog_cache[stage] = Builder(stage).build()
    return _prog_cache[stage]


def kernel(L0_attn_a_hidden_states, L0_attn_a_slope_rate, L0_attn_a_w_qkv,
           L0_attn_a_w_gate, L0_attn_a_w_out, L0_attn_a_norm_w,
           L0_attn_b_B, L0_attn_b_C, L0_attn_b_D,
           L0_ffn_gate_w, L0_ffn_gate_scale, L0_ffn_up_w, L0_ffn_up_scale,
           L0_ffn_down_w, L0_ffn_down_scale):
    cc = np.ascontiguousarray
    x = cc(np.asarray(L0_attn_a_hidden_states, np.float32)).reshape(8192, D)
    col = lambda v, n: cc(np.asarray(v, np.float32).reshape(n, 128).T)
    blk = lambda a, nb: cc(a.reshape(16, 128, nb, 256).transpose(2, 1, 0, 3))
    gw_t = np.asarray(L0_ffn_gate_w, np.int32).T
    uw_t = np.asarray(L0_ffn_up_w, np.int32).T
    dw_t = np.asarray(L0_ffn_down_w, np.int32).T
    cT_t = np.asarray(L0_attn_b_C, np.float32).T
    shared = {
        "wqkvT": blk(np.asarray(L0_attn_a_w_qkv, np.float32).T, 24),
        "wgateT": blk(np.asarray(L0_attn_a_w_gate, np.float32).T, 8),
        "woutT": blk(np.asarray(L0_attn_a_w_out, np.float32).T, 8),
        "bT": blk(np.asarray(L0_attn_b_B, np.float32).T, 4),
        "cT": cc(cT_t.reshape(8, 128, 8, 256).transpose(2, 1, 0, 3)),
        "gwT": cc(gw_t.reshape(16, 128, NI, 128).transpose(2, 1, 0, 3)),
        "uwT": cc(uw_t.reshape(16, 128, NI, 128).transpose(2, 1, 0, 3)),
        "dwT": cc(dw_t.reshape(16, 4, 128, 2, 1024).transpose(0, 3, 2, 1, 4)),
        "slope": cc(np.asarray(L0_attn_a_slope_rate, np.float32).reshape(1, H)),
        "nw_c": col(L0_attn_a_norm_w, NF),
        "dv_c": col(L0_attn_b_D, NF),
        "gsc_c": col(L0_ffn_gate_scale, NI),
        "usc_c": col(L0_ffn_up_scale, NI),
        "dsc_c": col(L0_ffn_down_scale, NF),
        "ident": np.eye(128, dtype=np.float32),
    }
    zeros = np.zeros((TOK, D), np.float32)
    in_maps = []
    for c in range(NCORES):
        m = dict(shared)
        m["xo"] = cc(x[c * TOK:(c + 1) * TOK])
        m["xp"] = cc(x[(c - 1) * TOK:c * TOK]) if c % 2 == 1 else zeros
        in_maps.append(m)
    nc = _get_prog(STAGE)
    res = run_bass_kernel_spmd(nc, in_maps, core_ids=list(range(NCORES)))
    global _last_results
    _last_results = res.results
    outs = [np.asarray(r["out"]) for r in res.results]
    return np.concatenate(outs, axis=0).reshape(4, 2048, D)
